# version 32
# baseline (speedup 1.0000x reference)
import numpy as np
import concourse.bass as bass
import concourse.mybir as mybir

F32 = mybir.dt.float32
BF16 = mybir.dt.bfloat16
AF = mybir.ActivationFunctionType
ALU = mybir.AluOpType
AXL = mybir.AxisListType


class Geom:
    def __init__(self, S=4096, DFF=4096):
        self.B, self.S, self.E = 4, S, 1024
        self.NCH = 8
        self.OWN = S // 2
        self.HALO = 256
        self.EXT = self.OWN + 2 * self.HALO
        self.EXTT = self.EXT // 128
        self.OWNT = self.OWN // 128
        self.ST = S // 128
        self.H, self.KVH, self.D = 16, 4, 64
        self.QL, self.KVL = 64, 32
        self.ROPE = 10000.0
        self.SCL = self.D ** -0.5
        self.NG = self.EXTT // 4
        self.DFF = DFF
        self.FFCH = DFF // 128
        self.NEXP = 8
        self.FFPE = DFF // self.NEXP
        self.WIN = 512
        self.K_i = [min(i + self.OWNT - 1, self.ST) for i in range(self.EXTT)]
        self.K_i = [max(k, 1) for k in self.K_i]
        self.K_g = [self.K_i[4 * g + 3] for g in range(self.NG)]
        self.PAIRS0 = [(h, h + 4) for h in range(4)] + [(h + 8, h + 12) for h in range(4)]
        self.PAIRS1 = [(2 * p, 2 * p + 1) for p in range(8)]
        self.ZB = 128 * (max(self.K_g) - 1)
        zs = []
        for gi in range(self.NG):
            for j in range(self.K_g[gi]):
                if (128 * j + 127) <= (-self.HALO + 512 * gi):
                    continue
                zs.append(self.ZB + 512 * gi - 128 * j)
        self.ZOFF = min(zs)
        self.ZW = max(zs) + 512 + 1 - self.ZOFF
        self.ZW = (self.ZW + 3) // 4 * 4


def _dmchunk(w):
    din, dout = w.shape
    assert din % 128 == 0
    return np.ascontiguousarray(
        w.reshape(din // 128, 128, dout).transpose(1, 0, 2).reshape(128, -1)
    ).astype(np.float32)


def _dmchunk_f(w):
    din, dout = w.shape
    assert din == 1024
    a = w.reshape(8, 128, dout // 128, 128)
    return np.ascontiguousarray(
        a.transpose(1, 2, 0, 3).reshape(128, -1)).astype(np.float32)


def stage_core_inputs(g: Geom, core, full):
    b, h = core // 2, core % 2
    S, E, EXT, OWN, HALO = g.S, g.E, g.EXT, g.OWN, g.HALO
    d = {}
    x = np.asarray(full["L0_attn_x"])
    xb = x[b]
    lo = h * OWN - HALO
    ext = np.zeros((EXT, E), np.float32)
    s0, s1 = max(lo, 0), min(lo + EXT, S)
    ext[s0 - lo:s1 - lo] = xb[s0:s1]
    d["xext"] = np.ascontiguousarray(ext.T)
    d["xkv"] = np.ascontiguousarray(xb.T)

    ones = np.ones((128, 128), np.float32)
    d["cones"] = ones
    d["cneg"] = -ones
    d["cid"] = np.eye(128, dtype=np.float32)
    r = np.arange(128)[:, None]; f = np.arange(640)[None, :]
    d["band"] = ((f - r >= 0) & (f - r <= 511)).astype(np.float32)
    d["bandkq"] = ((f - r >= 1) & (f - r <= 512)).astype(np.float32)
    z = np.arange(g.ZW)[None, :]
    rr = np.arange(128)[:, None]
    keep = rr <= z + g.ZOFF + (lo - g.ZB)
    d["m01"] = keep.astype(np.float32)
    cf = np.zeros((128, 8), np.float32)
    cf[:, 0] = 1e-5
    cf[:, 1] = 1.0
    cf[:, 2] = -300.0
    cf[:, 3] = 300.0
    d["cf32"] = cf
    valid = np.zeros(EXT, np.float32)
    valid[s0 - lo:s1 - lo] = 1.0
    d["validrow"] = valid[None, :]
    d["validcol"] = np.ascontiguousarray(valid.reshape(g.EXTT, 128).T)

    jj = np.arange(32, dtype=np.float64)
    invf = g.ROPE ** (-jj / 32.0)
    posk = np.arange(S, dtype=np.float64)
    ang_k = posk[None, :] * invf[:, None]
    ck = np.cos(ang_k); sk = np.sin(ang_k)
    d["ropekc"] = np.tile(ck, (4, 1)).astype(np.float32)
    d["ropeks"] = np.tile(sk, (4, 1)).astype(np.float32)
    posq = np.arange(lo, lo + EXT, dtype=np.float64)
    ang_q = posq[None, :] * invf[:, None]
    cq = (np.cos(ang_q) * g.SCL).astype(np.float32)
    sq_ = (np.sin(ang_q) * g.SCL).astype(np.float32)
    d["ropeqc"] = np.tile(cq, (4, 1)).astype(np.float32)
    d["ropeqs"] = np.tile(sq_, (4, 1)).astype(np.float32)

    d["wqa"] = _dmchunk(np.asarray(full["L0_attn_Wqa"]))
    wqb = np.asarray(full["L0_attn_Wqb"])
    cols_e, cols_o = [], []
    for e in range(4):
        for hh in range(4 * e, 4 * e + 4):
            cols_e += [hh * 64 + 2 * j for j in range(32)]
            cols_o += [hh * 64 + 2 * j + 1 for j in range(32)]
    d["wqbpre"] = np.ascontiguousarray(
        np.concatenate([wqb[:, cols_e], wqb[:, cols_o]], axis=1)).astype(np.float32)
    d["wkva"] = _dmchunk(np.asarray(full["L0_attn_Wkva"]))
    wkvb = np.asarray(full["L0_attn_Wkvb"])
    colkE = [m * 128 + 4 * j for m in range(4) for j in range(32)]
    colkO = [m * 128 + 4 * j + 2 for m in range(4) for j in range(32)]
    d["wkvbkE"] = np.ascontiguousarray(wkvb[:, colkE]).astype(np.float32)
    d["wkvbkO"] = np.ascontiguousarray(wkvb[:, colkO]).astype(np.float32)
    wv = np.zeros((33, 4 * 65), np.float32)
    for m in range(4):
        for dd in range(64):
            wv[:32, m * 65 + dd] = wkvb[:, m * 128 + 2 * dd + 1]
        wv[32, m * 65 + 64] = 1.0
    d["wkvbv"] = wv
    wout0 = np.asarray(full["L0_attn_Wout"])
    rows = []
    for (h1, h2) in g.PAIRS0:
        for hh in (h1, h2):
            rows += [hh * 64 + 2 * j for j in range(32)]
            rows += [hh * 64 + 2 * j + 1 for j in range(32)]
    d["wout0"] = _dmchunk(np.ascontiguousarray(wout0[rows]))
    d["bout0"] = np.ascontiguousarray(
        np.asarray(full["L0_attn_bout"]).reshape(8, 128).T).astype(np.float32)

    d["rw0"] = _dmchunk(np.asarray(full["L0_ffn_router_w"]))
    d["rb0"] = np.asarray(full["L0_ffn_router_b"]).reshape(1, 8).astype(np.float32)
    W1 = np.asarray(full["L0_ffn_W1"])
    w1s = np.concatenate([W1[n] for n in range(g.NEXP)], axis=1)
    d["w10"] = _dmchunk_f(w1s)
    d["b10"] = np.ascontiguousarray(
        np.asarray(full["L0_ffn_b1"]).reshape(-1).reshape(g.FFCH, 128).T).astype(np.float32)
    W2 = np.asarray(full["L0_ffn_W2"])
    w2s = np.concatenate([W2[n] for n in range(g.NEXP)], axis=0)
    d["w20"] = np.ascontiguousarray(
        w2s.reshape(g.FFCH, 128, 1024).transpose(1, 0, 2).reshape(128, -1)).astype(np.float32)
    d["b20r"] = np.asarray(full["L0_ffn_b2"]).astype(np.float32)
    e8 = np.zeros((8, 8, 128), np.float32)
    for n in range(8):
        e8[n, n, :] = 1.0
    d["e8sel"] = e8.transpose(1, 0, 2).reshape(8, 8 * 128)

    wqkv = np.asarray(full["L1_attn_Wqkv"]).T
    bqkv = np.asarray(full["L1_attn_bqkv"])
    wq, wk, wv1 = wqkv[:, :1024], wqkv[:, 1024:2048], wqkv[:, 2048:]
    bq, bk, bv1 = bqkv[:1024], bqkv[1024:2048], bqkv[2048:]
    wvp = np.zeros((1024, 16 * 65), np.float32)
    bvp = np.zeros(16 * 65, np.float32)
    for hh in range(16):
        wvp[:, hh * 65:hh * 65 + 64] = wv1[:, hh * 64:hh * 64 + 64]
        bvp[hh * 65:hh * 65 + 64] = bv1[hh * 64:hh * 64 + 64]
        bvp[hh * 65 + 64] = 1.0
    wcat = np.concatenate([wq, wk, wvp], axis=1)
    bcat = np.concatenate([bq, bk, bvp])
    d["wqkv1"] = _dmchunk(wcat)
    d["bqkv1"] = bcat.reshape(1, -1).astype(np.float32)
    d["wout1"] = _dmchunk(np.asarray(full["L1_attn_Wout"]).T)
    d["bout1"] = np.ascontiguousarray(
        np.asarray(full["L1_attn_bout"]).reshape(8, 128).T).astype(np.float32)

    d["rw1"] = _dmchunk(np.asarray(full["L1_ffn_router_w"]))
    d["w11"] = _dmchunk_f(np.asarray(full["L1_ffn_w1"]))
    d["b11"] = np.ascontiguousarray(
        np.asarray(full["L1_ffn_b1"]).reshape(g.FFCH, 128).T).astype(np.float32)
    d["w21"] = np.ascontiguousarray(
        np.asarray(full["L1_ffn_w2"]).reshape(g.FFCH, 128, 1024)
        .transpose(1, 0, 2).reshape(128, -1)).astype(np.float32)
    d["b21"] = np.ascontiguousarray(
        np.asarray(full["L1_ffn_b2"]).reshape(8, 128).T).astype(np.float32)
    return d


ENGS = ("sync", "tensor", "vector", "scalar", "gpsimd")


class Prog:

    def __init__(self, nc, es):
        self.nc = nc
        self.es = es
        self.stream = {e: [] for e in ENGS}
        self.sem = {}
        self.cnt = {}
        for e in ("tensor", "vector", "scalar", "gpsimd"):
            self._mksem(e)
        self.waited = {e: {} for e in ENGS}
        self.opidx = {e: 0 for e in ENGS}
        self.drained = {e: 0 for e in ENGS}
        self.last_mark_op = {e: 0 for e in ENGS}
        self.nwait = 0
        self.ninstr = 0
        self.trace_sites = True

    def _mksem(self, key):
        if key not in self.sem:
            self.sem[key] = self.es.enter_context(self.nc.semaphore("s_" + key))
            self.cnt[key] = 0
        return self.sem[key]

    def emit(self, e, fn, mark=False, dmaq=None):
        self.ninstr += 1
        self.opidx[e] += 1
        incs = []
        tok = None
        if dmaq is not None:
            self._mksem(dmaq)
            self.cnt[dmaq] += 16
            incs.append((self.sem[dmaq], 16))
            tok = (dmaq, self.cnt[dmaq], None)
        elif mark:
            self.cnt[e] += 1
            incs.append((self.sem[e], 1))
            tok = (e, self.cnt[e], self.opidx[e])
            self.last_mark_op[e] = self.opidx[e]
        if self.trace_sites:
            import traceback
            st = traceback.extract_stack(limit=4)
            site = "|".join(f"{f.filename.split('/')[-1]}:{f.lineno}" for f in st[:-1])
        else:
            site = ""
        self.stream[e].append(("i", fn, incs, site))
        return tok

    def wait(self, consumer, tok):
        if tok is None:
            return
        src, val, op = tok
        if op is None and src not in ("tensor", "vector", "scalar", "gpsimd"):
            val = self.cnt[src]
        if src == consumer:
            if op is None or op <= self.drained[consumer]:
                return
            w = self.waited[consumer]
            if w.get(src, -1) < val:
                self.stream[consumer].append(("w", self.sem[src], val, src))
                self.nwait += 1
                w[src] = val
            self.drained[consumer] = max(self.drained[consumer], op)
            return
        w = self.waited[consumer]
        if w.get(src, -1) >= val:
            return
        self.stream[consumer].append(("w", self.sem[src], val, src))
        self.nwait += 1
        w[src] = val

    def waits(self, consumer, toks):
        for t in toks:
            self.wait(consumer, t)

    def barrier_token(self):
        toks = []
        for k, v in self.cnt.items():
            if v <= 0:
                continue
            if k in ("tensor", "vector", "scalar", "gpsimd"):
                toks.append((k, v, self.opidx[k]))
            else:
                toks.append((k, v, None))
        return toks

    def barrier(self):
        toks = self.barrier_token()
        for e in ENGS:
            self.waits(e, toks)

    def run(self):
        nc = self.nc
        with nc.Block() as blk:
            def mk(e):
                def body(h):
                    for kind, a, b, s in self.stream[e]:
                        if kind == "i":
                            inst = a(h)
                            for sem, v in b:
                                inst.then_inc(sem, v)
                        elif kind == "w":
                            h.wait_ge(a, b)
                        elif kind == "d":
                            h.drain()
                return body
            blk.sync(mk("sync"))
            blk.tensor(mk("tensor"))
            blk.vector(mk("vector"))
            blk.scalar(mk("scalar"))
            blk.gpsimd(mk("gpsimd"))


class Buf:
    __slots__ = ("wtok", "rtoks", "rpos")

    def __init__(self):
        self.wtok = None
        self.rtoks = []
        self.rpos = {}

    def before_write(self, P, engine):
        for t in (self.wtok if isinstance(self.wtok, list) else [self.wtok]):
            P.wait(engine, t)
        for t in self.rtoks:
            P.wait(engine, t)
        for e, pos in self.rpos.items():
            if e == engine:
                if pos > P.drained[engine]:
                    P.stream[engine].append(("d", None, None, ""))
                    P.opidx[engine] += 1
                    P.drained[engine] = P.opidx[engine]
            else:
                if P.last_mark_op[e] < pos:
                    raise RuntimeError(
                        f"before_write: unmarked trailing read on {e} at {pos} "
                        f"(last mark {P.last_mark_op[e]})")
                P.wait(engine, (e, P.cnt[e], P.last_mark_op[e]))

    def wrote(self, tok):
        self.wtok = tok
        self.rtoks = []
        self.rpos = {}

    def before_read(self, P, engine):
        for t in (self.wtok if isinstance(self.wtok, list) else [self.wtok]):
            P.wait(engine, t)

    def read(self, tok):
        if tok is not None:
            self.rtoks.append(tok)

    def read_at(self, P, engine):
        self.rpos[engine] = max(self.rpos.get(engine, 0), P.opidx[engine])


class Ctx:
    pass


def build(g: Geom, dbg=(), race_safe=True):
    from contextlib import ExitStack
    nc = bass.Bass("TRN2", target_bir_lowering=False,
                   detect_race_conditions=bool(race_safe))
    es = ExitStack()
    P = Prog(nc, es)
    C = Ctx()
    C.g, C.nc, C.P, C.es = g, nc, P, C
    C.dbg = set(dbg)
    C.dumps = []

    D = {}
    def din(name, shape):
        D[name] = nc.dram_tensor(name, list(shape), F32, kind="ExternalInput").ap()
    for nm, shp in [
        ("xext", (g.E, g.EXT)), ("xkv", (g.E, g.S)),
        ("cones", (128, 128)), ("cneg", (128, 128)), ("cid", (128, 128)),
        ("band", (128, 640)), ("bandkq", (128, 640)),
        ("m01", (128, g.ZW)), ("cf32", (128, 8)),
        ("validrow", (1, g.EXT)), ("validcol", (128, g.EXTT)),
        ("ropekc", (128, g.S)), ("ropeks", (128, g.S)),
        ("ropeqc", (128, g.EXT)), ("ropeqs", (128, g.EXT)),
        ("wqa", (128, 8 * 64)), ("wqbpre", (64, 1024)),
        ("wkva", (128, 8 * 32)), ("wkvbkE", (32, 128)), ("wkvbkO", (32, 128)),
        ("wkvbv", (33, 4 * 65)),
        ("wout0", (128, 8 * 1024)), ("bout0", (128, 8)),
        ("rw0", (128, 8 * 8)), ("rb0", (1, 8)),
        ("w10", (128, 8 * g.DFF)), ("b10", (128, g.FFCH)),
        ("w20", (128, g.FFCH * 1024)), ("b20r", (8, 1024)),
        ("e8sel", (8, 8 * 128)),
        ("wqkv1", (128, 8 * 3088)), ("bqkv1", (1, 3088)),
        ("wout1", (128, 8 * 1024)), ("bout1", (128, 8)),
        ("rw1", (128, 8 * 8)),
        ("w11", (128, 8 * g.DFF)), ("b11", (128, g.FFCH)),
        ("w21", (128, g.FFCH * 1024)), ("b21", (128, 8)),
    ]:
        din(nm, shp)
    C.D = D
    y = nc.dram_tensor("y", [128, g.NCH * g.OWN], F32, kind="ExternalOutput").ap()
    C.y = y
    SC = {}
    for nm in ("wqa", "wqbpre", "wkva", "wkvbkE", "wkvbkO", "wkvbv", "wout0",
               "rw0", "rb0", "w10", "w20", "b20r", "e8sel", "wqkv1", "bqkv1",
               "wout1", "rw1", "w11", "w21", "band", "bandkq", "m01",
               "cones", "cneg", "cid"):
        shp = list(D[nm].shape)
        SC[nm] = nc.dram_tensor("sc_" + nm, shp, BF16, kind="Internal").ap()
    C.SC = SC
    SC["k1"] = nc.dram_tensor("sc_k1", [128, 8 * g.EXT], BF16, kind="Internal").ap()
    SC["q1"] = nc.dram_tensor("sc_q1", [128, 8 * g.EXT], BF16, kind="Internal").ap()
    SC["v1"] = nc.dram_tensor("sc_v1", [128, g.EXTT * 1040], BF16, kind="Internal").ap()

    C.dbgd = {}
    def dbg_out(name, shape, dtype=F32):
        if name in C.dbg:
            C.dbgd[name] = nc.dram_tensor("dbg_" + name, list(shape), dtype,
                                          kind="ExternalOutput").ap()
    C.dbg_out = dbg_out
    dbg_out("ckv", (33, g.S), BF16)
    dbg_out("ka", (128, g.S), BF16)
    dbg_out("kb", (128, g.S), BF16)
    dbg_out("vtm", (128, g.ST * 260), BF16)
    dbg_out("kmaxsq", (1, 8))
    for p in range(8):
        dbg_out(f"qp{p}", (128, g.EXT), BF16)
    for c in range(8):
        dbg_out(f"x1_{c}", (128, g.EXT))
        dbg_out(f"x2_{c}", (128, g.EXT))
        dbg_out(f"x3_{c}", (128, g.EXT))

    def dump(name, ap, toks):
        if name not in C.dbg:
            return
        P.waits("sync", toks)
        t = P.emit("sync", lambda h, a=ap, d=C.dbgd[name]: h.dma_start(out=d, in_=a),
                   dmaq="dbg")
        C.final_toks.append(t)
    C.dump = dump
    C.final_toks = []

    C.cur_es = es
    C.glob_es = es
    sb = lambda nm, shp, dt: C.cur_es.enter_context(nc.sbuf_tensor(nm, list(shp), dt))
    C.sb = sb
    ps = lambda nm, shp, dt=F32: C.cur_es.enter_context(nc.psum_tensor(nm, list(shp), dt))
    C.ps = ps

    C.x = [sb(f"x{c}", (128, g.EXT), F32) for c in range(8)]
    C.xB = [Buf() for _ in range(8)]
    C.ones = sb("ones", (128, 128), BF16); C.onesB = Buf()
    C.neg = sb("negs", (1, 128), BF16); C.negB = Buf()
    C.ident = sb("ident", (128, 128), BF16); C.identB = Buf()
    C.identf = sb("identf", (128, 128), F32); C.identfB = Buf()
    C.onesw = sb("onesw", (1, 512), BF16); C.oneswB = Buf()
    C.cf32 = sb("cf32s", (128, 8), F32); C.cf32B = Buf()

    build_phases(C)
    globals()["_last_C"] = C

    P.waits("sync", C.final_toks)
    P.run()
    es.close()
    return nc


def ld_cast(C, name, q="qsync"):
    P, nc, g = C.P, C.nc, C.g
    src, dst = C.D[name], C.SC[name]
    pr, fr = src.shape
    step = 4096
    for off in range(0, fr, step):
        w = min(step, fr - off)
        stf = C.cast_f[C.cast_i % 4]; stb = C.cast_b[C.cast_i % 4]
        C.cast_i += 1
        stf.buf.before_write(P, "sync")
        t = P.emit("sync", lambda h, s=src, o=off, w=w, d=stf.t: h.dma_start(
            out=d[0:pr, 0:w], in_=s[:, o:o + w]), dmaq="cf" + str(C.cast_i % 4))
        stf.buf.wrote(t)
        stf.buf.before_read(P, "vector")
        stb.buf.before_write(P, "vector")
        t2 = P.emit("vector", lambda h, a=stf.t, b=stb.t, p=pr, w=w: h.tensor_copy(
            b[0:p, 0:w], a[0:p, 0:w]), mark=True)
        stf.buf.read(t2)
        stb.buf.wrote(t2)
        stb.buf.before_read(P, "sync")
        t3 = P.emit("sync", lambda h, d=dst, o=off, w=w, b=stb.t, p=pr: h.dma_start(
            out=d[:, o:o + w], in_=b[0:p, 0:w]), dmaq="cb" + str(C.cast_i % 4))
        stb.buf.read(t3)


class TB:
    def __init__(self, t):
        self.t = t
        self.buf = Buf()


def mm(C, out_ap, lhsT_ap, rhs_ap, *, start, stop, rbufs=(), wbuf=None, mark=False,
       tile_position=None):
    P = C.P
    for b in rbufs:
        b.before_read(P, "tensor")
    if wbuf is not None:
        if start:
            wbuf.before_write(P, "tensor")
        else:
            P.wait("tensor", wbuf.wtok)
    kw = {}
    if tile_position is not None:
        kw["tile_position"] = tile_position
    t = P.emit("tensor", lambda h, o=out_ap, l=lhsT_ap, r=rhs_ap, s=start, st=stop,
               kw=kw: h.matmul(o, l, r, start=s, stop=st, **kw), mark=mark)
    for b in rbufs:
        b.read_at(P, "tensor")
    if wbuf is not None:
        wbuf.wrote(t if mark else wbuf.wtok)
    return t


def ln_group(C, xf_chunks_ap, xbf, sqbf, pstat, rowbufs, W, prefix):
    P, g = C.P, C.g
    xbf.buf.before_write(P, "vector")
    last = None
    for c, (ap, b) in enumerate(xf_chunks_ap):
        b.before_read(P, "vector")
        last = P.emit("vector", lambda h, a=ap, d=xbf.t, c=c, W=W: h.tensor_copy(
            d[:, c * W:(c + 1) * W], a), mark=(c == 7))
        b.read(last if c == 7 else None)
    xbf.buf.wrote(last)
    psA, psB = pstat
    onescol = C.ones[:, 0:1]
    for c in range(8):
        mm(C, psA.t[0:1, 0:W], onescol, xbf.t[:, c * W:(c + 1) * W],
           start=(c == 0), stop=(c == 7), rbufs=[C.onesB, xbf.buf],
           wbuf=psA.buf if c == 0 else None, mark=(c == 7))
    xbf.buf.before_read(P, "scalar")
    sqbf.buf.before_write(P, "scalar")
    t = P.emit("scalar", lambda h, a=xbf.t, d=sqbf.t, W=W: h.activation(
        d[:, 0:8 * W], a[:, 0:8 * W], AF.Square), mark=True)
    sqbf.buf.wrote(t)
    sqbf.buf.before_read(P, "tensor")
    for c in range(8):
        mm(C, psB.t[0:1, 0:W], onescol, sqbf.t[:, c * W:(c + 1) * W],
           start=(c == 0), stop=(c == 7), rbufs=[sqbf.buf],
           wbuf=psB.buf if c == 0 else None, mark=(c == 7))
    tstat = ("tensor", P.cnt["tensor"], P.opidx["tensor"])
    psA.buf.wrote(tstat); psB.buf.wrote(tstat)
    nm, rr, v1, v2 = (rowbufs[k] for k in ("nm", "r", "v1", "v2"))
    nmb, rb = rowbufs["nmb"], rowbufs["rb"]
    psA.buf.before_read(P, "vector")
    nm.buf.before_write(P, "vector")
    t = P.emit("vector", lambda h, p=psA.t, d=nm.t, W=W: h.tensor_scalar_mul(
        d[0:1, 0:W], p[0:1, 0:W], -1.0 / 1024.0), mark=True)
    nm.buf.wrote(t); psA.buf.read(t)
    psB.buf.before_read(P, "vector")
    v1.buf.before_write(P, "vector")
    t = P.emit("vector", lambda h, p=psB.t, d=v1.t, W=W: h.tensor_scalar_mul(
        d[0:1, 0:W], p[0:1, 0:W], 1.0 / 1024.0), mark=True)
    v1.buf.wrote(t); psB.buf.read(t)
    nm.buf.before_read(P, "scalar")
    v2.buf.before_write(P, "scalar")
    t = P.emit("scalar", lambda h, a=nm.t, d=v2.t, W=W: h.activation(
        d[0:1, 0:W], a[0:1, 0:W], AF.Square), mark=True)
    v2.buf.wrote(t); nm.buf.read(t)
    v2.buf.before_read(P, "vector")
    v1.buf.before_write(P, "vector")
    t = P.emit("vector", lambda h, a=v1.t, b=v2.t, W=W: h.tensor_sub(
        a[0:1, 0:W], a[0:1, 0:W], b[0:1, 0:W]), mark=True)
    v1.buf.wrote(t); v2.buf.read(t)
    v1.buf.before_read(P, "scalar")
    v2.buf.before_write(P, "scalar")
    t = P.emit("scalar", lambda h, a=v1.t, d=v2.t, W=W, e=C.cf32: h.activation(
        d[0:1, 0:W], a[0:1, 0:W], AF.Sqrt, bias=e[0:1, 0:1]), mark=True)
    v2.buf.wrote(t); v1.buf.read(t); C.cf32B.read(t)
    v2.buf.before_read(P, "vector")
    rr.buf.before_write(P, "vector")
    t = P.emit("vector", lambda h, a=v2.t, d=rr.t, W=W: h.reciprocal(
        d[0:1, 0:W], a[0:1, 0:W]), mark=True)
    rr.buf.wrote(t); v2.buf.read(t)
    for s_, d_ in ((nm, nmb), (rr, rb)):
        s_.buf.before_read(P, "vector")
        d_.buf.before_write(P, "vector")
        t = P.emit("vector", lambda h, a=s_.t, d=d_.t, W=W: h.tensor_copy(
            d[0:1, 0:W], a[0:1, 0:W]), mark=True)
        d_.buf.wrote(t); s_.buf.read(t)


def bcast_mm(C, dst_psum_ap, dst_buf, src_row_ap, src_buf, row=0):
    tp = (row, 0) if row else None
    lhsT = C.ones[row:row + 1, 0:128]
    t = mm(C, dst_psum_ap, lhsT, src_row_ap, start=True, stop=True,
           rbufs=[C.onesB, src_buf], wbuf=dst_buf, mark=True, tile_position=tp)
    dst_buf.wrote(t)
    return t


def xn_tts(C, xf_chunks_ap, bcn, bcr, tmp, xn, W):
    P = C.P
    bcnT, bcnS = bcn
    bcrT, bcrS = bcr
    xn.buf.before_write(P, "vector")
    bcnT.buf.before_read(P, "vector")
    bcrT.buf.before_read(P, "vector")
    last = None
    for c, (ap, b) in enumerate(xf_chunks_ap):
        b.before_read(P, "vector")
        tmp.buf.before_write(P, "vector")
        t = P.emit("vector", lambda h, a=ap, m=bcnT.t, d=tmp.t, W=W, s=bcnS:
                   h.tensor_add(d[:, 0:W], a, m[:, s.start:s.start + W]), mark=True)
        tmp.buf.wrote(t)
        tmp.buf.before_read(P, "vector")
        last = P.emit("vector", lambda h, a=tmp.t, m=bcrT.t, d=xn.t, c=c, W=W, s=bcrS:
                      h.tensor_mul(d[:, c * W:(c + 1) * W], a[:, 0:W],
                                   m[:, s.start:s.start + W]), mark=True)
        tmp.buf.read(last)
        b.read(last)
    xn.buf.wrote(last)
    bcnT.buf.read(last)
    bcrT.buf.read(last)


def build_phases(C):
    g, P, nc, sb, ps = C.g, C.P, C.nc, C.sb, C.ps
    D, SC = C.D, C.SC


    from contextlib import ExitStack as _ES
    C.cur_es = _ES()
    C.cast_f = [TB(sb(f"cstf{i}", (128, 4096), F32)) for i in range(4)]
    C.cast_b = [TB(sb(f"cstb{i}", (128, 4096), BF16)) for i in range(4)]
    C.cast_i = 0

    def ld_f32(dst_ap, dst_buf, src_ap, q="ld0"):
        dst_buf.before_write(P, "sync")
        t = P.emit("sync", lambda h, d=dst_ap, s=src_ap: h.dma_start(out=d, in_=s),
                   dmaq=q)
        dst_buf.wrote(t)
        return t

    def ld_bf(dst_ap, dst_buf, src_name, src_sl=None, q="ld1", eng="sync"):
        src = C.SC[src_name]
        if src_sl is not None:
            src = src[:, src_sl[0]:src_sl[1]]
        dst_buf.before_write(P, eng)
        t = P.emit(eng, lambda h, d=dst_ap, s=src: h.dma_start(out=d, in_=s), dmaq=q)
        dst_buf.wrote(t)
        return t

    def cast_direct(dst_ap, dst_buf, src_name):
        pr, fr = dst_ap.shape
        stf = C.cast_f[C.cast_i % 4]
        C.cast_i += 1
        stf.buf.before_write(P, "sync")
        t = P.emit("sync", lambda h, s=D[src_name], d=stf.t, p=pr, f=fr: h.dma_start(
            out=d[0:p, 0:f], in_=s[0:p, 0:f]), dmaq="cf" + str(C.cast_i % 4))
        stf.buf.wrote(t)
        stf.buf.before_read(P, "vector")
        dst_buf.before_write(P, "vector")
        t2 = P.emit("vector", lambda h, a=stf.t, d=dst_ap, p=pr, f=fr: h.tensor_copy(
            d, a[0:p, 0:f]), mark=True)
        stf.buf.read(t2)
        dst_buf.wrote(t2)

    cast_direct(C.ones[:], C.onesB, "cones")
    cast_direct(C.neg[:], C.negB, "cneg")
    cast_direct(C.ident[:], C.identB, "cid")
    ld_f32(C.identf[:], C.identfB, D["cid"])
    ld_f32(C.cf32[:], C.cf32B, D["cf32"])
    C.oneswB.before_write(P, "vector")
    t = P.emit("vector", lambda h: h.memset(C.onesw[:], 1.0), mark=True)
    C.oneswB.wrote(t)

    for nm in ("wkva", "wkvbkE", "wkvbkO", "wkvbv", "wqa", "wqbpre", "m01",
               "wout0", "rw0", "rb0", "e8sel", "b20r", "w10", "w20",
               "wqkv1", "bqkv1", "band", "bandkq", "wout1", "rw1", "w11", "w21"):
        ld_cast(C, nm)

    for c in range(8):
        ld_f32(C.x[c][:], C.xB[c], D["xext"][128 * c:128 * (c + 1), :])

    from contextlib import ExitStack
    _p0es = C.cur_es
    C.cur_es = C.glob_es
    C.attn_es = ExitStack()
    def phase(fn):
        C.cur_es = ExitStack()
        fn(C)
        P.barrier()
        C.cur_es.close()
        C.cur_es = C.glob_es
    P.barrier()
    _p0es.close()
    phase(p1_l0_kv)
    phase(p2_l0_q)
    phase(p3_l0_attn)
    C.attn_es.close()
    phase(p4_l0_ffn)
    phase(p5_l1_qkv)
    phase(p6_l1_attn)
    phase(p7_l1_ffn)
    p8_store(C)


_ROWCTR = [0]
def mk_rows(C, W=512):
    sb = C.sb
    _ROWCTR[0] += 1
    i = _ROWCTR[0]
    d = {k: TB(sb(f"row_{k}_{i}", (1, W), F32)) for k in ("nm", "r", "v1", "v2")}
    d.update({k: TB(sb(f"row_{k}_{i}", (1, W), BF16)) for k in ("nmb", "rb")})
    return d


def p1_l0_kv(C):
    g, P, sb, ps = C.g, C.P, C.sb, C.ps
    _sba = lambda nm, shp, dt: C.attn_es.enter_context(
        C.nc.sbuf_tensor(nm, list(shp), dt))
    C.KA = _sba("KA", (128, g.S), BF16); C.KAB = Buf()
    C.KB = _sba("KB", (128, g.S), BF16); C.KBB = Buf()
    C.Vtm = _sba("Vtm", (128, g.ST * 260), BF16); C.VtmB = Buf()
    C.kmaxsq = _sba("kmaxsq", (1, 8), F32); C.kmaxsqB = Buf()

    es = C.es
    xkvf = [TB(sb("xkvf0", (128, 8 * 512), F32))] * 2
    xkvbf2 = [TB(sb(f"xkvbf{i}", (128, 8 * 512), BF16)) for i in range(2)]
    pstat = (TB(ps("pstat1a", (1, 512))), TB(ps("pstat1b", (1, 512))))
    rows2 = [mk_rows(C), None]
    _r1 = dict(rows2[0])
    _ROWALT = {k: TB(sb(f"row_{k}_alt1", (1, 512),
                        BF16 if k in ("nmb", "rb") else F32))
               for k in ("nm", "r", "nmb", "rb")}
    _r1.update(_ROWALT)
    rows2[1] = _r1
    bc1 = TB(ps("pbc1", (128, 1024)))
    tmp = TB(sb("tmp1", (128, 512), F32))
    xnkv = TB(sb("xnkv", (128, 8 * 512), BF16))
    ckv = TB(sb("ckv", (33, g.S), BF16))
    ckvP = TB(ps("pckv", (32, 512)))
    KEp = TB(ps("pKE", (128, 512)))
    KOp = TB(ps("pKO", (128, 512)))
    t1 = TB(sb("rt1", (128, 512), F32))
    t2 = TB(sb("rt2", (128, 512), F32))
    rEb2 = [TB(sb(f"rEb{i}", (128, 512), BF16)) for i in range(2)]
    rOb2 = [TB(sb(f"rOb{i}", (128, 512), BF16)) for i in range(2)]
    Vp = TB(ps("pV", (128, 260)))
    ropekc = TB(sb("ropekcS", (128, 512), F32))
    ropeks = TB(sb("ropeksS", (128, 512), F32))
    wkvaR = TB(sb("wkvaR", (128, 256), BF16))
    wkEK = TB(sb("wkEK", (32, 128), BF16))
    wkOK = TB(sb("wkOK", (32, 128), BF16))
    wvR = TB(sb("wvR", (33, 260), BF16))

    for tb, nm in ((wkvaR, "wkva"), (wkEK, "wkvbkE"), (wkOK, "wkvbkO"), (wvR, "wkvbv")):
        tb.buf.before_write(P, "scalar")
        t = P.emit("scalar", lambda h, d=tb.t, s=C.SC[nm]: h.dma_start(out=d[:], in_=s),
                   dmaq="kvw")
        tb.buf.wrote(t)

    ckv.buf.before_write(P, "vector")
    t = P.emit("vector", lambda h, d=ckv.t: h.memset(d[32:33, :], 1.0), mark=True)
    ckv.buf.wrote(t)

    NKG = g.S // 512
    for ig in range(NKG):
        cur = xkvf[ig % 2]
        xkvbf = xkvbf2[ig % 2]
        sqbf = xkvbf
        rows = rows2[ig % 2]
        rEb, rOb = rEb2[ig % 2], rOb2[ig % 2]
        gsl = slice(512 * ig, 512 * (ig + 1))
        for tb, nm in ((ropekc, "ropekc"), (ropeks, "ropeks")):
            tb.buf.before_write(P, "sync")
            t = P.emit("sync", lambda h, d=tb.t, s=C.D[nm], gs=gsl: h.dma_start(
                out=d[:], in_=s[:, gs]), dmaq="ropek")
            tb.buf.wrote(t)
        cur.buf.before_write(P, "sync")
        last = None
        for c in range(8):
            last = P.emit("sync", lambda h, d=cur.t, s=C.D["xkv"], c=c, gs=gsl:
                          h.dma_start(out=d[:, 512 * c:512 * (c + 1)],
                                      in_=s[128 * c:128 * (c + 1), gs]), dmaq="xkvf" + str(ig % 2))
        cur.buf.wrote(last)
        xf_chunks = [(cur.t[:, 512 * c:512 * (c + 1)], cur.buf) for c in range(8)]
        ln_group(C, xf_chunks, xkvbf, sqbf, pstat, rows, 512, "kv")
        bcast_mm(C, bc1.t[:, 0:512], bc1.buf, rows["nmb"].t[0:1, :], rows["nmb"].buf)
        bcast_mm(C, bc1.t[:, 512:1024], bc1.buf, rows["rb"].t[0:1, :], rows["rb"].buf)
        xn_tts(C, xf_chunks, (bc1, slice(0, 512)), (bc1, slice(512, 1024)), tmp, xnkv, 512)
        for c in range(8):
            mm(C, ckvP.t[:], wkvaR.t[:, 32 * c:32 * (c + 1)],
               xnkv.t[:, 512 * c:512 * (c + 1)], start=(c == 0), stop=(c == 7),
               rbufs=[wkvaR.buf, xnkv.buf], wbuf=ckvP.buf if c == 0 else None,
               mark=(c == 7))
        ckvP.buf.wrote(("tensor", P.cnt["tensor"], P.opidx["tensor"]))
        ckvP.buf.before_read(P, "scalar")
        ckv.buf.before_write(P, "scalar")
        t = P.emit("scalar", lambda h, s=ckvP.t, d=ckv.t, gs=gsl: h.copy(
            d[0:32, gs], s[:]), mark=True)
        ckv.buf.wrote(t); ckvP.buf.read(t)
        mm(C, KEp.t[:], wkEK.t[:], ckv.t[0:32, gsl], start=True, stop=True,
           rbufs=[wkEK.buf, ckv.buf], wbuf=KEp.buf, mark=True)
        KEp.buf.wrote(("tensor", P.cnt["tensor"], P.opidx["tensor"]))
        mm(C, KOp.t[:], wkOK.t[:], ckv.t[0:32, gsl], start=True, stop=True,
           rbufs=[wkOK.buf], wbuf=KOp.buf, mark=True)
        KOp.buf.wrote(("tensor", P.cnt["tensor"], P.opidx["tensor"]))
        for (pa, pb, wa, wb, out, op) in (
                (KEp, KOp, ropekc, ropeks, rEb, "sub"),
                (KEp, KOp, ropeks, ropekc, rOb, "add")):
            pa.buf.before_read(P, "vector"); wa.buf.before_read(P, "vector")
            t1.buf.before_write(P, "vector")
            tt = P.emit("vector", lambda h, a=pa.t, w=wa.t, d=t1.t:
                        h.tensor_mul(d[:], a[:], w[:, 0:512]), mark=True)
            t1.buf.wrote(tt); wa.buf.read(tt)
            pb.buf.before_read(P, "vector"); wb.buf.before_read(P, "vector")
            t2.buf.before_write(P, "vector")
            tt = P.emit("vector", lambda h, a=pb.t, w=wb.t, d=t2.t:
                        h.tensor_mul(d[:], a[:], w[:, 0:512]), mark=True)
            t2.buf.wrote(tt); wb.buf.read(tt)
            t1.buf.before_read(P, "vector"); t2.buf.before_read(P, "vector")
            out.buf.before_write(P, "vector")
            if op == "sub":
                tt = P.emit("vector", lambda h, d=out.t: h.tensor_sub(
                    d[:], t1.t[:], t2.t[:]), mark=True)
            else:
                tt = P.emit("vector", lambda h, d=out.t: h.tensor_add(
                    d[:], t1.t[:], t2.t[:]), mark=True)
            out.buf.wrote(tt); t1.buf.read(tt); t2.buf.read(tt)
        KEp.buf.read(tt); KOp.buf.read(tt)
        for e_ in ("gpsimd", "sync"):
            rEb.buf.before_read(P, e_); rOb.buf.before_read(P, e_)
            C.KAB.before_write(P, e_); C.KBB.before_write(P, e_)
        dts = []
        for m in range(4):
            KT, KTB = (C.KA, C.KAB) if m < 2 else (C.KB, C.KBB)
            base = 64 * (m % 2)
            for ei, (srct, srcb, off) in enumerate(((rEb, rEb.buf, 0),
                                                    (rOb, rOb.buf, 32))):
                eng = "gpsimd" if (2 * m + ei) % 2 == 0 else "sync"
                tt = P.emit(eng, lambda h, d=KT, s=srct.t, m=m, b=base, o=off,
                            gs=gsl: h.dma_start(
                                out=d[b + o:b + o + 32, gs],
                                in_=s[32 * m:32 * m + 32, :]),
                            dmaq="krs" + eng[0])
                dts.append(tt)
        for tt in dts:
            rEb.buf.read(tt); rOb.buf.read(tt)
        C.KAB.wrote([dts[-1], dts[-2]]); C.KBB.wrote([dts[-1], dts[-2]])
        for k4 in range(4):
            kt = 4 * ig + k4
            mm(C, Vp.t[:], ckv.t[:, 512 * ig + 128 * k4: 512 * ig + 128 * (k4 + 1)],
               wvR.t[:], start=True, stop=True, rbufs=[ckv.buf, wvR.buf],
               wbuf=Vp.buf, mark=True)
            Vp.buf.wrote(("tensor", P.cnt["tensor"], P.opidx["tensor"]))
            Vp.buf.before_read(P, "scalar")
            C.VtmB.before_write(P, "scalar")
            t = P.emit("scalar", lambda h, s=Vp.t, d=C.Vtm, kt=kt: h.copy(
                d[:, 260 * kt:260 * (kt + 1)], s[:]), mark=True)
            C.VtmB.wrote(t); Vp.buf.read(t)
    C.kmaxsqB.before_write(P, "vector")
    t = P.emit("vector", lambda h, d=C.kmaxsq: h.memset(d[:], 0.0), mark=True)
    C.kmaxsqB.wrote(t)
    kcm = TB(sb("kcm", (1, 16), F32))
    sqk = TB(sb("sqk", (128, 512), BF16))
    psk = KEp
    for ti, (KT, KTB) in enumerate(((C.KA, C.KAB), (C.KB, C.KBB))):
        for chk in range(g.S // 512):
            KTB.before_read(P, "scalar")
            sqk.buf.before_write(P, "scalar")
            t = P.emit("scalar", lambda h, a=KT, d=sqk.t, c=chk: h.activation(
                d[:], a[:, 512 * c:512 * (c + 1)], AF.Square), mark=True)
            sqk.buf.wrote(t); KTB.read(t)
            mm(C, psk.t[0:1, 0:512], C.ones[:, 0:1], sqk.t[:], start=True, stop=True,
               rbufs=[C.onesB, sqk.buf], wbuf=psk.buf, mark=True)
            psk.buf.wrote(("tensor", P.cnt["tensor"], P.opidx["tensor"]))
            psk.buf.before_read(P, "vector")
            kcm.buf.before_write(P, "vector")
            t = P.emit("vector", lambda h, s=psk.t, d=kcm.t, i=8 * ti + chk:
                       h.reduce_max(d[0:1, i:i + 1], s[0:1, 0:512], axis=AXL.X), mark=True)
            kcm.buf.wrote(t); psk.buf.read(t)
        kcm.buf.before_read(P, "vector")
        C.kmaxsqB.before_write(P, "vector")
        nch = g.S // 512
        t = P.emit("vector", lambda h, s=kcm.t, d=C.kmaxsq, ti=ti, n=nch: h.reduce_max(
            d[0:1, ti:ti + 1], s[0:1, 8 * ti:8 * ti + n], axis=AXL.X), mark=True)
        C.kmaxsqB.wrote(t); kcm.buf.read(t)
    C.dump("ckv", ckv.t[:], [ckv.buf.wtok])
    C.dump("ka", C.KA[:], [C.KAB.wtok])
    C.dump("kb", C.KB[:], [C.KBB.wtok])
    C.dump("vtm", C.Vtm[:], [C.VtmB.wtok])
    C.dump("kmaxsq", C.kmaxsq[:], [C.kmaxsqB.wtok])


def _noop():
    from contextlib import nullcontext
    return nullcontext()


def p2_l0_q(C):
    g, P, sb, ps = C.g, C.P, C.sb, C.ps
    _sba = lambda nm, shp, dt: C.attn_es.enter_context(
        C.nc.sbuf_tensor(nm, list(shp), dt))
    C.Qp = [_sba(f"Qp{p}", (128, g.EXT), BF16) for p in range(8)]
    C.QpB = [Buf() for _ in range(8)]

    ropeqc = TB(sb("ropeqcS", (128, 512), F32))
    ropeqs = TB(sb("ropeqsS", (128, 512), F32))
    wqaR = TB(sb("wqaR", (128, 512), BF16))
    wqbR = TB(sb("wqbR", (64, 1024), BF16))
    xqbf = TB(sb("xqbf", (128, 8 * 512), BF16))
    sqbf = xqbf
    pstat = (TB(ps("pstat2a", (1, 512))), TB(ps("pstat2b", (1, 512))))
    rows = mk_rows(C)
    bc2 = TB(ps("pbc2", (128, 1024)))
    tmp = TB(sb("tmp2", (128, 512), F32))
    xnq = TB(sb("xnq", (128, 8 * 512), BF16))
    qlatP = TB(ps("pqlat", (64, 512)))
    qlat = TB(sb("qlat", (64, 512), BF16))
    QEp = TB(ps("pQE", (128, 512)))
    QOp = TB(ps("pQO", (128, 512)))
    t1v = [TB(sb(f"qt1{i}", (128, 512), F32)) for i in range(2)]
    t2v = [TB(sb(f"qt2{i}", (128, 512), F32)) for i in range(2)]
    rEbv = [TB(sb(f"qrEb{i}", (128, 512), BF16)) for i in range(2)]
    rObv = [TB(sb(f"qrOb{i}", (128, 512), BF16)) for i in range(2)]

    for tb, nm in ((wqaR, "wqa"), (wqbR, "wqbpre")):
        tb.buf.before_write(P, "scalar")
        t = P.emit("scalar", lambda h, d=tb.t, s=C.SC[nm]: h.dma_start(out=d[:], in_=s),
                   dmaq="qw")
        tb.buf.wrote(t)

    def pmap(h):
        if h < 4: return (h, 0)
        if h < 8: return (h - 4, 1)
        if h < 12: return (h - 4, 0)
        return (h - 8, 1)

    NQG = g.EXT // 512
    for ig in range(NQG):
        gsl = slice(512 * ig, 512 * (ig + 1))
        for tb, nm in ((ropeqc, "ropeqc"), (ropeqs, "ropeqs")):
            tb.buf.before_write(P, "sync")
            t = P.emit("sync", lambda h, d=tb.t, s=C.D[nm], gs=gsl: h.dma_start(
                out=d[:], in_=s[:, gs]), dmaq="ropeq")
            tb.buf.wrote(t)
        xf_chunks = [(C.x[c][:, gsl], C.xB[c]) for c in range(8)]
        ln_group(C, xf_chunks, xqbf, sqbf, pstat, rows, 512, "q")
        bcast_mm(C, bc2.t[:, 0:512], bc2.buf, rows["nmb"].t[0:1, :], rows["nmb"].buf)
        bcast_mm(C, bc2.t[:, 512:1024], bc2.buf, rows["rb"].t[0:1, :], rows["rb"].buf)
        xn_tts(C, xf_chunks, (bc2, slice(0, 512)), (bc2, slice(512, 1024)), tmp, xnq, 512)
        for c in range(8):
            mm(C, qlatP.t[:], wqaR.t[:, 64 * c:64 * (c + 1)],
               xnq.t[:, 512 * c:512 * (c + 1)], start=(c == 0), stop=(c == 7),
               rbufs=[wqaR.buf, xnq.buf], wbuf=qlatP.buf if c == 0 else None,
               mark=(c == 7))
        qlatP.buf.wrote(("tensor", P.cnt["tensor"], P.opidx["tensor"]))
        qlatP.buf.before_read(P, "scalar")
        qlat.buf.before_write(P, "scalar")
        t = P.emit("scalar", lambda h, s=qlatP.t, d=qlat.t: h.copy(d[:], s[:]),
                   mark=True)
        qlat.buf.wrote(t); qlatP.buf.read(t)
        for e in range(4):
            t1, t2 = t1v[e % 2], t2v[e % 2]
            rEb, rOb = rEbv[e % 2], rObv[e % 2]
            mm(C, QEp.t[:], wqbR.t[:, 128 * e:128 * (e + 1)], qlat.t[:],
               start=True, stop=True, rbufs=[wqbR.buf, qlat.buf], wbuf=QEp.buf,
               mark=True)
            QEp.buf.wrote(("tensor", P.cnt["tensor"], P.opidx["tensor"]))
            mm(C, QOp.t[:], wqbR.t[:, 128 * (e + 4):128 * (e + 5)], qlat.t[:],
               start=True, stop=True, rbufs=[], wbuf=QOp.buf, mark=True)
            QOp.buf.wrote(("tensor", P.cnt["tensor"], P.opidx["tensor"]))
            for (pa, pb, wa, wb, out, op) in (
                    (QEp, QOp, ropeqc, ropeqs, rEb, "sub"),
                    (QEp, QOp, ropeqs, ropeqc, rOb, "add")):
                pa.buf.before_read(P, "vector"); wa.buf.before_read(P, "vector")
                t1.buf.before_write(P, "vector")
                tt = P.emit("vector", lambda h, a=pa.t, w=wa.t, d=t1.t:
                            h.tensor_mul(d[:], a[:], w[:, 0:512]), mark=True)
                t1.buf.wrote(tt); wa.buf.read(tt)
                pb.buf.before_read(P, "vector"); wb.buf.before_read(P, "vector")
                t2.buf.before_write(P, "vector")
                tt = P.emit("vector", lambda h, a=pb.t, w=wb.t, d=t2.t:
                            h.tensor_mul(d[:], a[:], w[:, 0:512]), mark=True)
                t2.buf.wrote(tt); wb.buf.read(tt)
                t1.buf.before_read(P, "vector"); t2.buf.before_read(P, "vector")
                out.buf.before_write(P, "vector")
                if op == "sub":
                    tt = P.emit("vector", lambda h, d=out.t: h.tensor_sub(
                        d[:], t1.t[:], t2.t[:]), mark=True)
                else:
                    tt = P.emit("vector", lambda h, d=out.t: h.tensor_add(
                        d[:], t1.t[:], t2.t[:]), mark=True)
                out.buf.wrote(tt); t1.buf.read(tt); t2.buf.read(tt)
            QEp.buf.read(tt); QOp.buf.read(tt)
            dts = []
            for e_ in ("gpsimd", "sync"):
                rEb.buf.before_read(P, e_); rOb.buf.before_read(P, e_)
            for u in range(4):
                hh = 4 * e + u
                p, s = pmap(hh)
                eng = "gpsimd" if u % 2 == 0 else "sync"
                C.QpB[p].before_write(P, eng)
                tt = P.emit(eng, lambda h, d=C.Qp[p], s_=rEb.t, u=u, so=64 * s,
                            gs=gsl: h.dma_start(out=d[so:so + 32, gs],
                                                in_=s_[32 * u:32 * u + 32, :]),
                            dmaq="qrs" + eng[0])
                dts.append((p, tt))
                tt = P.emit(eng, lambda h, d=C.Qp[p], s_=rOb.t, u=u, so=64 * s,
                            gs=gsl: h.dma_start(out=d[so + 32:so + 64, gs],
                                                in_=s_[32 * u:32 * u + 32, :]),
                            dmaq="qrs" + eng[0])
                dts.append((p, tt))
            for p, tt in dts:
                C.QpB[p].wrote(tt)
                rEb.buf.read(tt); rOb.buf.read(tt)
    for p in range(8):
        C.dump(f"qp{p}", C.Qp[p][:], [C.QpB[p].wtok])


def p3_l0_attn(C):
    g, P, sb, ps = C.g, C.P, C.sb, C.ps
    m01R = TB(sb("m01R", (128, g.ZW), BF16))
    wout0R = TB(sb("wout0R", (128, 8 * 1024), BF16))
    bout0R = TB(sb("bout0R", (128, 8), F32))
    for tb, nm, q, eng in ((m01R, "m01", "qscalar", "scalar"),
                           (wout0R, "wout0", "qsync", "sync")):
        tb.buf.before_write(P, eng)
        t = P.emit(eng, lambda h, d=tb.t, s=C.SC[nm]: h.dma_start(out=d[:], in_=s),
                   dmaq=q)
        tb.buf.wrote(t)
    bout0R.buf.before_write(P, "sync")
    t = P.emit("sync", lambda h, d=bout0R.t, s=C.D["bout0"]: h.dma_start(out=d[:], in_=s),
               dmaq="bout0")
    bout0R.buf.wrote(t)

    negm = TB(sb("negm0", (128, 8), F32))
    sqq = [TB(sb(f"sqq3{i}", (128, 512), BF16)) for i in range(2)]
    psE = TB(ps("psE3", (128, 512)))
    qm = [TB(sb(f"qm3{i}", (1, 4), F32)) for i in range(2)]
    mhb = [TB(sb(f"mhb3{i}", (1, 1), BF16)) for i in range(2)]
    ST = [TB(ps(f"pST{i}", (128, 1024))) for i in range(2)]
    Praw = [TB(sb(f"Praw{i}", (128, 1024), BF16)) for i in range(5)]
    AV0 = TB(ps("pAV0", (65, 512)))
    AV1 = TB(ps("pAV1", (65, 512)))
    PO = TB(ps("pPO", (128, 512)))
    recf = TB(sb("recf3", (1, 512), F32))
    recb = TB(sb("recb3", (1, 512), BF16))
    recf1 = TB(sb("recf3b", (1, 512), F32))
    recb1 = TB(sb("recb3b", (1, 512), BF16))
    bcS = TB(sb("bcS3", (64, 512), BF16))
    bcS1 = TB(sb("bcS3b", (64, 512), BF16))
    attnout = [TB(sb(f"ao{p}", (128, 512), BF16)) for p in range(8)]

    for gi in range(g.NG):
        gsl = slice(512 * gi, 512 * (gi + 1))
        for p in range(8):
            kidx = 0 if p < 4 else 1
            pb = p % 2
            sq, qmc, mhc = sqq[pb], qm[pb], mhb[pb]
            PS = psE if pb == 0 else ST[0]
            C.QpB[p].before_read(P, "scalar")
            sq.buf.before_write(P, "scalar")
            t = P.emit("scalar", lambda h, a=C.Qp[p], d=sq.t, gs=gsl: h.activation(
                d[:], a[:, gs], AF.Square), mark=True)
            sq.buf.wrote(t); C.QpB[p].read(t)
            mm(C, PS.t[0:1, 0:512], C.ones[:, 0:1], sq.t[:], start=True, stop=True,
               rbufs=[C.onesB, sq.buf], wbuf=PS.buf, mark=True)
            PS.buf.wrote(("tensor", P.cnt["tensor"], P.opidx["tensor"]))
            PS.buf.before_read(P, "vector")
            qmc.buf.before_write(P, "vector")
            t = P.emit("vector", lambda h, s=PS.t, d=qmc.t: h.reduce_max(
                d[0:1, 0:1], s[0:1, 0:512], axis=AXL.X), mark=True)
            qmc.buf.wrote(t); PS.buf.read(t)
            qmc.buf.before_read(P, "scalar")
            C.kmaxsqB.before_read(P, "scalar")
            t = P.emit("scalar", lambda h, a=qmc.t, d=qmc.t, k=C.kmaxsq, ki=kidx:
                       h.activation(d[0:1, 1:2], a[0:1, 0:1], AF.Sqrt,
                                    scale=k[0:1, ki:ki + 1]), mark=True)
            qmc.buf.wrote(t); C.kmaxsqB.read(t)
            qmc.buf.before_read(P, "vector")
            mhc.buf.before_write(P, "vector")
            t = P.emit("vector", lambda h, a=qmc.t, d=mhc.t: h.tensor_copy(
                d[0:1, 0:1], a[0:1, 1:2]), mark=True)
            mhc.buf.wrote(t); qmc.buf.read(t)
            t = mm(C, PS.t[0:128, 0:1], C.neg[0:1, 0:128], mhc.t[0:1, 0:1],
                   start=True, stop=True, rbufs=[C.negB, mhc.buf], wbuf=PS.buf,
                   mark=True)
            PS.buf.wrote(t)
            PS.buf.before_read(P, "scalar")
            negm.buf.before_write(P, "scalar")
            t = P.emit("scalar", lambda h, s=PS.t, d=negm.t, p=p: h.copy(
                d[:, p:p + 1], s[0:128, 0:1]), mark=True)
            negm.buf.wrote(t); PS.buf.read(t)

        def emit_norm(p):
            slots = ((0, AV0, recf, recb, psE, bcS), (1, AV1, recf1, recb1, PO, bcS1))
            for s_, AVs, rf, rb, PB, bS in slots:
                AVs.buf.before_read(P, "vector")
                rf.buf.before_write(P, "vector")
                t = P.emit("vector", lambda h, a=AVs.t, d=rf.t: h.tensor_scalar_max(
                    d[0:1, :], a[64:65, :], 1e-12), mark=True)
                rf.buf.wrote(t); AVs.buf.read(t)
            for s_, AVs, rf, rb, PB, bS in slots:
                rf.buf.before_read(P, "scalar")
                rf.buf.before_write(P, "scalar")
                t = P.emit("scalar", lambda h, d=rf.t: h.activation(
                    d[0:1, :], d[0:1, :], AF.Ln), mark=True)
                rf.buf.wrote(t)
            for s_, AVs, rf, rb, PB, bS in slots:
                rf.buf.before_read(P, "scalar")
                rb.buf.before_write(P, "scalar")
                t = P.emit("scalar", lambda h, a=rf.t, d=rb.t: h.activation(
                    d[0:1, :], a[0:1, :], AF.Exp, scale=-1.0), mark=True)
                rb.buf.wrote(t); rf.buf.read(t)
            for s_, AVs, rf, rb, PB, bS in slots:
                t = mm(C, PB.t[:, 0:512], C.ones[0:1, 0:128], rb.t[0:1, :],
                       start=True, stop=True, rbufs=[C.onesB, rb.buf],
                       wbuf=PB.buf, mark=True)
                PB.buf.wrote(t)
            for s_, AVs, rf, rb, PB, bS in slots:
                PB.buf.before_read(P, "scalar")
                bS.buf.before_write(P, "scalar")
                t = P.emit("scalar", lambda h, s=PB.t, d=bS.t: h.copy(
                    d[0:64, :], s[0:64, 0:512]), mark=True)
                bS.buf.wrote(t); PB.buf.read(t)
            for s_, AVs, rf, rb, PB, bS in slots:
                bS.buf.before_read(P, "vector")
                AVs.buf.before_read(P, "vector")
                attnout[p].buf.before_write(P, "vector")
                r0 = 64 * s_
                t = P.emit("vector", lambda h, a=AVs.t, b=bS.t, d=attnout[p].t,
                           r0=r0: h.tensor_mul(d[r0:r0 + 64, :], a[0:64, :],
                                               b[0:64, :]), mark=True)
                attnout[p].buf.wrote(t)
                AVs.buf.read(t); bS.buf.read(t)

        for p in range(8):
            KT, KTB = (C.KA, C.KAB) if p < 4 else (C.KB, C.KBB)
            mA = 0 if p < 4 else 2
            mB = 1 if p < 4 else 3
            Kg = g.K_g[gi]
            pend = []

            def emit_av(j, Pout, PoutB, mA=mA, mB=mB, Kg=Kg):
                st = (j == 0); sp = (j == Kg - 1)
                PoutB.before_read(P, "tensor")
                mm(C, AV0.t[:], C.Vtm[:, 260 * j + 65 * mA:260 * j + 65 * mA + 65],
                   Pout[:, 0:512], start=st, stop=sp, rbufs=[C.VtmB],
                   wbuf=AV0.buf if st else None)
                t = mm(C, AV1.t[:], C.Vtm[:, 260 * j + 65 * mB:260 * j + 65 * mB + 65],
                       Pout[:, 512:1024], start=st, stop=sp, rbufs=[],
                       wbuf=AV1.buf if st else None, mark=sp)
                PoutB.read(t)

            for j in range(Kg):
                STc, Pr = ST[j % 2], Praw[j % 5]
                jsl = slice(128 * j, 128 * (j + 1))
                STc.buf.before_write(P, "tensor")
                mm(C, STc.t[:, 0:512], KT[0:64, jsl], C.Qp[p][0:64, gsl],
                   start=True, stop=True, rbufs=[KTB, C.QpB[p]], wbuf=None,
                   tile_position=(0, 0))
                t = mm(C, STc.t[:, 512:1024], KT[64:128, jsl], C.Qp[p][64:128, gsl],
                       start=True, stop=True, rbufs=[], wbuf=None,
                       tile_position=(64, 0), mark=True)
                STc.buf.wrote(t)
                STc.buf.before_read(P, "scalar")
                negm.buf.before_read(P, "scalar")
                Pr.buf.before_write(P, "scalar")
                t = P.emit("scalar", lambda h, s=STc.t, d=Pr.t, b=negm.t, p=p:
                           h.activation(d[:], s[:], AF.Exp, bias=b[:, p:p + 1]),
                           mark=True)
                Pr.buf.wrote(t); STc.buf.read(t); negm.buf.read(t)
                if len(pend) >= 4:
                    pend.pop(0)()
                skip = (128 * j + 127) <= (-g.HALO + 512 * gi)
                if not skip:
                    z0 = g.ZB + 512 * gi - 128 * j - g.ZOFF
                    Pr.buf.before_read(P, "vector")
                    m01R.buf.before_read(P, "vector")
                    Pr.buf.before_write(P, "vector")
                    for s_ in range(2):
                        t = P.emit("vector", lambda h, a=Pr.t, m=m01R.t,
                                   z=z0, s_=s_: h.scalar_tensor_tensor(
                                       a[:, 512 * s_:512 * (s_ + 1)],
                                       a[:, 512 * s_:512 * (s_ + 1)],
                                       1.0, m[:, z:z + 512],
                                       op0=ALU.mult, op1=ALU.mult), mark=(s_ == 1))
                    Pr.buf.wrote(t); m01R.buf.read(t)
                pend.append(lambda j=j, Pout=Pr.t, PoutB=Pr.buf: emit_av(j, Pout, PoutB))
            while pend:
                pend.pop(0)()
            tav = ("tensor", P.cnt["tensor"], P.opidx["tensor"])
            AV0.buf.wrote(tav); AV1.buf.wrote(tav)
            emit_norm(p)
        for o in range(8):
            for p in range(8):
                mm(C, PO.t[:], wout0R.t[:, 1024 * p + 128 * o:1024 * p + 128 * (o + 1)],
                   attnout[p].t[:], start=(p == 0), stop=(p == 7),
                   rbufs=[wout0R.buf, attnout[p].buf],
                   wbuf=PO.buf if p == 0 else None, mark=(p == 7))
            tpo = ("tensor", P.cnt["tensor"], P.opidx["tensor"])
            PO.buf.wrote(tpo)
            for p in range(8):
                attnout[p].buf.read(tpo)
            PO.buf.before_read(P, "vector")
            bout0R.buf.before_read(P, "vector")
            C.xB[o].before_write(P, "vector")
            t = P.emit("vector", lambda h, s=PO.t, b=bout0R.t, d=C.x[o], o=o, gs=gsl:
                       h.scalar_tensor_tensor(d[:, gs], s[:], b[:, o:o + 1], d[:, gs],
                                              op0=ALU.add, op1=ALU.add), mark=True)
            C.xB[o].wrote(t)
            PO.buf.read(t); bout0R.buf.read(t)
    for c in range(8):
        C.dump(f"x1_{c}", C.x[c][:], [C.xB[c].wtok])


class _TBV:
    def __init__(self, t, buf):
        self.t = t
        self.buf = buf


def ln_rows_full(C, tokens_lo, tokens_hi, rowsbf, tag, pstat):
    g, P = C.g, C.P
    sb, ps = C.sb, C.ps
    xbf = TB(sb(f"lnx_{tag}", (128, 8 * 512), BF16))
    sqbf = xbf
    rows = mk_rows(C)
    nmbF, rbF = rowsbf
    for lo in range(tokens_lo, tokens_hi, 512):
        gsl = slice(lo, lo + 512)
        xf_chunks = [(C.x[c][:, gsl], C.xB[c]) for c in range(8)]
        ln_group(C, xf_chunks, xbf, sqbf, pstat, rows, 512, tag)
        pairs = [(rows["nmb"], nmbF.t, nmbF.buf), (rows["rb"], rbF.t, rbF.buf)]
        for src, dstT, dstB in pairs:
            src.buf.before_read(P, "vector")
            dstB.before_write(P, "vector")
            t = P.emit("vector", lambda h, a=src.t, d=dstT, gs=gsl: h.tensor_copy(
                d[0:1, gs], a[0:1, :]), mark=True)
            dstB.wrote(t); src.buf.read(t)




def router_tokmajor(C, tok0, W, rwR, rbR, nmF, rbrows, SrowF, oneswF, rcolT,
                    Rp, ctx_bufs, want_experts, want_gate,
                    wT, maskrow, gaterow, col0, tag):
    g, P, sb = C.g, C.P, C.sb
    for half in range(W // 128):
        t0 = tok0 + 128 * half
        l0_ = 128 * half
        for c in range(8):
            mm(C, Rp.t[:, 0:8], C.x[c][:, t0:t0 + 128],
               rwR.t[:, 8 * c:8 * (c + 1)], start=(c == 0), stop=False,
               rbufs=[C.xB[c], rwR.buf], wbuf=Rp.buf if c == 0 else None)
        mm(C, Rp.t[:, 0:8], nmF.t[0:1, l0_:l0_ + 128], SrowF.t[0:1, 0:8],
           start=False, stop=(rbR is None), rbufs=[nmF.buf, SrowF.buf], wbuf=None,
           mark=(rbR is None))
        if rbR is not None:
            t = mm(C, Rp.t[:, 0:8], oneswF.t[0:1, 0:128], rbR.t[0:1, 0:8],
                   start=False, stop=True, rbufs=[oneswF.buf, rbR.buf], wbuf=None,
                   mark=True)
        Rp.buf.wrote(("tensor", P.cnt["tensor"], P.opidx["tensor"]))
        rbrows.buf.before_read(P, "tensor")
        C.identB.before_read(P, "tensor")
        rcolT[0].buf.before_write(P, "tensor")
        t = P.emit("tensor", lambda h, a=rbrows.t, d=rcolT[0].t, i=C.identf, t0=l0_:
                   h.transpose(d[0:128, 8:9], a[0:1, t0:t0 + 128], i[0:1, 0:1]), mark=True)
        rcolT[0].buf.wrote(t); rbrows.buf.read(t)
        rcolT[0].buf.before_read(P, "scalar")
        rcolT[1].buf.before_write(P, "scalar")
        t = P.emit("scalar", lambda h, s=rcolT[0].t, d=rcolT[1].t: h.copy(
            d[:, 0:1], s[0:128, 8:9]), mark=True)
        rcolT[1].buf.wrote(t); rcolT[0].buf.read(t)
        m1, ge1, Rm, m2, negm1, E, ge2, Eg, den, rden, wbf = ctx_bufs
        Rp.buf.before_read(P, "vector")
        m1.buf.before_write(P, "vector")
        t = P.emit("vector", lambda h, s=Rp.t, d=m1.t: h.reduce_max(
            d[:, 0:1], s[:, 0:8], axis=AXL.X), mark=True)
        m1.buf.wrote(t); Rp.buf.read(t)
        m1.buf.before_read(P, "vector"); ge1.buf.before_write(P, "vector")
        t = P.emit("vector", lambda h, s=Rp.t, m=m1.t, d=ge1.t: h.tensor_scalar(
            d[:, 0:8], s[:, 0:8], m[:, 0:1], None, op0=ALU.is_ge), mark=True)
        ge1.buf.wrote(t); m1.buf.read(t); Rp.buf.read(t)
        ge1.buf.before_read(P, "vector"); Rm.buf.before_write(P, "vector")
        t = P.emit("vector", lambda h, a=ge1.t, s=Rp.t, d=Rm.t:
                   h.scalar_tensor_tensor(d[:, 0:8], a[:, 0:8], -1e9, s[:, 0:8],
                                          op0=ALU.mult, op1=ALU.add), mark=True)
        Rm.buf.wrote(t); ge1.buf.read(t); Rp.buf.read(t)
        Rm.buf.before_read(P, "vector"); m2.buf.before_write(P, "vector")
        t = P.emit("vector", lambda h, s=Rm.t, d=m2.t: h.reduce_max(
            d[:, 0:1], s[:, 0:8], axis=AXL.X), mark=True)
        m2.buf.wrote(t); Rm.buf.read(t)
        m1.buf.before_read(P, "vector"); rcolT[1].buf.before_read(P, "vector")
        negm1.buf.before_write(P, "vector")
        t = P.emit("vector", lambda h, s=m1.t, r=rcolT[1].t, d=negm1.t:
                   h.scalar_tensor_tensor(d[:, 0:1], s[:, 0:1], -1.0, r[:, 0:1],
                                          op0=ALU.mult, op1=ALU.mult), mark=True)
        negm1.buf.wrote(t); m1.buf.read(t); rcolT[1].buf.read(t)
        negm1.buf.before_read(P, "scalar")
        rcolT[1].buf.before_read(P, "scalar")
        Rp.buf.before_read(P, "scalar")
        E.buf.before_write(P, "scalar")
        t = P.emit("scalar", lambda h, s=Rp.t, b=negm1.t, r=rcolT[1].t, d=E.t:
                   h.activation(d[:, 0:8], s[:, 0:8], AF.Exp, bias=b[:, 0:1],
                                scale=r[:, 0:1]), mark=True)
        E.buf.wrote(t); negm1.buf.read(t); Rp.buf.read(t); rcolT[1].buf.read(t)
        m2.buf.before_read(P, "vector"); ge2.buf.before_write(P, "vector")
        t = P.emit("vector", lambda h, s=Rp.t, m=m2.t, d=ge2.t: h.tensor_scalar(
            d[:, 0:8], s[:, 0:8], m[:, 0:1], None, op0=ALU.is_ge), mark=True)
        ge2.buf.wrote(t); m2.buf.read(t); Rp.buf.read(t)
        E.buf.before_read(P, "vector"); ge2.buf.before_read(P, "vector")
        Eg.buf.before_write(P, "vector")
        t = P.emit("vector", lambda h, a=E.t, b=ge2.t, d=Eg.t: h.tensor_mul(
            d[:, 0:8], a[:, 0:8], b[:, 0:8]), mark=True)
        Eg.buf.wrote(t); E.buf.read(t); ge2.buf.read(t)
        Eg.buf.before_read(P, "vector"); den.buf.before_write(P, "vector")
        t = P.emit("vector", lambda h, s=Eg.t, d=den.t: h.reduce_sum(
            d[:, 0:1], s[:, 0:8], axis=AXL.X), mark=True)
        den.buf.wrote(t); Eg.buf.read(t)
        den.buf.before_read(P, "vector"); rden.buf.before_write(P, "vector")
        t = P.emit("vector", lambda h, s=den.t, d=rden.t: h.reciprocal(
            d[:, 0:1], s[:, 0:1]), mark=True)
        rden.buf.wrote(t); den.buf.read(t)
        wbf.buf.before_write(P, "vector")
        rden.buf.before_read(P, "vector")
        if want_experts:
            t = P.emit("vector", lambda h, a=Eg.t, r=rden.t, d=wbf.t: h.tensor_scalar(
                d[:, 0:8], a[:, 0:8], r[:, 0:1], None, op0=ALU.mult), mark=True)
            Eg.buf.read(t)
        if want_gate:
            den.buf.before_read(P, "vector")
            t = P.emit("vector", lambda h, a=den.t, r=rden.t, d=wbf.t: h.tensor_mul(
                d[:, 32:33], a[:, 0:1], r[:, 0:1]), mark=True)
            den.buf.read(t)
        wbf.buf.wrote(t); rden.buf.read(t)
        wbf.buf.before_read(P, "tensor")
        C.identB.before_read(P, "tensor")
        wT.buf.before_write(P, "tensor")
        t = P.emit("tensor", lambda h, a=wbf.t, d=wT.t, i=C.ident: h.transpose(
            d[0:33, 0:128], a[:, 0:33], i[:]), mark=True)
        wbf.buf.read(t)
        wT.buf.wrote(t)
        C.identB.read(t)
        wT.buf.before_read(P, "scalar")
        if want_experts:
            maskrow.buf.before_write(P, "scalar")
            t = P.emit("scalar", lambda h, s=wT.t, d=maskrow.t, c0=col0, hf=half:
                       h.copy(d[0:8, c0 + 128 * hf:c0 + 128 * hf + 128],
                              s[0:8, :]), mark=True)
            maskrow.buf.wrote(t); wT.buf.read(t)
        if want_gate:
            gaterow.buf.before_write(P, "scalar")
            t = P.emit("scalar", lambda h, s=wT.t, d=gaterow.t, c0=col0, hf=half:
                       h.copy(d[0:1, c0 + 128 * hf:c0 + 128 * hf + 128],
                              s[32:33, 0:128]), mark=True)
            gaterow.buf.wrote(t); wT.buf.read(t)


def dense_ffn(C, tag, tok_lo, tok_hi, w1nm, b1nm, w2nm, gelu_af, rwnm, rbnm,
              b2nm, b2fold_rows, want_experts, want_gate, nres):
    g, P, sb, ps = C.g, C.P, C.sb, C.ps
    W = 256
    lnxbf = TB(sb(f"lnx_{tag}", (128, 8 * W), BF16))
    lnsq = lnxbf
    lnrows = mk_rows(C, W)
    w2R = TB(sb(f"w2R_{tag}", (128, g.FFCH * 1024), BF16))
    ld = lambda tb, nm, q="qsync", e="sync": (
        tb.buf.before_write(P, e),
        tb.buf.wrote(P.emit(e, lambda h, d=tb.t, s=C.SC[nm]: h.dma_start(
            out=d[:], in_=s), dmaq=q)))
    ld(w2R, w2nm)
    w1R = TB(sb(f"w1R_{tag}", (128, nres * 1024), BF16))
    w1R.buf.before_write(P, "sync")
    w1R.buf.wrote(P.emit("sync", lambda h, d=w1R.t, s=C.SC[w1nm]: h.dma_start(
        out=d[:], in_=s[:, 0:nres * 1024]), dmaq="w1R" + tag))
    w1S = [TB(sb(f"w1S_{tag}{i}", (128, 1024), BF16)) for i in range(4)]
    b1R = TB(sb(f"b1R_{tag}", (128, g.FFCH), F32))
    b1R.buf.before_write(P, "sync")
    b1R.buf.wrote(P.emit("sync", lambda h, d=b1R.t, s=C.D[b1nm]: h.dma_start(
        out=d[:], in_=s), dmaq="b1R" + tag))
    rwR = TB(sb(f"rwR_{tag}", (128, 64), F32))
    rwR.buf.before_write(P, "scalar")
    rwR.buf.wrote(P.emit("scalar", lambda h, d=rwR.t, s=C.D[rwnm]: h.dma_start(
        out=d[:], in_=s), dmaq="rwF" + tag))
    rbR = None
    if rbnm is not None:
        rbR = TB(sb(f"rbR_{tag}", (1, 8), F32))
        rbR.buf.before_write(P, "scalar")
        rbR.buf.wrote(P.emit("scalar", lambda h, d=rbR.t, s=C.D[rbnm]: h.dma_start(
            out=d[:], in_=s), dmaq="rbF" + tag))
    oneswF = TB(sb(f"oneswF_{tag}", (1, 128), F32))
    oneswF.buf.before_write(P, "vector")
    _t = P.emit("vector", lambda h, d=oneswF.t: h.memset(d[:], 1.0), mark=True)
    oneswF.buf.wrote(_t)
    SrowF = TB(sb(f"SrowF_{tag}", (1, 8), F32))
    Rp = TB(ps(f"ffR_{tag}", (128, 512)))
    for c in range(8):
        mm(C, Rp.t[0:1, 0:8], C.cf32[:, 1:2], rwR.t[:, 8 * c:8 * (c + 1)],
           start=(c == 0), stop=(c == 7), rbufs=[C.cf32B, rwR.buf],
           wbuf=Rp.buf if c == 0 else None, mark=(c == 7))
    Rp.buf.wrote(("tensor", P.cnt["tensor"], P.opidx["tensor"]))
    Rp.buf.before_read(P, "scalar")
    SrowF.buf.before_write(P, "scalar")
    _t = P.emit("scalar", lambda h, s=Rp.t, d=SrowF.t: h.copy(d[0:1, 0:8], s[0:1, 0:8]),
                mark=True)
    SrowF.buf.wrote(_t); Rp.buf.read(_t)
    b2R = None
    if b2nm is not None:
        b2R = TB(sb(f"b2R_{tag}", (128, 8), F32))
        b2R.buf.before_write(P, "sync")
        b2R.buf.wrote(P.emit("sync", lambda h, d=b2R.t, s=C.D[b2nm]: h.dma_start(
            out=d[:], in_=s), dmaq="b2R" + tag))
    b2rowsR = None
    if b2fold_rows is not None:
        b2rowsR = TB(sb(f"b2rows_{tag}", (8, 1024), BF16))
        ld(b2rowsR, b2fold_rows, "qscalar", "scalar")
    e8R = None
    if want_experts:
        e8R = TB(sb(f"e8R_{tag}", (8, 1024), BF16)); ld(e8R, "e8sel", "qscalar", "scalar")
    xn = TB(sb(f"ffxn_{tag}", (128, 8 * W), BF16))
    tmp = TB(sb(f"fftmp_{tag}", (128, W), F32))
    wTsh = TB(ps(f"ffwT_{tag}", (128, 128), BF16))
    wT = wTsh
    rt = lambda nm, w, dt=F32: TB(sb(f"ffr_{nm}_{tag}", (128, w), dt))
    ctx_bufs = (rt("m1", 1), rt("ge1", 8), rt("Rm", 8), rt("m2", 1),
                rt("negm1", 1), rt("E", 8), rt("ge2", 8), rt("Eg", 8),
                rt("den", 1), rt("rden", 1), rt("wbf", 33, BF16))
    rcolT = (Rp, TB(sb(f"ffrcol_{tag}", (128, 1), F32)))
    _wbf = ctx_bufs[10]
    _wbf.buf.before_write(P, "vector")
    _t = P.emit("vector", lambda h, d=_wbf.t: h.memset(d[:], 0.0), mark=True)
    _wbf.buf.wrote(_t)
    maskrow = TB(sb(f"ffmr_{tag}", (8, W), BF16)) if want_experts else None
    gaterow = TB(sb(f"ffgr_{tag}", (1, W), BF16)) if want_gate else None
    h1t = ps(f"ffh1_{tag}", (128, 512))
    _h1tb = TB(h1t)
    h1p = [_h1tb, _h1tb]
    h1sb = [TB(sb(f"ffh1s_{tag}{i}", (128, W), BF16)) for i in range(2)]
    h1sc = [TB(sb(f"ffh1c_{tag}{i}", (128, W), BF16)) for i in range(2)] \
        if want_experts else None
    mbp = TB(ps(f"ffmbp_{tag}", (128, 512)))
    PO = [TB(ps(f"ffPO_{tag}{i}", (128, 512))) for i in range(4)]
    gtmp = TB(sb(f"ffgt_{tag}", (128, W), F32)) if want_gate else None
    gateS = TB(sb(f"ffgs_{tag}", (128, W), F32)) if want_gate else None

    ntg = (tok_hi - tok_lo) // W
    lnxbf2 = [lnxbf, TB(sb(f"lnx2_{tag}", (128, 8 * W), BF16))]
    _r1 = dict(lnrows)
    _r1.update({k: TB(sb(f"ffrow_{k}_{tag}", (1, W),
                         BF16 if k in ("nmb", "rb") else F32))
                for k in ("nm", "r", "nmb", "rb")})
    lnrows2 = [lnrows, _r1]
    maskrow2 = [maskrow, TB(sb(f"ffmr2_{tag}", (8, W), BF16))] \
        if want_experts else [None, None]
    gaterow2 = [gaterow, TB(sb(f"ffgr2_{tag}", (1, W), BF16))] \
        if want_gate else [None, None]
    pstatR = (_TBV(Rp.t, Rp.buf), _TBV(Rp.t[:, 256:512], Rp.buf))

    def prologue(it):
        if it >= ntg:
            return
        plo = tok_lo + it * W
        xf = [(C.x[c][:, plo:plo + W], C.xB[c]) for c in range(8)]
        lx = lnxbf2[it % 2]
        rws = lnrows2[it % 2]
        ln_group(C, xf, lx, lx, pstatR, rws, W, tag)
        if want_experts or want_gate:
            router_tokmajor(C, plo, W, rwR, rbR, rws["nm"], rws["r"],
                            SrowF, oneswF, rcolT,
                            Rp, list(ctx_bufs), want_experts, want_gate, wT,
                            maskrow2[it % 2], gaterow2[it % 2], 0, tag)

    prologue(0)
    for it in range(ntg):
        lo = tok_lo + it * W
        tsl = slice(lo, lo + W)
        rws = lnrows2[it % 2]
        mrow = maskrow2[it % 2]
        grow = gaterow2[it % 2]
        xf_chunks = [(C.x[c][:, tsl], C.xB[c]) for c in range(8)]
        bcast_mm(C, h1p[0].t[:, 0:W], h1p[0].buf, rws["nmb"].t[0:1, 0:W],
                 rws["nmb"].buf)
        bcast_mm(C, mbp.t[:, 0:W], mbp.buf, rws["rb"].t[0:1, 0:W],
                 rws["rb"].buf)
        xn_tts(C, xf_chunks, (h1p[0], slice(0, W)), (mbp, slice(0, W)), tmp, xn, W)
        if want_gate:
            bcast_mm(C, mbp.t[:, 0:W], mbp.buf, grow.t[0:1, 0:W], grow.buf)
            mbp.buf.before_read(P, "scalar")
            gateS.buf.before_write(P, "scalar")
            _t = P.emit("scalar", lambda h, s=mbp.t, d=gateS.t, W=W: h.copy(
                d[:, 0:W], s[:, 0:W]), mark=True)
            gateS.buf.wrote(_t); mbp.buf.read(_t)
        def emit_oproj(f, h1v, h1vb):
            for o in range(8):
                bankstop = (b2fold_rows is None and f == g.FFCH - 1 and o % 2 == 1)
                lastmm = (b2fold_rows is None and f == g.FFCH - 1 and o == 7)
                mm(C, PO[o // 2].t[:, 256 * (o % 2):256 * (o % 2) + W],
                   w2R.t[:, 1024 * f + 128 * o:1024 * f + 128 * (o + 1)], h1v,
                   start=(f == 0 and o % 2 == 0), stop=bankstop,
                   rbufs=[w2R.buf, h1vb], wbuf=PO[o // 2].buf if (f == 0 and o % 2 == 0) else None,
                   mark=lastmm)

        def w1load(f):
            if f < nres or f >= g.FFCH:
                return
            wS = w1S[f % 4]
            wS.buf.before_write(P, "sync")
            wS.buf.wrote(P.emit("sync", lambda h, d=wS.t, s=C.SC[w1nm], f=f:
                                h.dma_start(out=d[:],
                                            in_=s[:, 1024 * f:1024 * (f + 1)]),
                                dmaq="w1S" + tag + str(f % 4)))

        for _pf in range(nres, nres + 3):
            w1load(_pf)
        pend = None
        for f in range(g.FFCH):
            fb = f % 2
            h1c, fsl = h1p[fb], slice(0, W)
            if f < nres:
                w1ap, w1b = w1R.t[:, 1024 * f:1024 * (f + 1)], w1R.buf
            else:
                w1ap, w1b = w1S[f % 4].t, w1S[f % 4].buf
            w1load(f + 3)
            if f == 12:
                prologue(it + 1)
            for c in range(8):
                mm(C, h1c.t[:, fsl], w1ap[:, 128 * c:128 * (c + 1)],
                   xn.t[:, W * c:W * (c + 1)], start=(c == 0), stop=(c == 7),
                   rbufs=[w1b, xn.buf], wbuf=h1c.buf if c == 0 else None,
                   mark=(c == 7))
            h1c.buf.wrote(("tensor", P.cnt["tensor"], P.opidx["tensor"]))
            h1c.buf.before_read(P, "scalar")
            b1R.buf.before_read(P, "scalar")
            h1sb[fb].buf.before_write(P, "scalar")
            t = P.emit("scalar", lambda h, s=h1c.t, b=b1R.t, d=h1sb[fb].t, f=f,
                       fsl=fsl: h.activation(d[:], s[:, fsl], gelu_af,
                                             bias=b[:, f:f + 1]), mark=True)
            h1sb[fb].buf.wrote(t); h1c.buf.read(t); b1R.buf.read(t)
            h1v, h1vb = h1sb[fb].t[:, 0:W], h1sb[fb].buf
            if pend is not None:
                pend()
            if want_experts:
                n = f // (g.FFCH // 8)
                if f % (g.FFCH // 8) == 0:
                    t = mm(C, mbp.t[:, 0:W], e8R.t[0:8, 128 * n:128 * (n + 1)],
                           mrow.t[0:8, 0:W], start=True, stop=True,
                           rbufs=[e8R.buf, mrow.buf], wbuf=mbp.buf, mark=True)
                    mbp.buf.wrote(t)
                h1sb[fb].buf.before_read(P, "vector")
                mbp.buf.before_read(P, "vector")
                h1sc[fb].buf.before_write(P, "vector")
                t = P.emit("vector", lambda h, a=h1sb[fb].t, b=mbp.t, d=h1sc[fb].t,
                           W=W: h.tensor_mul(d[:], a[:], b[:, 0:W]), mark=True)
                h1sc[fb].buf.wrote(t); h1sb[fb].buf.read(t); mbp.buf.read(t)
                h1v, h1vb = h1sc[fb].t[:, 0:W], h1sc[fb].buf
            pend = (lambda f=f, h1v=h1v, h1vb=h1vb: emit_oproj(f, h1v, h1vb))
        pend()
        tpo = None
        for o in range(8):
            if b2rowsR is not None:
                tpo = mm(C, PO[o // 2].t[:, 256 * (o % 2):256 * (o % 2) + W],
                         b2rowsR.t[0:8, 128 * o:128 * (o + 1)], mrow.t[0:8, 0:W],
                         start=False, stop=(o % 2 == 1), rbufs=[b2rowsR.buf, mrow.buf],
                         wbuf=None, mark=True)
            else:
                pass
        if b2rowsR is None:
            tpo = ("tensor", P.cnt["tensor"], P.opidx["tensor"])
        for o in range(4):
            PO[o].buf.wrote(tpo if isinstance(tpo, tuple) else
                            ("tensor", P.cnt["tensor"], P.opidx["tensor"]))
        for o in range(8):
            POt, sl = PO[o // 2], slice(256 * (o % 2), 256 * (o % 2) + W)
            POt.buf.before_read(P, "vector")
            C.xB[o].before_write(P, "vector")
            if want_gate:
                b2R.buf.before_read(P, "vector")
                gateS.buf.before_read(P, "vector")
                gtmp.buf.before_write(P, "vector")
                t = P.emit("vector", lambda h, s=POt.t, b=b2R.t, m=gateS.t, d=gtmp.t,
                           o=o, sl=sl: h.scalar_tensor_tensor(
                               d[:], s[:, sl], b[:, o:o + 1], m[:, 0:W],
                               op0=ALU.add, op1=ALU.mult), mark=True)
                gtmp.buf.wrote(t); POt.buf.read(t); b2R.buf.read(t); gateS.buf.read(t)
                gtmp.buf.before_read(P, "vector")
                t = P.emit("vector", lambda h, a=gtmp.t, d=C.x[o], ts=tsl:
                           h.tensor_add(d[:, ts], d[:, ts], a[:]), mark=True)
                C.xB[o].wrote(t); gtmp.buf.read(t)
            else:
                t = P.emit("vector", lambda h, s=POt.t, d=C.x[o], ts=tsl, sl=sl:
                           h.tensor_add(d[:, ts], d[:, ts], s[:, sl]), mark=True)
                C.xB[o].wrote(t); POt.buf.read(t)


def p4_l0_ffn(C):
    g = C.g
    dense_ffn(C, "f0", 0, g.EXT, "w10", "b10", "w20", AF.Gelu, "rw0", "rb0",
              None, "b20r", True, False, nres=12)
    for c in range(8):
        C.dump(f"x2_{c}", C.x[c][:], [C.xB[c].wtok])


def p5_l1_qkv(C):
    g, P, sb, ps = C.g, C.P, C.sb, C.ps
    wR = TB(sb("wqkv1R", (128, 8 * 3088), BF16))
    wR.buf.before_write(P, "sync")
    wR.buf.wrote(P.emit("sync", lambda h, d=wR.t, s=C.SC["wqkv1"]: h.dma_start(
        out=d[:], in_=s), dmaq="wqkv1R"))
    bR = TB(sb("bqkv1R", (1, 3088), BF16))
    bR.buf.before_write(P, "scalar")
    bR.buf.wrote(P.emit("scalar", lambda h, d=bR.t, s=C.SC["bqkv1"]: h.dma_start(
        out=d[:], in_=s), dmaq="bqkv1R"))
    vrowf = TB(sb("vrowf5", (1, g.EXT), F32))
    vrowb = TB(sb("vrowb5", (1, g.EXT), BF16))
    vrowf.buf.before_write(P, "sync")
    vrowf.buf.wrote(P.emit("sync", lambda h, d=vrowf.t, s=C.D["validrow"]:
                           h.dma_start(out=d[:], in_=s), dmaq="vrow"))
    vrowf.buf.before_read(P, "vector")
    vrowb.buf.before_write(P, "vector")
    _t = P.emit("vector", lambda h, a=vrowf.t, d=vrowb.t: h.tensor_copy(d[:], a[:]),
                mark=True)
    vrowb.buf.wrote(_t); vrowf.buf.read(_t)
    vcolR = TB(sb("vcolR5", (128, g.EXTT), F32))
    vcolR.buf.before_write(P, "sync")
    vcolR.buf.wrote(P.emit("sync", lambda h, d=vcolR.t, s=C.D["validcol"]:
                           h.dma_start(out=d[:], in_=s), dmaq="vcol"))
    C.nmbF5 = TB(sb("nmbF5", (1, g.EXT), BF16))
    C.rbF5 = TB(sb("rbF5", (1, g.EXT), BF16))
    bc5 = TB(ps("pbc5", (128, 1024)))
    pstat5 = (_TBV(bc5.t[:, 0:512], bc5.buf), _TBV(bc5.t[:, 512:1024], bc5.buf))
    ln_rows_full(C, 0, g.EXT, (C.nmbF5, C.rbF5), "p5", pstat5)

    xn = TB(sb("xn5", (128, 8 * 512), BF16))
    tmp = TB(sb("tmp5", (128, 512), F32))
    vb = TB(ps("pvb5", (128, 512)))
    vbS = TB(sb("vbS5", (128, 512), F32))
    Pq = TB(ps("pq5", (128, 512)))
    Pv = TB(ps("pv5", (128, 1040)))
    evb2 = [TB(sb(f"evb5{i}", (128, 512), BF16)) for i in range(2)]
    evv2 = [TB(sb(f"evv5{i}", (128, 1040), BF16)) for i in range(2)]

    NQG = g.EXT // 512
    for ig in range(NQG):
        gsl = slice(512 * ig, 512 * (ig + 1))
        bcast_mm(C, bc5.t[:, 0:512], bc5.buf, C.nmbF5.t[0:1, gsl], C.nmbF5.buf)
        bcast_mm(C, bc5.t[:, 512:1024], bc5.buf, C.rbF5.t[0:1, gsl], C.rbF5.buf)
        xf_chunks = [(C.x[c][:, gsl], C.xB[c]) for c in range(8)]
        xn_tts(C, xf_chunks, (bc5, slice(0, 512)), (bc5, slice(512, 1024)), tmp, xn, 512)
        bcast_mm(C, vb.t[:], vb.buf, vrowb.t[0:1, gsl], vrowb.buf)
        vb.buf.before_read(P, "scalar")
        vbS.buf.before_write(P, "scalar")
        _t = P.emit("scalar", lambda h, s=vb.t, d=vbS.t: h.copy(d[:], s[:]), mark=True)
        vbS.buf.wrote(_t); vb.buf.read(_t)
        for ob in range(16):
            evb = evb2[ob % 2]
            oc = 128 * ob
            for c in range(8):
                mm(C, Pq.t[:], wR.t[:, 3088 * c + oc:3088 * c + oc + 128],
                   xn.t[:, 512 * c:512 * (c + 1)], start=(c == 0), stop=False,
                   rbufs=[wR.buf, xn.buf], wbuf=Pq.buf if c == 0 else None)
            t = mm(C, Pq.t[:], bR.t[0:1, oc:oc + 128], C.onesw[0:1, 0:512],
                   start=False, stop=True, rbufs=[bR.buf, C.oneswB], wbuf=None,
                   mark=True)
            Pq.buf.wrote(t)
            is_k = ob >= 8
            Pq.buf.before_read(P, "vector" if is_k else "scalar")
            evb.buf.before_write(P, "vector" if is_k else "scalar")
            if is_k:
                vbS.buf.before_read(P, "vector")
                t = P.emit("vector", lambda h, s=Pq.t, v=vbS.t, d=evb.t:
                           h.tensor_mul(d[:], s[:], v[:]), mark=True)
                vbS.buf.read(t)
            else:
                t = P.emit("scalar", lambda h, s=Pq.t, d=evb.t, sc=C.g.SCL:
                           h.mul(d[:], s[:], sc), mark=True)
            evb.buf.wrote(t); Pq.buf.read(t)
            evb.buf.before_read(P, "sync")
            dst = C.SC["k1"] if is_k else C.SC["q1"]
            blk = ob - 8 if is_k else ob
            t = P.emit("sync", lambda h, s=evb.t, d=dst, b=blk, gs=gsl: h.dma_start(
                out=d[:, g.EXT * b + gs.start:g.EXT * b + gs.stop], in_=s[:]),
                dmaq="kqout" + str(ob % 2))
            evb.buf.read(t)
        for k4 in range(4):
            evv = evv2[k4 % 2]
            tile = 4 * ig + k4
            nparts = ((1040 + 511) // 512)
            for c in range(8):
                for vn in range(nparts):
                    n0 = 512 * vn
                    n1 = min(1040, 512 * (vn + 1))
                    mm(C, Pv.t[:, n0:n1],
                       xn.t[:, 512 * c + 128 * k4:512 * c + 128 * (k4 + 1)],
                       wR.t[:, 3088 * c + 2048 + n0:3088 * c + 2048 + n1],
                       start=(c == 0), stop=False,
                       rbufs=[wR.buf, xn.buf], wbuf=Pv.buf if (c == 0 and vn == 0) else None)
            for vn in range(nparts):
                n0 = 512 * vn
                n1 = min(1040, 512 * (vn + 1))
                t = mm(C, Pv.t[:, n0:n1], C.onesw[0:1, 0:128],
                       bR.t[0:1, 2048 + n0:2048 + n1], start=False, stop=True,
                       rbufs=[C.oneswB, bR.buf], wbuf=None, mark=(vn == nparts - 1))
            Pv.buf.wrote(t)
            Pv.buf.before_read(P, "vector")
            vcolR.buf.before_read(P, "vector")
            evv.buf.before_write(P, "vector")
            t = P.emit("vector", lambda h, s=Pv.t, v=vcolR.t, d=evv.t, ti=tile:
                       h.tensor_scalar(
                           d[:].rearrange("p (a f) -> p a f", f=65)[:, :, 0:64],
                           s[:].rearrange("p (a f) -> p a f", f=65)[:, :, 0:64],
                           v[:, ti:ti + 1], None, op0=ALU.mult), mark=True)
            evv.buf.wrote(t); Pv.buf.read(t); vcolR.buf.read(t)
            Pv.buf.before_read(P, "scalar")
            evv.buf.before_write(P, "scalar")
            t = P.emit("scalar", lambda h, s=Pv.t, d=evv.t: h.copy(
                d[:].rearrange("p (a f) -> p a f", f=65)[:, :, 64:65],
                s[:].rearrange("p (a f) -> p a f", f=65)[:, :, 64:65]), mark=True)
            evv.buf.wrote(t); Pv.buf.read(t)
            evv.buf.before_read(P, "sync")
            t = P.emit("sync", lambda h, s=evv.t, d=C.SC["v1"], ti=tile: h.dma_start(
                out=d[:, 1040 * ti:1040 * (ti + 1)], in_=s[:]), dmaq="vout" + str(k4 % 2))
            evv.buf.read(t)


def p6_l1_attn(C):
    g, P, sb, ps = C.g, C.P, C.sb, C.ps
    wout1R = TB(sb("wout1R", (128, 8 * 1024), BF16))
    wout1R.buf.before_write(P, "sync")
    wout1R.buf.wrote(P.emit("sync", lambda h, d=wout1R.t, s=C.SC["wout1"]:
                            h.dma_start(out=d[:], in_=s), dmaq="wout1R"))
    bout1R = TB(sb("bout1R", (128, 8), F32))
    bout1R.buf.before_write(P, "sync")
    bout1R.buf.wrote(P.emit("sync", lambda h, d=bout1R.t, s=C.D["bout1"]:
                            h.dma_start(out=d[:], in_=s), dmaq="bout1R"))
    bandR = TB(sb("bandR", (128, 640), BF16))
    bandR.buf.before_write(P, "scalar")
    bandR.buf.wrote(P.emit("scalar", lambda h, d=bandR.t, s=C.SC["band"]:
                           h.dma_start(out=d[:], in_=s), dmaq="bandR"))
    bandKQ = TB(sb("bandKQ", (128, 640), BF16))
    bandKQ.buf.before_write(P, "scalar")
    bandKQ.buf.wrote(P.emit("scalar", lambda h, d=bandKQ.t, s=C.SC["bandkq"]:
                            h.dma_start(out=d[:], in_=s), dmaq="bandR"))

    Kwin = [TB(sb(f"Kw{i}", (128, 8 * 1024), BF16)) for i in range(2)]
    Vwin = [TB(sb("Vw0", (128, 8 * 1040), BF16))] * 2
    Qg = [TB(sb(f"Qg{i}", (128, 8 * 512), BF16)) for i in range(2)]
    psE = TB(ps("psE6", (128, 512)))
    ST = [TB(ps(f"pST6{i}", (128, 1024))) for i in range(2)]
    scm = [TB(sb(f"scm6{i}", (128, 640), F32)) for i in range(2)]
    mcol = TB(sb("mcol6", (128, 8), F32))
    negq = [TB(sb(f"negq6{i}", (1, 512), BF16)) for i in range(2)]
    Pmf = [TB(sb(f"Pmf6{i}", (128, 1024), F32)) for i in range(2)]
    Praw = [TB(sb(f"Praw6{i}", (128, 1024), BF16)) for i in range(3)]
    AV0 = TB(ps("pAV06", (65, 512)))
    AV1 = TB(ps("pAV16", (65, 512)))
    PO = TB(ps("pPO6", (128, 512)))
    recf = TB(sb("recf6", (1, 512), F32))
    recb = TB(sb("recb6", (1, 512), BF16))
    recf1 = TB(sb("recf6b", (1, 512), F32))
    recb1 = TB(sb("recb6b", (1, 512), BF16))
    bcS = TB(sb("bcS6", (64, 512), BF16))
    bcS1 = TB(sb("bcS6b", (64, 512), BF16))
    attnout = [TB(sb(f"ao6{p}", (128, 512), BF16)) for p in range(8)]
    CB = 300.0

    NGO = g.OWNT // 4
    relorder = [3, 0, 1, 2, 4, 5, 6, 7]
    for go in range(NGO):
        t0 = 2 + 4 * go
        cur = go % 2
        KwC, VwC, QgC = Kwin[cur], Vwin[cur], Qg[cur]
        KwC.buf.before_write(P, "sync")
        last = None
        for kb in range(8):
            last = P.emit("sync", lambda h, d=KwC.t, s=C.SC["k1"], kb=kb, t0=t0:
                          h.dma_start(out=d[:, 1024 * kb:1024 * (kb + 1)],
                                      in_=s[:, g.EXT * kb + 128 * (t0 - 2):
                                            g.EXT * kb + 128 * (t0 + 6)]),
                          dmaq="Kw" + str(cur))
        KwC.buf.wrote(last)
        VwC.buf.before_write(P, "sync")
        last = P.emit("sync", lambda h, d=VwC.t, s=C.SC["v1"], t0=t0: h.dma_start(
            out=d[:], in_=s[:, 1040 * (t0 - 2):1040 * (t0 + 6)]), dmaq="Vw" + str(cur))
        VwC.buf.wrote(last)
        QgC.buf.before_write(P, "sync")
        last = None
        for p in range(8):
            last = P.emit("sync", lambda h, d=QgC.t, s=C.SC["q1"], p=p, t0=t0:
                          h.dma_start(out=d[:, 512 * p:512 * (p + 1)],
                                      in_=s[:, g.EXT * p + 128 * t0:
                                            g.EXT * p + 128 * (t0 + 4)]),
                          dmaq="Qgl" + str(cur))
        QgC.buf.wrote(last)
        for p in range(8):
            for i in range(4):
                qsl = slice(512 * p + 128 * i, 512 * p + 128 * (i + 1))
                for hh in range(2):
                    r0 = 64 * hh
                    STc = ST[hh]
                    tlast = None
                    for (n0, n1) in ((0, 512), (512, 640)):
                        tlast = mm(C, STc.t[:, n0:n1], QgC.t[r0:r0 + 64, qsl],
                                   KwC.t[r0:r0 + 64, 1024 * p + 128 * i + n0:
                                         1024 * p + 128 * i + n1],
                                   start=True, stop=True,
                                   rbufs=[QgC.buf, KwC.buf], wbuf=STc.buf,
                                   tile_position=(r0, 0), mark=(n0 == 512))
                    STc.buf.wrote(tlast)
                for hh in range(2):
                    STc = ST[hh]
                    sc = scm[hh]
                    STc.buf.before_read(P, "vector")
                    bandR.buf.before_read(P, "vector")
                    sc.buf.before_write(P, "vector")
                    for (n0, n1) in ((0, 512), (512, 640)):
                        t = P.emit("vector", lambda h, a=STc.t, m=bandR.t, d=sc.t,
                                   n0=n0, n1=n1: h.scalar_tensor_tensor(
                                       d[:, n0:n1], a[:, n0:n1], CB, m[:, n0:n1],
                                       op0=ALU.add, op1=ALU.mult), mark=(n0 == 512))
                    sc.buf.wrote(t); STc.buf.read(t); bandR.buf.read(t)
                    sc.buf.before_read(P, "vector")
                    mcol.buf.before_write(P, "vector")
                    t = P.emit("vector", lambda h, a=sc.t, d=mcol.t, mc=2 * i + hh:
                               h.reduce_max(d[:, mc:mc + 1], a[:, 0:640],
                                            axis=AXL.X), mark=True)
                    mcol.buf.wrote(t); sc.buf.read(t)
            mcol.buf.before_read(P, "tensor")
            C.identfB.before_read(P, "tensor")
            psE.buf.before_write(P, "tensor")
            PO.buf.before_write(P, "tensor")
            tt = None
            for i in range(4):
                for hh in range(2):
                    PT = psE if hh == 0 else PO
                    tt = P.emit("tensor", lambda h, s=mcol.t, d=PT.t, mc=2 * i + hh,
                                i=i, i_=C.identf: h.transpose(
                                    d[0:1, 128 * i:128 * (i + 1)], s[:, mc:mc + 1],
                                    i_[:]), mark=(2 * i + hh == 7))
            psE.buf.wrote(tt); PO.buf.wrote(tt)
            mcol.buf.read(tt); C.identfB.read(tt)
            tneg = None
            for hh in range(2):
                PT = psE if hh == 0 else PO
                PT.buf.before_read(P, "scalar")
                negq[hh].buf.before_write(P, "scalar")
                tneg = P.emit("scalar", lambda h, s=PT.t, d=negq[hh].t:
                              h.activation(d[0:1, 0:512], s[0:1, 0:512], AF.Copy,
                                           scale=-1.0, bias=float(CB)),
                              mark=True)
                negq[hh].buf.wrote(tneg)
                PT.buf.read(tneg)
            pend = []
            h1g, h2g = 2 * p, 2 * p + 1

            def emit_av(ri, ag, Pr, N, qlo, qhi):
                st = (ri == 0); sp = (ri == 7)
                Pr.buf.before_read(P, "tensor")
                mm(C, AV0.t[:, qlo:qhi],
                   VwC.t[:, 1040 * ag + 65 * h1g:1040 * ag + 65 * (h1g + 1)],
                   Pr.t[:, 0:N], start=st, stop=sp, rbufs=[VwC.buf],
                   wbuf=AV0.buf if st else None)
                t = mm(C, AV1.t[:, qlo:qhi],
                       VwC.t[:, 1040 * ag + 65 * h2g:1040 * ag + 65 * (h2g + 1)],
                       Pr.t[:, 512:512 + N], start=st, stop=sp, rbufs=[],
                       wbuf=AV1.buf if st else None, mark=sp)
                Pr.buf.read(t)

            for ri, ag in enumerate(relorder):
                qlo = 128 * max(0, ag - 4)
                qhi = 128 * min(4, ag + 1)
                N = qhi - qlo
                zb = 128 * (4 - ag) + qlo
                STc = ST[ri % 2]
                Pf = Pmf[ri % 2]
                Pr = Praw[ri % 3]
                STc.buf.before_write(P, "tensor")
                for hh, (c0, tp) in ((0, (0, (0, 0))), (1, (512, (64, 0)))):
                    r0 = 64 * hh
                    mm(C, STc.t[:, c0:c0 + N],
                       KwC.t[r0:r0 + 64, 1024 * p + 128 * ag:1024 * p + 128 * (ag + 1)],
                       QgC.t[r0:r0 + 64, 512 * p + qlo:512 * p + qhi],
                       start=True, stop=False, rbufs=[KwC.buf, QgC.buf], wbuf=None,
                       tile_position=tp)
                    tlast = mm(C, STc.t[:, c0:c0 + N], C.onesw[0:1, 0:128],
                               negq[hh].t[0:1, qlo:qhi], start=False, stop=True,
                               rbufs=[C.oneswB, negq[hh].buf], wbuf=None,
                               mark=(hh == 1))
                STc.buf.wrote(tlast)
                STc.buf.before_read(P, "vector")
                bandKQ.buf.before_read(P, "vector")
                Pf.buf.before_write(P, "vector")
                for s_ in range(2):
                    t = P.emit("vector", lambda h, a=STc.t, m=bandKQ.t, d=Pf.t,
                               z=zb, N=N, s_=s_: h.scalar_tensor_tensor(
                                   d[:, 512 * s_:512 * s_ + N],
                                   a[:, 512 * s_:512 * s_ + N],
                                   CB, m[:, z:z + N],
                                   op0=ALU.add, op1=ALU.mult), mark=(s_ == 1))
                Pf.buf.wrote(t); STc.buf.read(t); bandKQ.buf.read(t)
                Pf.buf.before_read(P, "scalar")
                Pr.buf.before_write(P, "scalar")
                t = P.emit("scalar", lambda h, s=Pf.t, d=Pr.t, N=N, e=C.cf32:
                           h.activation(d[:, 0:N], s[:, 0:N], AF.Exp,
                                        bias=e[:, 2:3]), mark=False)
                t = P.emit("scalar", lambda h, s=Pf.t, d=Pr.t, N=N, e=C.cf32:
                           h.activation(d[:, 512:512 + N], s[:, 512:512 + N],
                                        AF.Exp, bias=e[:, 2:3]), mark=True)
                Pr.buf.wrote(t); Pf.buf.read(t)
                if len(pend) >= 2:
                    pend.pop(0)()
                pend.append(lambda ri=ri, ag=ag, Pr=Pr, N=N, qlo=qlo, qhi=qhi:
                            emit_av(ri, ag, Pr, N, qlo, qhi))
            while pend:
                pend.pop(0)()
            tav = ("tensor", P.cnt["tensor"], P.opidx["tensor"])
            AV0.buf.wrote(tav); AV1.buf.wrote(tav)
            slots = ((0, AV0, recf, recb, psE, bcS), (1, AV1, recf1, recb1, PO, bcS1))
            for s_, AVs, rf, rb, PB, bS in slots:
                AVs.buf.before_read(P, "scalar")
                rf.buf.before_write(P, "scalar")
                t = P.emit("scalar", lambda h, a=AVs.t, d=rf.t: h.activation(
                    d[0:1, :], a[64:65, :], AF.Ln), mark=True)
                rf.buf.wrote(t); AVs.buf.read(t)
            for s_, AVs, rf, rb, PB, bS in slots:
                rf.buf.before_read(P, "scalar")
                rb.buf.before_write(P, "scalar")
                t = P.emit("scalar", lambda h, a=rf.t, d=rb.t: h.activation(
                    d[0:1, :], a[0:1, :], AF.Exp, scale=-1.0), mark=True)
                rb.buf.wrote(t); rf.buf.read(t)
            for s_, AVs, rf, rb, PB, bS in slots:
                t = mm(C, PB.t[:, 0:512], C.ones[0:1, 0:128], rb.t[0:1, :],
                       start=True, stop=True, rbufs=[C.onesB, rb.buf],
                       wbuf=PB.buf, mark=True)
                PB.buf.wrote(t)
            for s_, AVs, rf, rb, PB, bS in slots:
                PB.buf.before_read(P, "scalar")
                bS.buf.before_write(P, "scalar")
                t = P.emit("scalar", lambda h, s=PB.t, d=bS.t: h.copy(
                    d[0:64, :], s[0:64, 0:512]), mark=True)
                bS.buf.wrote(t); PB.buf.read(t)
            for s_, AVs, rf, rb, PB, bS in slots:
                bS.buf.before_read(P, "vector")
                AVs.buf.before_read(P, "vector")
                attnout[p].buf.before_write(P, "vector")
                r0 = 64 * s_
                t = P.emit("vector", lambda h, a=AVs.t, b=bS.t, d=attnout[p].t,
                           r0=r0: h.tensor_mul(d[r0:r0 + 64, :], a[0:64, :],
                                               b[0:64, :]), mark=True)
                attnout[p].buf.wrote(t)
                AVs.buf.read(t); bS.buf.read(t)
        osl = slice(128 * t0, 128 * (t0 + 4))
        for o in range(8):
            for p in range(8):
                mm(C, PO.t[:], wout1R.t[:, 1024 * p + 128 * o:1024 * p + 128 * (o + 1)],
                   attnout[p].t[:], start=(p == 0), stop=(p == 7),
                   rbufs=[wout1R.buf, attnout[p].buf],
                   wbuf=PO.buf if p == 0 else None, mark=(p == 7))
            tpo = ("tensor", P.cnt["tensor"], P.opidx["tensor"])
            PO.buf.wrote(tpo)
            for p in range(8):
                attnout[p].buf.read(tpo)
            PO.buf.before_read(P, "vector")
            bout1R.buf.before_read(P, "vector")
            C.xB[o].before_write(P, "vector")
            t = P.emit("vector", lambda h, s=PO.t, b=bout1R.t, d=C.x[o], o=o, os_=osl:
                       h.scalar_tensor_tensor(d[:, os_], s[:], b[:, o:o + 1], d[:, os_],
                                              op0=ALU.add, op1=ALU.add), mark=True)
            C.xB[o].wrote(t)
            PO.buf.read(t); bout1R.buf.read(t)
    for c in range(8):
        C.dump(f"x3_{c}", C.x[c][:], [C.xB[c].wtok])


def p7_l1_ffn(C):
    g = C.g
    dense_ffn(C, "f1", g.HALO, g.HALO + g.OWN, "w11", "b11", "w21",
              AF.Gelu_apprx_tanh, "rw1", None, "b21", None, False, True, nres=13)


def p8_store(C):
    g, P = C.g, C.P
    for c in range(8):
        C.xB[c].before_read(P, "sync")
        t = P.emit("sync", lambda h, s=C.x[c], d=C.y, c=c: h.dma_start(
            out=d[:, g.OWN * c:g.OWN * (c + 1)],
            in_=s[:, g.HALO:g.HALO + g.OWN]), dmaq="yout")
        C.xB[c].read(t)
        C.final_toks.append(t)


def kernel(**inputs):
    g = Geom()
    nc = build(g, dbg=(), race_safe=True)
    in_maps = [stage_core_inputs(g, c, inputs) for c in range(8)]
    from concourse.bass_utils import run_bass_kernel_spmd
    res = run_bass_kernel_spmd(nc, in_maps, core_ids=list(range(8)))
    B, S, E = g.B, g.S, g.E
    full = np.zeros((B, S, E), np.float32)
    for c in range(8):
        b, h = c // 2, c % 2
        yc = np.asarray(res.results[c]["y"])
        xm = yc.reshape(128, 8, g.OWN).transpose(1, 0, 2).reshape(E, g.OWN)
        full[b, h * g.OWN:(h + 1) * g.OWN] = xm.T
    return full


# version 33
# speedup vs baseline: 1.0029x; 1.0029x over previous
import numpy as np
import concourse.bass as bass
import concourse.mybir as mybir

F32 = mybir.dt.float32
BF16 = mybir.dt.bfloat16
AF = mybir.ActivationFunctionType
ALU = mybir.AluOpType
AXL = mybir.AxisListType


class Geom:
    def __init__(self, S=4096, DFF=4096):
        self.B, self.S, self.E = 4, S, 1024
        self.NCH = 8
        self.OWN = S // 2
        self.HALO = 256
        self.EXT = self.OWN + 2 * self.HALO
        self.EXTT = self.EXT // 128
        self.OWNT = self.OWN // 128
        self.ST = S // 128
        self.H, self.KVH, self.D = 16, 4, 64
        self.QL, self.KVL = 64, 32
        self.ROPE = 10000.0
        self.SCL = self.D ** -0.5
        self.NG = self.EXTT // 4
        self.DFF = DFF
        self.FFCH = DFF // 128
        self.NEXP = 8
        self.FFPE = DFF // self.NEXP
        self.WIN = 512
        self.K_i = [min(i + self.OWNT - 1, self.ST) for i in range(self.EXTT)]
        self.K_i = [max(k, 1) for k in self.K_i]
        self.K_g = [self.K_i[4 * g + 3] for g in range(self.NG)]
        self.PAIRS0 = [(h, h + 4) for h in range(4)] + [(h + 8, h + 12) for h in range(4)]
        self.PAIRS1 = [(2 * p, 2 * p + 1) for p in range(8)]
        self.ZB = 128 * (max(self.K_g) - 1)
        zs = []
        for gi in range(self.NG):
            for j in range(self.K_g[gi]):
                if (128 * j + 127) <= (-self.HALO + 512 * gi):
                    continue
                zs.append(self.ZB + 512 * gi - 128 * j)
        self.ZOFF = min(zs)
        self.ZW = max(zs) + 512 + 1 - self.ZOFF
        self.ZW = (self.ZW + 3) // 4 * 4


def _dmchunk(w):
    din, dout = w.shape
    assert din % 128 == 0
    return np.ascontiguousarray(
        w.reshape(din // 128, 128, dout).transpose(1, 0, 2).reshape(128, -1)
    ).astype(np.float32)


def _dmchunk_f(w):
    din, dout = w.shape
    assert din == 1024
    a = w.reshape(8, 128, dout // 128, 128)
    return np.ascontiguousarray(
        a.transpose(1, 2, 0, 3).reshape(128, -1)).astype(np.float32)


def stage_core_inputs(g: Geom, core, full):
    b, h = core // 2, core % 2
    S, E, EXT, OWN, HALO = g.S, g.E, g.EXT, g.OWN, g.HALO
    d = {}
    x = np.asarray(full["L0_attn_x"])
    xb = x[b]
    lo = h * OWN - HALO
    ext = np.zeros((EXT, E), np.float32)
    s0, s1 = max(lo, 0), min(lo + EXT, S)
    ext[s0 - lo:s1 - lo] = xb[s0:s1]
    d["xext"] = np.ascontiguousarray(ext.T)
    d["xkv"] = np.ascontiguousarray(xb.T)

    ones = np.ones((128, 128), np.float32)
    d["cones"] = ones
    d["cneg"] = -ones
    d["cid"] = np.eye(128, dtype=np.float32)
    r = np.arange(128)[:, None]; f = np.arange(640)[None, :]
    d["band"] = ((f - r >= 0) & (f - r <= 511)).astype(np.float32)
    d["bandkq"] = ((f - r >= 1) & (f - r <= 512)).astype(np.float32)
    z = np.arange(g.ZW)[None, :]
    rr = np.arange(128)[:, None]
    keep = rr <= z + g.ZOFF + (lo - g.ZB)
    d["m01"] = keep.astype(np.float32)
    cf = np.zeros((128, 8), np.float32)
    cf[:, 0] = 1e-5
    cf[:, 1] = 1.0
    cf[:, 2] = -300.0
    cf[:, 3] = 300.0
    d["cf32"] = cf
    valid = np.zeros(EXT, np.float32)
    valid[s0 - lo:s1 - lo] = 1.0
    d["validrow"] = valid[None, :]
    d["validcol"] = np.ascontiguousarray(valid.reshape(g.EXTT, 128).T)

    jj = np.arange(32, dtype=np.float64)
    invf = g.ROPE ** (-jj / 32.0)
    posk = np.arange(S, dtype=np.float64)
    ang_k = posk[None, :] * invf[:, None]
    ck = np.cos(ang_k); sk = np.sin(ang_k)
    d["ropekc"] = np.tile(ck, (4, 1)).astype(np.float32)
    d["ropeks"] = np.tile(sk, (4, 1)).astype(np.float32)
    posq = np.arange(lo, lo + EXT, dtype=np.float64)
    ang_q = posq[None, :] * invf[:, None]
    cq = (np.cos(ang_q) * g.SCL).astype(np.float32)
    sq_ = (np.sin(ang_q) * g.SCL).astype(np.float32)
    d["ropeqc"] = np.tile(cq, (4, 1)).astype(np.float32)
    d["ropeqs"] = np.tile(sq_, (4, 1)).astype(np.float32)

    d["wqa"] = _dmchunk(np.asarray(full["L0_attn_Wqa"]))
    wqb = np.asarray(full["L0_attn_Wqb"])
    cols_e, cols_o = [], []
    for e in range(4):
        for hh in range(4 * e, 4 * e + 4):
            cols_e += [hh * 64 + 2 * j for j in range(32)]
            cols_o += [hh * 64 + 2 * j + 1 for j in range(32)]
    d["wqbpre"] = np.ascontiguousarray(
        np.concatenate([wqb[:, cols_e], wqb[:, cols_o]], axis=1)).astype(np.float32)
    d["wkva"] = _dmchunk(np.asarray(full["L0_attn_Wkva"]))
    wkvb = np.asarray(full["L0_attn_Wkvb"])
    colkE = [m * 128 + 4 * j for m in range(4) for j in range(32)]
    colkO = [m * 128 + 4 * j + 2 for m in range(4) for j in range(32)]
    d["wkvbkE"] = np.ascontiguousarray(wkvb[:, colkE]).astype(np.float32)
    d["wkvbkO"] = np.ascontiguousarray(wkvb[:, colkO]).astype(np.float32)
    wv = np.zeros((33, 4 * 65), np.float32)
    for m in range(4):
        for dd in range(64):
            wv[:32, m * 65 + dd] = wkvb[:, m * 128 + 2 * dd + 1]
        wv[32, m * 65 + 64] = 1.0
    d["wkvbv"] = wv
    wout0 = np.asarray(full["L0_attn_Wout"])
    rows = []
    for (h1, h2) in g.PAIRS0:
        for hh in (h1, h2):
            rows += [hh * 64 + 2 * j for j in range(32)]
            rows += [hh * 64 + 2 * j + 1 for j in range(32)]
    d["wout0"] = _dmchunk(np.ascontiguousarray(wout0[rows]))
    d["bout0"] = np.ascontiguousarray(
        np.asarray(full["L0_attn_bout"]).reshape(8, 128).T).astype(np.float32)

    d["rw0"] = _dmchunk(np.asarray(full["L0_ffn_router_w"]))
    d["rb0"] = np.asarray(full["L0_ffn_router_b"]).reshape(1, 8).astype(np.float32)
    W1 = np.asarray(full["L0_ffn_W1"])
    w1s = np.concatenate([W1[n] for n in range(g.NEXP)], axis=1)
    d["w10"] = _dmchunk_f(w1s)
    d["b10"] = np.ascontiguousarray(
        np.asarray(full["L0_ffn_b1"]).reshape(-1).reshape(g.FFCH, 128).T).astype(np.float32)
    W2 = np.asarray(full["L0_ffn_W2"])
    w2s = np.concatenate([W2[n] for n in range(g.NEXP)], axis=0)
    d["w20"] = np.ascontiguousarray(
        w2s.reshape(g.FFCH, 128, 1024).transpose(1, 0, 2).reshape(128, -1)).astype(np.float32)
    d["b20r"] = np.asarray(full["L0_ffn_b2"]).astype(np.float32)
    e8 = np.zeros((8, 8, 128), np.float32)
    for n in range(8):
        e8[n, n, :] = 1.0
    d["e8sel"] = e8.transpose(1, 0, 2).reshape(8, 8 * 128)

    wqkv = np.asarray(full["L1_attn_Wqkv"]).T
    bqkv = np.asarray(full["L1_attn_bqkv"])
    wq, wk, wv1 = wqkv[:, :1024], wqkv[:, 1024:2048], wqkv[:, 2048:]
    bq, bk, bv1 = bqkv[:1024], bqkv[1024:2048], bqkv[2048:]
    wvp = np.zeros((1024, 16 * 65), np.float32)
    bvp = np.zeros(16 * 65, np.float32)
    for hh in range(16):
        wvp[:, hh * 65:hh * 65 + 64] = wv1[:, hh * 64:hh * 64 + 64]
        bvp[hh * 65:hh * 65 + 64] = bv1[hh * 64:hh * 64 + 64]
        bvp[hh * 65 + 64] = 1.0
    wcat = np.concatenate([wq, wk, wvp], axis=1)
    bcat = np.concatenate([bq, bk, bvp])
    d["wqkv1"] = _dmchunk(wcat)
    d["bqkv1"] = bcat.reshape(1, -1).astype(np.float32)
    d["wout1"] = _dmchunk(np.asarray(full["L1_attn_Wout"]).T)
    d["bout1"] = np.ascontiguousarray(
        np.asarray(full["L1_attn_bout"]).reshape(8, 128).T).astype(np.float32)

    d["rw1"] = _dmchunk(np.asarray(full["L1_ffn_router_w"]))
    d["w11"] = _dmchunk_f(np.asarray(full["L1_ffn_w1"]))
    d["b11"] = np.ascontiguousarray(
        np.asarray(full["L1_ffn_b1"]).reshape(g.FFCH, 128).T).astype(np.float32)
    d["w21"] = np.ascontiguousarray(
        np.asarray(full["L1_ffn_w2"]).reshape(g.FFCH, 128, 1024)
        .transpose(1, 0, 2).reshape(128, -1)).astype(np.float32)
    d["b21"] = np.ascontiguousarray(
        np.asarray(full["L1_ffn_b2"]).reshape(8, 128).T).astype(np.float32)
    return d


ENGS = ("sync", "tensor", "vector", "scalar", "gpsimd")


class Prog:

    def __init__(self, nc, es):
        self.nc = nc
        self.es = es
        self.stream = {e: [] for e in ENGS}
        self.sem = {}
        self.cnt = {}
        for e in ("tensor", "vector", "scalar", "gpsimd"):
            self._mksem(e)
        self.waited = {e: {} for e in ENGS}
        self.opidx = {e: 0 for e in ENGS}
        self.drained = {e: 0 for e in ENGS}
        self.last_mark_op = {e: 0 for e in ENGS}
        self.nwait = 0
        self.ninstr = 0
        self.trace_sites = True

    def _mksem(self, key):
        if key not in self.sem:
            self.sem[key] = self.es.enter_context(self.nc.semaphore("s_" + key))
            self.cnt[key] = 0
        return self.sem[key]

    def emit(self, e, fn, mark=False, dmaq=None):
        self.ninstr += 1
        self.opidx[e] += 1
        incs = []
        tok = None
        if dmaq is not None:
            self._mksem(dmaq)
            self.cnt[dmaq] += 16
            incs.append((self.sem[dmaq], 16))
            tok = (dmaq, self.cnt[dmaq], None)
        elif mark:
            self.cnt[e] += 1
            incs.append((self.sem[e], 1))
            tok = (e, self.cnt[e], self.opidx[e])
            self.last_mark_op[e] = self.opidx[e]
        if self.trace_sites:
            import traceback
            st = traceback.extract_stack(limit=4)
            site = "|".join(f"{f.filename.split('/')[-1]}:{f.lineno}" for f in st[:-1])
        else:
            site = ""
        self.stream[e].append(("i", fn, incs, site))
        return tok

    def wait(self, consumer, tok):
        if tok is None:
            return
        src, val, op = tok
        if op is None and src not in ("tensor", "vector", "scalar", "gpsimd"):
            val = self.cnt[src]
        if src == consumer:
            if op is None or op <= self.drained[consumer]:
                return
            w = self.waited[consumer]
            if w.get(src, -1) < val:
                self.stream[consumer].append(("w", self.sem[src], val, src))
                self.nwait += 1
                w[src] = val
            self.drained[consumer] = max(self.drained[consumer], op)
            return
        w = self.waited[consumer]
        if w.get(src, -1) >= val:
            return
        self.stream[consumer].append(("w", self.sem[src], val, src))
        self.nwait += 1
        w[src] = val

    def waits(self, consumer, toks):
        for t in toks:
            self.wait(consumer, t)

    def barrier_token(self):
        toks = []
        for k, v in self.cnt.items():
            if v <= 0:
                continue
            if k in ("tensor", "vector", "scalar", "gpsimd"):
                toks.append((k, v, self.opidx[k]))
            else:
                toks.append((k, v, None))
        return toks

    def barrier(self):
        toks = self.barrier_token()
        for e in ENGS:
            self.waits(e, toks)

    def run(self):
        nc = self.nc
        with nc.Block() as blk:
            def mk(e):
                def body(h):
                    for kind, a, b, s in self.stream[e]:
                        if kind == "i":
                            inst = a(h)
                            for sem, v in b:
                                inst.then_inc(sem, v)
                        elif kind == "w":
                            h.wait_ge(a, b)
                        elif kind == "d":
                            h.drain()
                return body
            blk.sync(mk("sync"))
            blk.tensor(mk("tensor"))
            blk.vector(mk("vector"))
            blk.scalar(mk("scalar"))
            blk.gpsimd(mk("gpsimd"))


class Buf:
    __slots__ = ("wtok", "rtoks", "rpos")

    def __init__(self):
        self.wtok = None
        self.rtoks = []
        self.rpos = {}

    def before_write(self, P, engine):
        for t in (self.wtok if isinstance(self.wtok, list) else [self.wtok]):
            P.wait(engine, t)
        for t in self.rtoks:
            P.wait(engine, t)
        for e, pos in self.rpos.items():
            if e == engine:
                if pos > P.drained[engine]:
                    P.stream[engine].append(("d", None, None, ""))
                    P.opidx[engine] += 1
                    P.drained[engine] = P.opidx[engine]
            else:
                if P.last_mark_op[e] < pos:
                    raise RuntimeError(
                        f"before_write: unmarked trailing read on {e} at {pos} "
                        f"(last mark {P.last_mark_op[e]})")
                P.wait(engine, (e, P.cnt[e], P.last_mark_op[e]))

    def wrote(self, tok):
        self.wtok = tok
        self.rtoks = []
        self.rpos = {}

    def before_read(self, P, engine):
        for t in (self.wtok if isinstance(self.wtok, list) else [self.wtok]):
            P.wait(engine, t)

    def read(self, tok):
        if tok is not None:
            self.rtoks.append(tok)

    def read_at(self, P, engine):
        self.rpos[engine] = max(self.rpos.get(engine, 0), P.opidx[engine])


class Ctx:
    pass


def build(g: Geom, dbg=(), race_safe=True):
    from contextlib import ExitStack
    nc = bass.Bass("TRN2", target_bir_lowering=False,
                   detect_race_conditions=bool(race_safe))
    es = ExitStack()
    P = Prog(nc, es)
    C = Ctx()
    C.g, C.nc, C.P, C.es = g, nc, P, C
    C.dbg = set(dbg)
    C.dumps = []

    D = {}
    def din(name, shape):
        D[name] = nc.dram_tensor(name, list(shape), F32, kind="ExternalInput").ap()
    for nm, shp in [
        ("xext", (g.E, g.EXT)), ("xkv", (g.E, g.S)),
        ("cones", (128, 128)), ("cneg", (128, 128)), ("cid", (128, 128)),
        ("band", (128, 640)), ("bandkq", (128, 640)),
        ("m01", (128, g.ZW)), ("cf32", (128, 8)),
        ("validrow", (1, g.EXT)), ("validcol", (128, g.EXTT)),
        ("ropekc", (128, g.S)), ("ropeks", (128, g.S)),
        ("ropeqc", (128, g.EXT)), ("ropeqs", (128, g.EXT)),
        ("wqa", (128, 8 * 64)), ("wqbpre", (64, 1024)),
        ("wkva", (128, 8 * 32)), ("wkvbkE", (32, 128)), ("wkvbkO", (32, 128)),
        ("wkvbv", (33, 4 * 65)),
        ("wout0", (128, 8 * 1024)), ("bout0", (128, 8)),
        ("rw0", (128, 8 * 8)), ("rb0", (1, 8)),
        ("w10", (128, 8 * g.DFF)), ("b10", (128, g.FFCH)),
        ("w20", (128, g.FFCH * 1024)), ("b20r", (8, 1024)),
        ("e8sel", (8, 8 * 128)),
        ("wqkv1", (128, 8 * 3088)), ("bqkv1", (1, 3088)),
        ("wout1", (128, 8 * 1024)), ("bout1", (128, 8)),
        ("rw1", (128, 8 * 8)),
        ("w11", (128, 8 * g.DFF)), ("b11", (128, g.FFCH)),
        ("w21", (128, g.FFCH * 1024)), ("b21", (128, 8)),
    ]:
        din(nm, shp)
    C.D = D
    y = nc.dram_tensor("y", [128, g.NCH * g.OWN], F32, kind="ExternalOutput").ap()
    C.y = y
    SC = {}
    for nm in ("wqa", "wqbpre", "wkva", "wkvbkE", "wkvbkO", "wkvbv", "wout0",
               "rw0", "rb0", "w10", "w20", "b20r", "e8sel", "wqkv1", "bqkv1",
               "wout1", "rw1", "w11", "w21", "band", "bandkq", "m01",
               "cones", "cneg", "cid"):
        shp = list(D[nm].shape)
        SC[nm] = nc.dram_tensor("sc_" + nm, shp, BF16, kind="Internal").ap()
    C.SC = SC
    SC["k1"] = nc.dram_tensor("sc_k1", [128, 8 * g.EXT], BF16, kind="Internal").ap()
    SC["q1"] = nc.dram_tensor("sc_q1", [128, 8 * g.EXT], BF16, kind="Internal").ap()
    SC["v1"] = nc.dram_tensor("sc_v1", [128, g.EXTT * 1040], BF16, kind="Internal").ap()

    C.dbgd = {}
    def dbg_out(name, shape, dtype=F32):
        if name in C.dbg:
            C.dbgd[name] = nc.dram_tensor("dbg_" + name, list(shape), dtype,
                                          kind="ExternalOutput").ap()
    C.dbg_out = dbg_out
    dbg_out("ckv", (33, g.S), BF16)
    dbg_out("ka", (128, g.S), BF16)
    dbg_out("kb", (128, g.S), BF16)
    dbg_out("vtm", (128, g.ST * 260), BF16)
    dbg_out("kmaxsq", (1, 8))
    for p in range(8):
        dbg_out(f"qp{p}", (128, g.EXT), BF16)
    for c in range(8):
        dbg_out(f"x1_{c}", (128, g.EXT))
        dbg_out(f"x2_{c}", (128, g.EXT))
        dbg_out(f"x3_{c}", (128, g.EXT))

    def dump(name, ap, toks):
        if name not in C.dbg:
            return
        P.waits("sync", toks)
        t = P.emit("sync", lambda h, a=ap, d=C.dbgd[name]: h.dma_start(out=d, in_=a),
                   dmaq="dbg")
        C.final_toks.append(t)
    C.dump = dump
    C.final_toks = []

    C.cur_es = es
    C.glob_es = es
    sb = lambda nm, shp, dt: C.cur_es.enter_context(nc.sbuf_tensor(nm, list(shp), dt))
    C.sb = sb
    ps = lambda nm, shp, dt=F32: C.cur_es.enter_context(nc.psum_tensor(nm, list(shp), dt))
    C.ps = ps

    C.x = [sb(f"x{c}", (128, g.EXT), F32) for c in range(8)]
    C.xB = [Buf() for _ in range(8)]
    C.ones = sb("ones", (128, 128), BF16); C.onesB = Buf()
    C.neg = sb("negs", (1, 128), BF16); C.negB = Buf()
    C.ident = sb("ident", (128, 128), BF16); C.identB = Buf()
    C.identf = sb("identf", (128, 128), F32); C.identfB = Buf()
    C.onesw = sb("onesw", (1, 512), BF16); C.oneswB = Buf()
    C.cf32 = sb("cf32s", (128, 8), F32); C.cf32B = Buf()

    build_phases(C)
    globals()["_last_C"] = C

    P.waits("sync", C.final_toks)
    P.run()
    es.close()
    return nc


def ld_cast(C, name, q="qsync"):
    P, nc, g = C.P, C.nc, C.g
    src, dst = C.D[name], C.SC[name]
    pr, fr = src.shape
    step = 4096
    for off in range(0, fr, step):
        w = min(step, fr - off)
        stf = C.cast_f[C.cast_i % 4]; stb = C.cast_b[C.cast_i % 4]
        C.cast_i += 1
        stf.buf.before_write(P, "sync")
        t = P.emit("sync", lambda h, s=src, o=off, w=w, d=stf.t: h.dma_start(
            out=d[0:pr, 0:w], in_=s[:, o:o + w]), dmaq="cf" + str(C.cast_i % 4))
        stf.buf.wrote(t)
        stf.buf.before_read(P, "vector")
        stb.buf.before_write(P, "vector")
        t2 = P.emit("vector", lambda h, a=stf.t, b=stb.t, p=pr, w=w: h.tensor_copy(
            b[0:p, 0:w], a[0:p, 0:w]), mark=True)
        stf.buf.read(t2)
        stb.buf.wrote(t2)
        stb.buf.before_read(P, "sync")
        t3 = P.emit("sync", lambda h, d=dst, o=off, w=w, b=stb.t, p=pr: h.dma_start(
            out=d[:, o:o + w], in_=b[0:p, 0:w]), dmaq="cb" + str(C.cast_i % 4))
        stb.buf.read(t3)


class TB:
    def __init__(self, t):
        self.t = t
        self.buf = Buf()


def mm(C, out_ap, lhsT_ap, rhs_ap, *, start, stop, rbufs=(), wbuf=None, mark=False,
       tile_position=None):
    P = C.P
    for b in rbufs:
        b.before_read(P, "tensor")
    if wbuf is not None:
        if start:
            wbuf.before_write(P, "tensor")
        else:
            P.wait("tensor", wbuf.wtok)
    kw = {}
    if tile_position is not None:
        kw["tile_position"] = tile_position
    t = P.emit("tensor", lambda h, o=out_ap, l=lhsT_ap, r=rhs_ap, s=start, st=stop,
               kw=kw: h.matmul(o, l, r, start=s, stop=st, **kw), mark=mark)
    for b in rbufs:
        b.read_at(P, "tensor")
    if wbuf is not None:
        wbuf.wrote(t if mark else wbuf.wtok)
    return t


def ln_group(C, xf_chunks_ap, xbf, sqbf, pstat, rowbufs, W, prefix):
    P, g = C.P, C.g
    xbf.buf.before_write(P, "vector")
    last = None
    for c, (ap, b) in enumerate(xf_chunks_ap):
        b.before_read(P, "vector")
        last = P.emit("vector", lambda h, a=ap, d=xbf.t, c=c, W=W: h.tensor_copy(
            d[:, c * W:(c + 1) * W], a), mark=(c == 7))
        b.read(last if c == 7 else None)
    xbf.buf.wrote(last)
    psA, psB = pstat
    onescol = C.ones[:, 0:1]
    for c in range(8):
        mm(C, psA.t[0:1, 0:W], onescol, xbf.t[:, c * W:(c + 1) * W],
           start=(c == 0), stop=(c == 7), rbufs=[C.onesB, xbf.buf],
           wbuf=psA.buf if c == 0 else None, mark=(c == 7))
    xbf.buf.before_read(P, "scalar")
    sqbf.buf.before_write(P, "scalar")
    t = P.emit("scalar", lambda h, a=xbf.t, d=sqbf.t, W=W: h.activation(
        d[:, 0:8 * W], a[:, 0:8 * W], AF.Square), mark=True)
    sqbf.buf.wrote(t)
    sqbf.buf.before_read(P, "tensor")
    for c in range(8):
        mm(C, psB.t[0:1, 0:W], onescol, sqbf.t[:, c * W:(c + 1) * W],
           start=(c == 0), stop=(c == 7), rbufs=[sqbf.buf],
           wbuf=psB.buf if c == 0 else None, mark=(c == 7))
    tstat = ("tensor", P.cnt["tensor"], P.opidx["tensor"])
    psA.buf.wrote(tstat); psB.buf.wrote(tstat)
    nm, rr, v1, v2 = (rowbufs[k] for k in ("nm", "r", "v1", "v2"))
    nmb, rb = rowbufs["nmb"], rowbufs["rb"]
    psA.buf.before_read(P, "vector")
    nm.buf.before_write(P, "vector")
    t = P.emit("vector", lambda h, p=psA.t, d=nm.t, W=W: h.tensor_scalar_mul(
        d[0:1, 0:W], p[0:1, 0:W], -1.0 / 1024.0), mark=True)
    nm.buf.wrote(t); psA.buf.read(t)
    psB.buf.before_read(P, "vector")
    v1.buf.before_write(P, "vector")
    t = P.emit("vector", lambda h, p=psB.t, d=v1.t, W=W: h.tensor_scalar_mul(
        d[0:1, 0:W], p[0:1, 0:W], 1.0 / 1024.0), mark=True)
    v1.buf.wrote(t); psB.buf.read(t)
    nm.buf.before_read(P, "scalar")
    v2.buf.before_write(P, "scalar")
    t = P.emit("scalar", lambda h, a=nm.t, d=v2.t, W=W: h.activation(
        d[0:1, 0:W], a[0:1, 0:W], AF.Square), mark=True)
    v2.buf.wrote(t); nm.buf.read(t)
    v2.buf.before_read(P, "vector")
    v1.buf.before_write(P, "vector")
    t = P.emit("vector", lambda h, a=v1.t, b=v2.t, W=W: h.tensor_sub(
        a[0:1, 0:W], a[0:1, 0:W], b[0:1, 0:W]), mark=True)
    v1.buf.wrote(t); v2.buf.read(t)
    v1.buf.before_read(P, "scalar")
    v2.buf.before_write(P, "scalar")
    t = P.emit("scalar", lambda h, a=v1.t, d=v2.t, W=W, e=C.cf32: h.activation(
        d[0:1, 0:W], a[0:1, 0:W], AF.Sqrt, bias=e[0:1, 0:1]), mark=True)
    v2.buf.wrote(t); v1.buf.read(t); C.cf32B.read(t)
    v2.buf.before_read(P, "vector")
    rr.buf.before_write(P, "vector")
    t = P.emit("vector", lambda h, a=v2.t, d=rr.t, W=W: h.reciprocal(
        d[0:1, 0:W], a[0:1, 0:W]), mark=True)
    rr.buf.wrote(t); v2.buf.read(t)
    for s_, d_ in ((nm, nmb), (rr, rb)):
        s_.buf.before_read(P, "vector")
        d_.buf.before_write(P, "vector")
        t = P.emit("vector", lambda h, a=s_.t, d=d_.t, W=W: h.tensor_copy(
            d[0:1, 0:W], a[0:1, 0:W]), mark=True)
        d_.buf.wrote(t); s_.buf.read(t)


def bcast_mm(C, dst_psum_ap, dst_buf, src_row_ap, src_buf, row=0):
    tp = (row, 0) if row else None
    lhsT = C.ones[row:row + 1, 0:128]
    t = mm(C, dst_psum_ap, lhsT, src_row_ap, start=True, stop=True,
           rbufs=[C.onesB, src_buf], wbuf=dst_buf, mark=True, tile_position=tp)
    dst_buf.wrote(t)
    return t


def xn_tts(C, xf_chunks_ap, bcn, bcr, tmp, xn, W):
    P = C.P
    bcnT, bcnS = bcn
    bcrT, bcrS = bcr
    xn.buf.before_write(P, "vector")
    bcnT.buf.before_read(P, "vector")
    bcrT.buf.before_read(P, "vector")
    last = None
    for c, (ap, b) in enumerate(xf_chunks_ap):
        b.before_read(P, "vector")
        tmp.buf.before_write(P, "vector")
        t = P.emit("vector", lambda h, a=ap, m=bcnT.t, d=tmp.t, W=W, s=bcnS:
                   h.tensor_add(d[:, 0:W], a, m[:, s.start:s.start + W]), mark=True)
        tmp.buf.wrote(t)
        tmp.buf.before_read(P, "vector")
        last = P.emit("vector", lambda h, a=tmp.t, m=bcrT.t, d=xn.t, c=c, W=W, s=bcrS:
                      h.tensor_mul(d[:, c * W:(c + 1) * W], a[:, 0:W],
                                   m[:, s.start:s.start + W]), mark=True)
        tmp.buf.read(last)
        b.read(last)
    xn.buf.wrote(last)
    bcnT.buf.read(last)
    bcrT.buf.read(last)


def build_phases(C):
    g, P, nc, sb, ps = C.g, C.P, C.nc, C.sb, C.ps
    D, SC = C.D, C.SC


    from contextlib import ExitStack as _ES
    C.cur_es = _ES()
    C.cast_f = [TB(sb(f"cstf{i}", (128, 4096), F32)) for i in range(4)]
    C.cast_b = [TB(sb(f"cstb{i}", (128, 4096), BF16)) for i in range(4)]
    C.cast_i = 0

    def ld_f32(dst_ap, dst_buf, src_ap, q="ld0"):
        dst_buf.before_write(P, "sync")
        t = P.emit("sync", lambda h, d=dst_ap, s=src_ap: h.dma_start(out=d, in_=s),
                   dmaq=q)
        dst_buf.wrote(t)
        return t

    def ld_bf(dst_ap, dst_buf, src_name, src_sl=None, q="ld1", eng="sync"):
        src = C.SC[src_name]
        if src_sl is not None:
            src = src[:, src_sl[0]:src_sl[1]]
        dst_buf.before_write(P, eng)
        t = P.emit(eng, lambda h, d=dst_ap, s=src: h.dma_start(out=d, in_=s), dmaq=q)
        dst_buf.wrote(t)
        return t

    def cast_direct(dst_ap, dst_buf, src_name):
        pr, fr = dst_ap.shape
        stf = C.cast_f[C.cast_i % 4]
        C.cast_i += 1
        stf.buf.before_write(P, "sync")
        t = P.emit("sync", lambda h, s=D[src_name], d=stf.t, p=pr, f=fr: h.dma_start(
            out=d[0:p, 0:f], in_=s[0:p, 0:f]), dmaq="cf" + str(C.cast_i % 4))
        stf.buf.wrote(t)
        stf.buf.before_read(P, "vector")
        dst_buf.before_write(P, "vector")
        t2 = P.emit("vector", lambda h, a=stf.t, d=dst_ap, p=pr, f=fr: h.tensor_copy(
            d, a[0:p, 0:f]), mark=True)
        stf.buf.read(t2)
        dst_buf.wrote(t2)

    cast_direct(C.ones[:], C.onesB, "cones")
    cast_direct(C.neg[:], C.negB, "cneg")
    cast_direct(C.ident[:], C.identB, "cid")
    ld_f32(C.identf[:], C.identfB, D["cid"])
    ld_f32(C.cf32[:], C.cf32B, D["cf32"])
    C.oneswB.before_write(P, "vector")
    t = P.emit("vector", lambda h: h.memset(C.onesw[:], 1.0), mark=True)
    C.oneswB.wrote(t)

    for nm in ("wkva", "wkvbkE", "wkvbkO", "wkvbv", "wqa", "wqbpre", "m01",
               "wout0", "rw0", "rb0", "e8sel", "b20r", "w10", "w20",
               "wqkv1", "bqkv1", "band", "bandkq", "wout1", "rw1", "w11", "w21"):
        ld_cast(C, nm)

    for c in range(8):
        ld_f32(C.x[c][:], C.xB[c], D["xext"][128 * c:128 * (c + 1), :])

    from contextlib import ExitStack
    _p0es = C.cur_es
    C.cur_es = C.glob_es
    C.attn_es = ExitStack()
    def phase(fn):
        C.cur_es = ExitStack()
        fn(C)
        P.barrier()
        C.cur_es.close()
        C.cur_es = C.glob_es
    P.barrier()
    _p0es.close()
    phase(p1_l0_kv)
    phase(p2_l0_q)
    phase(p3_l0_attn)
    C.attn_es.close()
    phase(p4_l0_ffn)
    phase(p5_l1_qkv)
    phase(p6_l1_attn)
    phase(p7_l1_ffn)
    p8_store(C)


_ROWCTR = [0]
def mk_rows(C, W=512):
    sb = C.sb
    _ROWCTR[0] += 1
    i = _ROWCTR[0]
    d = {k: TB(sb(f"row_{k}_{i}", (1, W), F32)) for k in ("nm", "r", "v1", "v2")}
    d.update({k: TB(sb(f"row_{k}_{i}", (1, W), BF16)) for k in ("nmb", "rb")})
    return d


def p1_l0_kv(C):
    g, P, sb, ps = C.g, C.P, C.sb, C.ps
    _sba = lambda nm, shp, dt: C.attn_es.enter_context(
        C.nc.sbuf_tensor(nm, list(shp), dt))
    C.KA = _sba("KA", (128, g.S), BF16); C.KAB = Buf()
    C.KB = _sba("KB", (128, g.S), BF16); C.KBB = Buf()
    C.Vtm = _sba("Vtm", (128, g.ST * 260), BF16); C.VtmB = Buf()
    C.kmaxsq = _sba("kmaxsq", (1, 8), F32); C.kmaxsqB = Buf()

    es = C.es
    xkvf = [TB(sb("xkvf0", (128, 8 * 512), F32))] * 2
    xkvbf2 = [TB(sb(f"xkvbf{i}", (128, 8 * 512), BF16)) for i in range(2)]
    pstat = (TB(ps("pstat1a", (1, 512))), TB(ps("pstat1b", (1, 512))))
    rows2 = [mk_rows(C), None]
    _r1 = dict(rows2[0])
    _ROWALT = {k: TB(sb(f"row_{k}_alt1", (1, 512),
                        BF16 if k in ("nmb", "rb") else F32))
               for k in ("nm", "r", "nmb", "rb")}
    _r1.update(_ROWALT)
    rows2[1] = _r1
    bc1 = TB(ps("pbc1", (128, 1024)))
    tmp = TB(sb("tmp1", (128, 512), F32))
    xnkv = TB(sb("xnkv", (128, 8 * 512), BF16))
    ckv = TB(sb("ckv", (33, g.S), BF16))
    ckvP = TB(ps("pckv", (32, 512)))
    KEp = TB(ps("pKE", (128, 512)))
    KOp = TB(ps("pKO", (128, 512)))
    t1 = TB(sb("rt1", (128, 512), F32))
    t2 = TB(sb("rt2", (128, 512), F32))
    rEb2 = [TB(sb(f"rEb{i}", (128, 512), BF16)) for i in range(2)]
    rOb2 = [TB(sb(f"rOb{i}", (128, 512), BF16)) for i in range(2)]
    Vp = TB(ps("pV", (128, 260)))
    ropekc = TB(sb("ropekcS", (128, 512), F32))
    ropeks = TB(sb("ropeksS", (128, 512), F32))
    wkvaR = TB(sb("wkvaR", (128, 256), BF16))
    wkEK = TB(sb("wkEK", (32, 128), BF16))
    wkOK = TB(sb("wkOK", (32, 128), BF16))
    wvR = TB(sb("wvR", (33, 260), BF16))

    for tb, nm in ((wkvaR, "wkva"), (wkEK, "wkvbkE"), (wkOK, "wkvbkO"), (wvR, "wkvbv")):
        tb.buf.before_write(P, "scalar")
        t = P.emit("scalar", lambda h, d=tb.t, s=C.SC[nm]: h.dma_start(out=d[:], in_=s),
                   dmaq="kvw")
        tb.buf.wrote(t)

    ckv.buf.before_write(P, "vector")
    t = P.emit("vector", lambda h, d=ckv.t: h.memset(d[32:33, :], 1.0), mark=True)
    ckv.buf.wrote(t)

    NKG = g.S // 512
    for ig in range(NKG):
        cur = xkvf[ig % 2]
        xkvbf = xkvbf2[ig % 2]
        sqbf = xkvbf
        rows = rows2[ig % 2]
        rEb, rOb = rEb2[ig % 2], rOb2[ig % 2]
        gsl = slice(512 * ig, 512 * (ig + 1))
        for tb, nm in ((ropekc, "ropekc"), (ropeks, "ropeks")):
            tb.buf.before_write(P, "sync")
            t = P.emit("sync", lambda h, d=tb.t, s=C.D[nm], gs=gsl: h.dma_start(
                out=d[:], in_=s[:, gs]), dmaq="ropek")
            tb.buf.wrote(t)
        cur.buf.before_write(P, "sync")
        last = None
        for c in range(8):
            last = P.emit("sync", lambda h, d=cur.t, s=C.D["xkv"], c=c, gs=gsl:
                          h.dma_start(out=d[:, 512 * c:512 * (c + 1)],
                                      in_=s[128 * c:128 * (c + 1), gs]), dmaq="xkvf" + str(ig % 2))
        cur.buf.wrote(last)
        xf_chunks = [(cur.t[:, 512 * c:512 * (c + 1)], cur.buf) for c in range(8)]
        ln_group(C, xf_chunks, xkvbf, sqbf, pstat, rows, 512, "kv")
        bcast_mm(C, bc1.t[:, 0:512], bc1.buf, rows["nmb"].t[0:1, :], rows["nmb"].buf)
        bcast_mm(C, bc1.t[:, 512:1024], bc1.buf, rows["rb"].t[0:1, :], rows["rb"].buf)
        xn_tts(C, xf_chunks, (bc1, slice(0, 512)), (bc1, slice(512, 1024)), tmp, xnkv, 512)
        for c in range(8):
            mm(C, ckvP.t[:], wkvaR.t[:, 32 * c:32 * (c + 1)],
               xnkv.t[:, 512 * c:512 * (c + 1)], start=(c == 0), stop=(c == 7),
               rbufs=[wkvaR.buf, xnkv.buf], wbuf=ckvP.buf if c == 0 else None,
               mark=(c == 7))
        ckvP.buf.wrote(("tensor", P.cnt["tensor"], P.opidx["tensor"]))
        ckvP.buf.before_read(P, "scalar")
        ckv.buf.before_write(P, "scalar")
        t = P.emit("scalar", lambda h, s=ckvP.t, d=ckv.t, gs=gsl: h.copy(
            d[0:32, gs], s[:]), mark=True)
        ckv.buf.wrote(t); ckvP.buf.read(t)
        mm(C, KEp.t[:], wkEK.t[:], ckv.t[0:32, gsl], start=True, stop=True,
           rbufs=[wkEK.buf, ckv.buf], wbuf=KEp.buf, mark=True)
        KEp.buf.wrote(("tensor", P.cnt["tensor"], P.opidx["tensor"]))
        mm(C, KOp.t[:], wkOK.t[:], ckv.t[0:32, gsl], start=True, stop=True,
           rbufs=[wkOK.buf], wbuf=KOp.buf, mark=True)
        KOp.buf.wrote(("tensor", P.cnt["tensor"], P.opidx["tensor"]))
        for (pa, pb, wa, wb, out, op) in (
                (KEp, KOp, ropekc, ropeks, rEb, "sub"),
                (KEp, KOp, ropeks, ropekc, rOb, "add")):
            pa.buf.before_read(P, "vector"); wa.buf.before_read(P, "vector")
            t1.buf.before_write(P, "vector")
            tt = P.emit("vector", lambda h, a=pa.t, w=wa.t, d=t1.t:
                        h.tensor_mul(d[:], a[:], w[:, 0:512]), mark=True)
            t1.buf.wrote(tt); wa.buf.read(tt)
            pb.buf.before_read(P, "vector"); wb.buf.before_read(P, "vector")
            t2.buf.before_write(P, "vector")
            tt = P.emit("vector", lambda h, a=pb.t, w=wb.t, d=t2.t:
                        h.tensor_mul(d[:], a[:], w[:, 0:512]), mark=True)
            t2.buf.wrote(tt); wb.buf.read(tt)
            t1.buf.before_read(P, "vector"); t2.buf.before_read(P, "vector")
            out.buf.before_write(P, "vector")
            if op == "sub":
                tt = P.emit("vector", lambda h, d=out.t: h.tensor_sub(
                    d[:], t1.t[:], t2.t[:]), mark=True)
            else:
                tt = P.emit("vector", lambda h, d=out.t: h.tensor_add(
                    d[:], t1.t[:], t2.t[:]), mark=True)
            out.buf.wrote(tt); t1.buf.read(tt); t2.buf.read(tt)
        KEp.buf.read(tt); KOp.buf.read(tt)
        for e_ in ("gpsimd", "sync"):
            rEb.buf.before_read(P, e_); rOb.buf.before_read(P, e_)
            C.KAB.before_write(P, e_); C.KBB.before_write(P, e_)
        dts = []
        for m in range(4):
            KT, KTB = (C.KA, C.KAB) if m < 2 else (C.KB, C.KBB)
            base = 64 * (m % 2)
            for ei, (srct, srcb, off) in enumerate(((rEb, rEb.buf, 0),
                                                    (rOb, rOb.buf, 32))):
                eng = "gpsimd" if (2 * m + ei) % 2 == 0 else "sync"
                tt = P.emit(eng, lambda h, d=KT, s=srct.t, m=m, b=base, o=off,
                            gs=gsl: h.dma_start(
                                out=d[b + o:b + o + 32, gs],
                                in_=s[32 * m:32 * m + 32, :]),
                            dmaq="krs" + eng[0])
                dts.append(tt)
        for tt in dts:
            rEb.buf.read(tt); rOb.buf.read(tt)
        C.KAB.wrote([dts[-1], dts[-2]]); C.KBB.wrote([dts[-1], dts[-2]])
        for k4 in range(4):
            kt = 4 * ig + k4
            mm(C, Vp.t[:], ckv.t[:, 512 * ig + 128 * k4: 512 * ig + 128 * (k4 + 1)],
               wvR.t[:], start=True, stop=True, rbufs=[ckv.buf, wvR.buf],
               wbuf=Vp.buf, mark=True)
            Vp.buf.wrote(("tensor", P.cnt["tensor"], P.opidx["tensor"]))
            Vp.buf.before_read(P, "scalar")
            C.VtmB.before_write(P, "scalar")
            t = P.emit("scalar", lambda h, s=Vp.t, d=C.Vtm, kt=kt: h.copy(
                d[:, 260 * kt:260 * (kt + 1)], s[:]), mark=True)
            C.VtmB.wrote(t); Vp.buf.read(t)
    C.kmaxsqB.before_write(P, "vector")
    t = P.emit("vector", lambda h, d=C.kmaxsq: h.memset(d[:], 0.0), mark=True)
    C.kmaxsqB.wrote(t)
    kcm = TB(sb("kcm", (1, 16), F32))
    sqk = TB(sb("sqk", (128, 512), BF16))
    psk = KEp
    for ti, (KT, KTB) in enumerate(((C.KA, C.KAB), (C.KB, C.KBB))):
        for chk in range(g.S // 512):
            KTB.before_read(P, "scalar")
            sqk.buf.before_write(P, "scalar")
            t = P.emit("scalar", lambda h, a=KT, d=sqk.t, c=chk: h.activation(
                d[:], a[:, 512 * c:512 * (c + 1)], AF.Square), mark=True)
            sqk.buf.wrote(t); KTB.read(t)
            mm(C, psk.t[0:1, 0:512], C.ones[:, 0:1], sqk.t[:], start=True, stop=True,
               rbufs=[C.onesB, sqk.buf], wbuf=psk.buf, mark=True)
            psk.buf.wrote(("tensor", P.cnt["tensor"], P.opidx["tensor"]))
            psk.buf.before_read(P, "vector")
            kcm.buf.before_write(P, "vector")
            t = P.emit("vector", lambda h, s=psk.t, d=kcm.t, i=8 * ti + chk:
                       h.reduce_max(d[0:1, i:i + 1], s[0:1, 0:512], axis=AXL.X), mark=True)
            kcm.buf.wrote(t); psk.buf.read(t)
        kcm.buf.before_read(P, "vector")
        C.kmaxsqB.before_write(P, "vector")
        nch = g.S // 512
        t = P.emit("vector", lambda h, s=kcm.t, d=C.kmaxsq, ti=ti, n=nch: h.reduce_max(
            d[0:1, ti:ti + 1], s[0:1, 8 * ti:8 * ti + n], axis=AXL.X), mark=True)
        C.kmaxsqB.wrote(t); kcm.buf.read(t)
    C.dump("ckv", ckv.t[:], [ckv.buf.wtok])
    C.dump("ka", C.KA[:], [C.KAB.wtok])
    C.dump("kb", C.KB[:], [C.KBB.wtok])
    C.dump("vtm", C.Vtm[:], [C.VtmB.wtok])
    C.dump("kmaxsq", C.kmaxsq[:], [C.kmaxsqB.wtok])


def _noop():
    from contextlib import nullcontext
    return nullcontext()


def p2_l0_q(C):
    g, P, sb, ps = C.g, C.P, C.sb, C.ps
    _sba = lambda nm, shp, dt: C.attn_es.enter_context(
        C.nc.sbuf_tensor(nm, list(shp), dt))
    C.Qp = [_sba(f"Qp{p}", (128, g.EXT), BF16) for p in range(8)]
    C.QpB = [Buf() for _ in range(8)]

    ropeqc = TB(sb("ropeqcS", (128, 512), F32))
    ropeqs = TB(sb("ropeqsS", (128, 512), F32))
    wqaR = TB(sb("wqaR", (128, 512), BF16))
    wqbR = TB(sb("wqbR", (64, 1024), BF16))
    xqbf = TB(sb("xqbf", (128, 8 * 512), BF16))
    sqbf = xqbf
    pstat = (TB(ps("pstat2a", (1, 512))), TB(ps("pstat2b", (1, 512))))
    rows = mk_rows(C)
    bc2 = TB(ps("pbc2", (128, 1024)))
    tmp = TB(sb("tmp2", (128, 512), F32))
    xnq = TB(sb("xnq", (128, 8 * 512), BF16))
    qlatP = TB(ps("pqlat", (64, 512)))
    qlat = TB(sb("qlat", (64, 512), BF16))
    QEp = TB(ps("pQE", (128, 512)))
    QOp = TB(ps("pQO", (128, 512)))
    t1v = [TB(sb(f"qt1{i}", (128, 512), F32)) for i in range(2)]
    t2v = [TB(sb(f"qt2{i}", (128, 512), F32)) for i in range(2)]
    rEbv = [TB(sb(f"qrEb{i}", (128, 512), BF16)) for i in range(2)]
    rObv = [TB(sb(f"qrOb{i}", (128, 512), BF16)) for i in range(2)]

    for tb, nm in ((wqaR, "wqa"), (wqbR, "wqbpre")):
        tb.buf.before_write(P, "scalar")
        t = P.emit("scalar", lambda h, d=tb.t, s=C.SC[nm]: h.dma_start(out=d[:], in_=s),
                   dmaq="qw")
        tb.buf.wrote(t)

    def pmap(h):
        if h < 4: return (h, 0)
        if h < 8: return (h - 4, 1)
        if h < 12: return (h - 4, 0)
        return (h - 8, 1)

    NQG = g.EXT // 512
    for ig in range(NQG):
        gsl = slice(512 * ig, 512 * (ig + 1))
        for tb, nm in ((ropeqc, "ropeqc"), (ropeqs, "ropeqs")):
            tb.buf.before_write(P, "sync")
            t = P.emit("sync", lambda h, d=tb.t, s=C.D[nm], gs=gsl: h.dma_start(
                out=d[:], in_=s[:, gs]), dmaq="ropeq")
            tb.buf.wrote(t)
        xf_chunks = [(C.x[c][:, gsl], C.xB[c]) for c in range(8)]
        ln_group(C, xf_chunks, xqbf, sqbf, pstat, rows, 512, "q")
        bcast_mm(C, bc2.t[:, 0:512], bc2.buf, rows["nmb"].t[0:1, :], rows["nmb"].buf)
        bcast_mm(C, bc2.t[:, 512:1024], bc2.buf, rows["rb"].t[0:1, :], rows["rb"].buf)
        xn_tts(C, xf_chunks, (bc2, slice(0, 512)), (bc2, slice(512, 1024)), tmp, xnq, 512)
        for c in range(8):
            mm(C, qlatP.t[:], wqaR.t[:, 64 * c:64 * (c + 1)],
               xnq.t[:, 512 * c:512 * (c + 1)], start=(c == 0), stop=(c == 7),
               rbufs=[wqaR.buf, xnq.buf], wbuf=qlatP.buf if c == 0 else None,
               mark=(c == 7))
        qlatP.buf.wrote(("tensor", P.cnt["tensor"], P.opidx["tensor"]))
        qlatP.buf.before_read(P, "scalar")
        qlat.buf.before_write(P, "scalar")
        t = P.emit("scalar", lambda h, s=qlatP.t, d=qlat.t: h.copy(d[:], s[:]),
                   mark=True)
        qlat.buf.wrote(t); qlatP.buf.read(t)
        for e in range(4):
            t1, t2 = t1v[e % 2], t2v[e % 2]
            rEb, rOb = rEbv[e % 2], rObv[e % 2]
            mm(C, QEp.t[:], wqbR.t[:, 128 * e:128 * (e + 1)], qlat.t[:],
               start=True, stop=True, rbufs=[wqbR.buf, qlat.buf], wbuf=QEp.buf,
               mark=True)
            QEp.buf.wrote(("tensor", P.cnt["tensor"], P.opidx["tensor"]))
            mm(C, QOp.t[:], wqbR.t[:, 128 * (e + 4):128 * (e + 5)], qlat.t[:],
               start=True, stop=True, rbufs=[], wbuf=QOp.buf, mark=True)
            QOp.buf.wrote(("tensor", P.cnt["tensor"], P.opidx["tensor"]))
            for (pa, pb, wa, wb, out, op) in (
                    (QEp, QOp, ropeqc, ropeqs, rEb, "sub"),
                    (QEp, QOp, ropeqs, ropeqc, rOb, "add")):
                pa.buf.before_read(P, "vector"); wa.buf.before_read(P, "vector")
                t1.buf.before_write(P, "vector")
                tt = P.emit("vector", lambda h, a=pa.t, w=wa.t, d=t1.t:
                            h.tensor_mul(d[:], a[:], w[:, 0:512]), mark=True)
                t1.buf.wrote(tt); wa.buf.read(tt)
                pb.buf.before_read(P, "vector"); wb.buf.before_read(P, "vector")
                t2.buf.before_write(P, "vector")
                tt = P.emit("vector", lambda h, a=pb.t, w=wb.t, d=t2.t:
                            h.tensor_mul(d[:], a[:], w[:, 0:512]), mark=True)
                t2.buf.wrote(tt); wb.buf.read(tt)
                t1.buf.before_read(P, "vector"); t2.buf.before_read(P, "vector")
                out.buf.before_write(P, "vector")
                if op == "sub":
                    tt = P.emit("vector", lambda h, d=out.t: h.tensor_sub(
                        d[:], t1.t[:], t2.t[:]), mark=True)
                else:
                    tt = P.emit("vector", lambda h, d=out.t: h.tensor_add(
                        d[:], t1.t[:], t2.t[:]), mark=True)
                out.buf.wrote(tt); t1.buf.read(tt); t2.buf.read(tt)
            QEp.buf.read(tt); QOp.buf.read(tt)
            dts = []
            for e_ in ("gpsimd", "sync"):
                rEb.buf.before_read(P, e_); rOb.buf.before_read(P, e_)
            for u in range(4):
                hh = 4 * e + u
                p, s = pmap(hh)
                eng = "gpsimd" if u % 2 == 0 else "sync"
                C.QpB[p].before_write(P, eng)
                tt = P.emit(eng, lambda h, d=C.Qp[p], s_=rEb.t, u=u, so=64 * s,
                            gs=gsl: h.dma_start(out=d[so:so + 32, gs],
                                                in_=s_[32 * u:32 * u + 32, :]),
                            dmaq="qrs" + eng[0])
                dts.append((p, tt))
                tt = P.emit(eng, lambda h, d=C.Qp[p], s_=rOb.t, u=u, so=64 * s,
                            gs=gsl: h.dma_start(out=d[so + 32:so + 64, gs],
                                                in_=s_[32 * u:32 * u + 32, :]),
                            dmaq="qrs" + eng[0])
                dts.append((p, tt))
            for p, tt in dts:
                C.QpB[p].wrote(tt)
                rEb.buf.read(tt); rOb.buf.read(tt)
    for p in range(8):
        C.dump(f"qp{p}", C.Qp[p][:], [C.QpB[p].wtok])


def p3_l0_attn(C):
    g, P, sb, ps = C.g, C.P, C.sb, C.ps
    m01R = TB(sb("m01R", (128, g.ZW), BF16))
    wout0R = TB(sb("wout0R", (128, 8 * 1024), BF16))
    bout0R = TB(sb("bout0R", (128, 8), F32))
    for tb, nm, q, eng in ((m01R, "m01", "qscalar", "scalar"),
                           (wout0R, "wout0", "qsync", "sync")):
        tb.buf.before_write(P, eng)
        t = P.emit(eng, lambda h, d=tb.t, s=C.SC[nm]: h.dma_start(out=d[:], in_=s),
                   dmaq=q)
        tb.buf.wrote(t)
    bout0R.buf.before_write(P, "sync")
    t = P.emit("sync", lambda h, d=bout0R.t, s=C.D["bout0"]: h.dma_start(out=d[:], in_=s),
               dmaq="bout0")
    bout0R.buf.wrote(t)

    negm = TB(sb("negm0", (128, 8), F32))
    sqq = [TB(sb(f"sqq3{i}", (128, 512), BF16)) for i in range(2)]
    psE = TB(ps("psE3", (128, 512)))
    qm = [TB(sb(f"qm3{i}", (1, 4), F32)) for i in range(2)]
    mhb = [TB(sb(f"mhb3{i}", (1, 1), BF16)) for i in range(2)]
    ST = [TB(ps(f"pST{i}", (128, 1024))) for i in range(2)]
    Praw = [TB(sb(f"Praw{i}", (128, 1024), BF16)) for i in range(5)]
    AV0 = TB(ps("pAV0", (65, 512)))
    AV1 = TB(ps("pAV1", (65, 512)))
    PO = TB(ps("pPO", (128, 512)))
    recf = TB(sb("recf3", (1, 512), F32))
    recb = TB(sb("recb3", (1, 512), BF16))
    recf1 = TB(sb("recf3b", (1, 512), F32))
    recb1 = TB(sb("recb3b", (1, 512), BF16))
    bcS = TB(sb("bcS3", (64, 512), BF16))
    bcS1 = TB(sb("bcS3b", (64, 512), BF16))
    attnout = [TB(sb(f"ao{p}", (128, 512), BF16)) for p in range(8)]

    for gi in range(g.NG):
        gsl = slice(512 * gi, 512 * (gi + 1))
        for p in range(8):
            kidx = 0 if p < 4 else 1
            pb = p % 2
            sq, qmc, mhc = sqq[pb], qm[pb], mhb[pb]
            PS = psE if pb == 0 else ST[0]
            C.QpB[p].before_read(P, "scalar")
            sq.buf.before_write(P, "scalar")
            t = P.emit("scalar", lambda h, a=C.Qp[p], d=sq.t, gs=gsl: h.activation(
                d[:], a[:, gs], AF.Square), mark=True)
            sq.buf.wrote(t); C.QpB[p].read(t)
            mm(C, PS.t[0:1, 0:512], C.ones[:, 0:1], sq.t[:], start=True, stop=True,
               rbufs=[C.onesB, sq.buf], wbuf=PS.buf, mark=True)
            PS.buf.wrote(("tensor", P.cnt["tensor"], P.opidx["tensor"]))
            PS.buf.before_read(P, "vector")
            qmc.buf.before_write(P, "vector")
            t = P.emit("vector", lambda h, s=PS.t, d=qmc.t: h.reduce_max(
                d[0:1, 0:1], s[0:1, 0:512], axis=AXL.X), mark=True)
            qmc.buf.wrote(t); PS.buf.read(t)
            qmc.buf.before_read(P, "scalar")
            C.kmaxsqB.before_read(P, "scalar")
            t = P.emit("scalar", lambda h, a=qmc.t, d=qmc.t, k=C.kmaxsq, ki=kidx:
                       h.activation(d[0:1, 1:2], a[0:1, 0:1], AF.Sqrt,
                                    scale=k[0:1, ki:ki + 1]), mark=True)
            qmc.buf.wrote(t); C.kmaxsqB.read(t)
            qmc.buf.before_read(P, "vector")
            mhc.buf.before_write(P, "vector")
            t = P.emit("vector", lambda h, a=qmc.t, d=mhc.t: h.tensor_copy(
                d[0:1, 0:1], a[0:1, 1:2]), mark=True)
            mhc.buf.wrote(t); qmc.buf.read(t)
            t = mm(C, PS.t[0:128, 0:1], C.neg[0:1, 0:128], mhc.t[0:1, 0:1],
                   start=True, stop=True, rbufs=[C.negB, mhc.buf], wbuf=PS.buf,
                   mark=True)
            PS.buf.wrote(t)
            PS.buf.before_read(P, "scalar")
            negm.buf.before_write(P, "scalar")
            t = P.emit("scalar", lambda h, s=PS.t, d=negm.t, p=p: h.copy(
                d[:, p:p + 1], s[0:128, 0:1]), mark=True)
            negm.buf.wrote(t); PS.buf.read(t)

        def emit_norm(p):
            slots = ((0, AV0, recf, recb, psE, bcS), (1, AV1, recf1, recb1, PO, bcS1))
            for s_, AVs, rf, rb, PB, bS in slots:
                AVs.buf.before_read(P, "vector")
                rf.buf.before_write(P, "vector")
                t = P.emit("vector", lambda h, a=AVs.t, d=rf.t: h.tensor_scalar_max(
                    d[0:1, :], a[64:65, :], 1e-12), mark=True)
                rf.buf.wrote(t); AVs.buf.read(t)
            for s_, AVs, rf, rb, PB, bS in slots:
                rf.buf.before_read(P, "scalar")
                rf.buf.before_write(P, "scalar")
                t = P.emit("scalar", lambda h, d=rf.t: h.activation(
                    d[0:1, :], d[0:1, :], AF.Ln), mark=True)
                rf.buf.wrote(t)
            for s_, AVs, rf, rb, PB, bS in slots:
                rf.buf.before_read(P, "scalar")
                rb.buf.before_write(P, "scalar")
                t = P.emit("scalar", lambda h, a=rf.t, d=rb.t: h.activation(
                    d[0:1, :], a[0:1, :], AF.Exp, scale=-1.0), mark=True)
                rb.buf.wrote(t); rf.buf.read(t)
            for s_, AVs, rf, rb, PB, bS in slots:
                t = mm(C, PB.t[:, 0:512], C.ones[0:1, 0:128], rb.t[0:1, :],
                       start=True, stop=True, rbufs=[C.onesB, rb.buf],
                       wbuf=PB.buf, mark=True)
                PB.buf.wrote(t)
            for s_, AVs, rf, rb, PB, bS in slots:
                PB.buf.before_read(P, "scalar")
                bS.buf.before_write(P, "scalar")
                t = P.emit("scalar", lambda h, s=PB.t, d=bS.t: h.copy(
                    d[0:64, :], s[0:64, 0:512]), mark=True)
                bS.buf.wrote(t); PB.buf.read(t)
            for s_, AVs, rf, rb, PB, bS in slots:
                bS.buf.before_read(P, "vector")
                AVs.buf.before_read(P, "vector")
                attnout[p].buf.before_write(P, "vector")
                r0 = 64 * s_
                t = P.emit("vector", lambda h, a=AVs.t, b=bS.t, d=attnout[p].t,
                           r0=r0: h.tensor_mul(d[r0:r0 + 64, :], a[0:64, :],
                                               b[0:64, :]), mark=True)
                attnout[p].buf.wrote(t)
                AVs.buf.read(t); bS.buf.read(t)

        for p in range(8):
            KT, KTB = (C.KA, C.KAB) if p < 4 else (C.KB, C.KBB)
            mA = 0 if p < 4 else 2
            mB = 1 if p < 4 else 3
            Kg = g.K_g[gi]
            pend = []

            def emit_av(j, Pout, PoutB, mA=mA, mB=mB, Kg=Kg):
                st = (j == 0); sp = (j == Kg - 1)
                PoutB.before_read(P, "tensor")
                mm(C, AV0.t[:], C.Vtm[:, 260 * j + 65 * mA:260 * j + 65 * mA + 65],
                   Pout[:, 0:512], start=st, stop=sp, rbufs=[C.VtmB],
                   wbuf=AV0.buf if st else None)
                t = mm(C, AV1.t[:], C.Vtm[:, 260 * j + 65 * mB:260 * j + 65 * mB + 65],
                       Pout[:, 512:1024], start=st, stop=sp, rbufs=[],
                       wbuf=AV1.buf if st else None, mark=sp)
                PoutB.read(t)

            for j in range(Kg):
                STc, Pr = ST[j % 2], Praw[j % 5]
                jsl = slice(128 * j, 128 * (j + 1))
                STc.buf.before_write(P, "tensor")
                mm(C, STc.t[:, 0:512], KT[0:64, jsl], C.Qp[p][0:64, gsl],
                   start=True, stop=True, rbufs=[KTB, C.QpB[p]], wbuf=None,
                   tile_position=(0, 0))
                t = mm(C, STc.t[:, 512:1024], KT[64:128, jsl], C.Qp[p][64:128, gsl],
                       start=True, stop=True, rbufs=[], wbuf=None,
                       tile_position=(64, 0), mark=True)
                STc.buf.wrote(t)
                STc.buf.before_read(P, "scalar")
                negm.buf.before_read(P, "scalar")
                Pr.buf.before_write(P, "scalar")
                t = P.emit("scalar", lambda h, s=STc.t, d=Pr.t, b=negm.t, p=p:
                           h.activation(d[:], s[:], AF.Exp, bias=b[:, p:p + 1]),
                           mark=True)
                Pr.buf.wrote(t); STc.buf.read(t); negm.buf.read(t)
                if len(pend) >= 4:
                    pend.pop(0)()
                skip = (128 * j + 127) <= (-g.HALO + 512 * gi)
                if not skip:
                    z0 = g.ZB + 512 * gi - 128 * j - g.ZOFF
                    Pr.buf.before_read(P, "vector")
                    m01R.buf.before_read(P, "vector")
                    Pr.buf.before_write(P, "vector")
                    for s_ in range(2):
                        t = P.emit("vector", lambda h, a=Pr.t, m=m01R.t,
                                   z=z0, s_=s_: h.scalar_tensor_tensor(
                                       a[:, 512 * s_:512 * (s_ + 1)],
                                       a[:, 512 * s_:512 * (s_ + 1)],
                                       1.0, m[:, z:z + 512],
                                       op0=ALU.mult, op1=ALU.mult), mark=(s_ == 1))
                    Pr.buf.wrote(t); m01R.buf.read(t)
                pend.append(lambda j=j, Pout=Pr.t, PoutB=Pr.buf: emit_av(j, Pout, PoutB))
            while pend:
                pend.pop(0)()
            tav = ("tensor", P.cnt["tensor"], P.opidx["tensor"])
            AV0.buf.wrote(tav); AV1.buf.wrote(tav)
            emit_norm(p)
        for o in range(8):
            for p in range(8):
                mm(C, PO.t[:], wout0R.t[:, 1024 * p + 128 * o:1024 * p + 128 * (o + 1)],
                   attnout[p].t[:], start=(p == 0), stop=(p == 7),
                   rbufs=[wout0R.buf, attnout[p].buf],
                   wbuf=PO.buf if p == 0 else None, mark=(p == 7))
            tpo = ("tensor", P.cnt["tensor"], P.opidx["tensor"])
            PO.buf.wrote(tpo)
            for p in range(8):
                attnout[p].buf.read(tpo)
            PO.buf.before_read(P, "vector")
            bout0R.buf.before_read(P, "vector")
            C.xB[o].before_write(P, "vector")
            t = P.emit("vector", lambda h, s=PO.t, b=bout0R.t, d=C.x[o], o=o, gs=gsl:
                       h.scalar_tensor_tensor(d[:, gs], s[:], b[:, o:o + 1], d[:, gs],
                                              op0=ALU.add, op1=ALU.add), mark=True)
            C.xB[o].wrote(t)
            PO.buf.read(t); bout0R.buf.read(t)
    for c in range(8):
        C.dump(f"x1_{c}", C.x[c][:], [C.xB[c].wtok])


class _TBV:
    def __init__(self, t, buf):
        self.t = t
        self.buf = buf


def ln_rows_full(C, tokens_lo, tokens_hi, rowsbf, tag, pstat):
    g, P = C.g, C.P
    sb, ps = C.sb, C.ps
    xbf = TB(sb(f"lnx_{tag}", (128, 8 * 512), BF16))
    sqbf = xbf
    rows = mk_rows(C)
    nmbF, rbF = rowsbf
    for lo in range(tokens_lo, tokens_hi, 512):
        gsl = slice(lo, lo + 512)
        xf_chunks = [(C.x[c][:, gsl], C.xB[c]) for c in range(8)]
        ln_group(C, xf_chunks, xbf, sqbf, pstat, rows, 512, tag)
        pairs = [(rows["nmb"], nmbF.t, nmbF.buf), (rows["rb"], rbF.t, rbF.buf)]
        for src, dstT, dstB in pairs:
            src.buf.before_read(P, "vector")
            dstB.before_write(P, "vector")
            t = P.emit("vector", lambda h, a=src.t, d=dstT, gs=gsl: h.tensor_copy(
                d[0:1, gs], a[0:1, :]), mark=True)
            dstB.wrote(t); src.buf.read(t)




def router_tokmajor(C, tok0, W, rwR, rbR, nmF, rbrows, SrowF, oneswF, rcolT,
                    Rp, ctx_bufs, want_experts, want_gate,
                    wT, maskrow, gaterow, col0, tag):
    g, P, sb = C.g, C.P, C.sb
    for half in range(W // 128):
        t0 = tok0 + 128 * half
        l0_ = 128 * half
        for c in range(8):
            mm(C, Rp.t[:, 0:8], C.x[c][:, t0:t0 + 128],
               rwR.t[:, 8 * c:8 * (c + 1)], start=(c == 0), stop=False,
               rbufs=[C.xB[c], rwR.buf], wbuf=Rp.buf if c == 0 else None)
        mm(C, Rp.t[:, 0:8], nmF.t[0:1, l0_:l0_ + 128], SrowF.t[0:1, 0:8],
           start=False, stop=(rbR is None), rbufs=[nmF.buf, SrowF.buf], wbuf=None,
           mark=(rbR is None))
        if rbR is not None:
            t = mm(C, Rp.t[:, 0:8], oneswF.t[0:1, 0:128], rbR.t[0:1, 0:8],
                   start=False, stop=True, rbufs=[oneswF.buf, rbR.buf], wbuf=None,
                   mark=True)
        Rp.buf.wrote(("tensor", P.cnt["tensor"], P.opidx["tensor"]))
        rbrows.buf.before_read(P, "tensor")
        C.identB.before_read(P, "tensor")
        rcolT[0].buf.before_write(P, "tensor")
        t = P.emit("tensor", lambda h, a=rbrows.t, d=rcolT[0].t, i=C.identf, t0=l0_:
                   h.transpose(d[0:128, 8:9], a[0:1, t0:t0 + 128], i[0:1, 0:1]), mark=True)
        rcolT[0].buf.wrote(t); rbrows.buf.read(t)
        rcolT[0].buf.before_read(P, "scalar")
        rcolT[1].buf.before_write(P, "scalar")
        t = P.emit("scalar", lambda h, s=rcolT[0].t, d=rcolT[1].t: h.copy(
            d[:, 0:1], s[0:128, 8:9]), mark=True)
        rcolT[1].buf.wrote(t); rcolT[0].buf.read(t)
        m1, ge1, Rm, m2, negm1, E, ge2, Eg, den, rden, wbf = ctx_bufs
        Rp.buf.before_read(P, "vector")
        m1.buf.before_write(P, "vector")
        t = P.emit("vector", lambda h, s=Rp.t, d=m1.t: h.reduce_max(
            d[:, 0:1], s[:, 0:8], axis=AXL.X), mark=True)
        m1.buf.wrote(t); Rp.buf.read(t)
        m1.buf.before_read(P, "vector"); ge1.buf.before_write(P, "vector")
        t = P.emit("vector", lambda h, s=Rp.t, m=m1.t, d=ge1.t: h.tensor_scalar(
            d[:, 0:8], s[:, 0:8], m[:, 0:1], None, op0=ALU.is_ge), mark=True)
        ge1.buf.wrote(t); m1.buf.read(t); Rp.buf.read(t)
        ge1.buf.before_read(P, "vector"); Rm.buf.before_write(P, "vector")
        t = P.emit("vector", lambda h, a=ge1.t, s=Rp.t, d=Rm.t:
                   h.scalar_tensor_tensor(d[:, 0:8], a[:, 0:8], -1e9, s[:, 0:8],
                                          op0=ALU.mult, op1=ALU.add), mark=True)
        Rm.buf.wrote(t); ge1.buf.read(t); Rp.buf.read(t)
        Rm.buf.before_read(P, "vector"); m2.buf.before_write(P, "vector")
        t = P.emit("vector", lambda h, s=Rm.t, d=m2.t: h.reduce_max(
            d[:, 0:1], s[:, 0:8], axis=AXL.X), mark=True)
        m2.buf.wrote(t); Rm.buf.read(t)
        m1.buf.before_read(P, "vector"); rcolT[1].buf.before_read(P, "vector")
        negm1.buf.before_write(P, "vector")
        t = P.emit("vector", lambda h, s=m1.t, r=rcolT[1].t, d=negm1.t:
                   h.scalar_tensor_tensor(d[:, 0:1], s[:, 0:1], -1.0, r[:, 0:1],
                                          op0=ALU.mult, op1=ALU.mult), mark=True)
        negm1.buf.wrote(t); m1.buf.read(t); rcolT[1].buf.read(t)
        negm1.buf.before_read(P, "scalar")
        rcolT[1].buf.before_read(P, "scalar")
        Rp.buf.before_read(P, "scalar")
        E.buf.before_write(P, "scalar")
        t = P.emit("scalar", lambda h, s=Rp.t, b=negm1.t, r=rcolT[1].t, d=E.t:
                   h.activation(d[:, 0:8], s[:, 0:8], AF.Exp, bias=b[:, 0:1],
                                scale=r[:, 0:1]), mark=True)
        E.buf.wrote(t); negm1.buf.read(t); Rp.buf.read(t); rcolT[1].buf.read(t)
        m2.buf.before_read(P, "vector"); ge2.buf.before_write(P, "vector")
        t = P.emit("vector", lambda h, s=Rp.t, m=m2.t, d=ge2.t: h.tensor_scalar(
            d[:, 0:8], s[:, 0:8], m[:, 0:1], None, op0=ALU.is_ge), mark=True)
        ge2.buf.wrote(t); m2.buf.read(t); Rp.buf.read(t)
        E.buf.before_read(P, "vector"); ge2.buf.before_read(P, "vector")
        Eg.buf.before_write(P, "vector")
        t = P.emit("vector", lambda h, a=E.t, b=ge2.t, d=Eg.t: h.tensor_mul(
            d[:, 0:8], a[:, 0:8], b[:, 0:8]), mark=True)
        Eg.buf.wrote(t); E.buf.read(t); ge2.buf.read(t)
        Eg.buf.before_read(P, "vector"); den.buf.before_write(P, "vector")
        t = P.emit("vector", lambda h, s=Eg.t, d=den.t: h.reduce_sum(
            d[:, 0:1], s[:, 0:8], axis=AXL.X), mark=True)
        den.buf.wrote(t); Eg.buf.read(t)
        den.buf.before_read(P, "vector"); rden.buf.before_write(P, "vector")
        t = P.emit("vector", lambda h, s=den.t, d=rden.t: h.reciprocal(
            d[:, 0:1], s[:, 0:1]), mark=True)
        rden.buf.wrote(t); den.buf.read(t)
        wbf.buf.before_write(P, "vector")
        rden.buf.before_read(P, "vector")
        if want_experts:
            t = P.emit("vector", lambda h, a=Eg.t, r=rden.t, d=wbf.t: h.tensor_scalar(
                d[:, 0:8], a[:, 0:8], r[:, 0:1], None, op0=ALU.mult), mark=True)
            Eg.buf.read(t)
        if want_gate:
            den.buf.before_read(P, "vector")
            t = P.emit("vector", lambda h, a=den.t, r=rden.t, d=wbf.t: h.tensor_mul(
                d[:, 32:33], a[:, 0:1], r[:, 0:1]), mark=True)
            den.buf.read(t)
        wbf.buf.wrote(t); rden.buf.read(t)
        wbf.buf.before_read(P, "tensor")
        C.identB.before_read(P, "tensor")
        wT.buf.before_write(P, "tensor")
        t = P.emit("tensor", lambda h, a=wbf.t, d=wT.t, i=C.ident: h.transpose(
            d[0:33, 0:128], a[:, 0:33], i[:]), mark=True)
        wbf.buf.read(t)
        wT.buf.wrote(t)
        C.identB.read(t)
        wT.buf.before_read(P, "scalar")
        if want_experts:
            maskrow.buf.before_write(P, "scalar")
            t = P.emit("scalar", lambda h, s=wT.t, d=maskrow.t, c0=col0, hf=half:
                       h.copy(d[0:8, c0 + 128 * hf:c0 + 128 * hf + 128],
                              s[0:8, :]), mark=True)
            maskrow.buf.wrote(t); wT.buf.read(t)
        if want_gate:
            gaterow.buf.before_write(P, "scalar")
            t = P.emit("scalar", lambda h, s=wT.t, d=gaterow.t, c0=col0, hf=half:
                       h.copy(d[0:1, c0 + 128 * hf:c0 + 128 * hf + 128],
                              s[32:33, 0:128]), mark=True)
            gaterow.buf.wrote(t); wT.buf.read(t)


def dense_ffn(C, tag, tok_lo, tok_hi, w1nm, b1nm, w2nm, gelu_af, rwnm, rbnm,
              b2nm, b2fold_rows, want_experts, want_gate, nres, final_store=False):
    g, P, sb, ps = C.g, C.P, C.sb, C.ps
    W = 256
    lnxbf = TB(sb(f"lnx_{tag}", (128, 8 * W), BF16))
    lnsq = lnxbf
    lnrows = mk_rows(C, W)
    w2R = TB(sb(f"w2R_{tag}", (128, g.FFCH * 1024), BF16))
    ld = lambda tb, nm, q="qsync", e="sync": (
        tb.buf.before_write(P, e),
        tb.buf.wrote(P.emit(e, lambda h, d=tb.t, s=C.SC[nm]: h.dma_start(
            out=d[:], in_=s), dmaq=q)))
    ld(w2R, w2nm)
    w1R = TB(sb(f"w1R_{tag}", (128, nres * 1024), BF16))
    w1R.buf.before_write(P, "sync")
    w1R.buf.wrote(P.emit("sync", lambda h, d=w1R.t, s=C.SC[w1nm]: h.dma_start(
        out=d[:], in_=s[:, 0:nres * 1024]), dmaq="w1R" + tag))
    w1S = [TB(sb(f"w1S_{tag}{i}", (128, 1024), BF16)) for i in range(4)]
    b1R = TB(sb(f"b1R_{tag}", (128, g.FFCH), F32))
    b1R.buf.before_write(P, "sync")
    b1R.buf.wrote(P.emit("sync", lambda h, d=b1R.t, s=C.D[b1nm]: h.dma_start(
        out=d[:], in_=s), dmaq="b1R" + tag))
    rwR = TB(sb(f"rwR_{tag}", (128, 64), F32))
    rwR.buf.before_write(P, "scalar")
    rwR.buf.wrote(P.emit("scalar", lambda h, d=rwR.t, s=C.D[rwnm]: h.dma_start(
        out=d[:], in_=s), dmaq="rwF" + tag))
    rbR = None
    if rbnm is not None:
        rbR = TB(sb(f"rbR_{tag}", (1, 8), F32))
        rbR.buf.before_write(P, "scalar")
        rbR.buf.wrote(P.emit("scalar", lambda h, d=rbR.t, s=C.D[rbnm]: h.dma_start(
            out=d[:], in_=s), dmaq="rbF" + tag))
    oneswF = TB(sb(f"oneswF_{tag}", (1, 128), F32))
    oneswF.buf.before_write(P, "vector")
    _t = P.emit("vector", lambda h, d=oneswF.t: h.memset(d[:], 1.0), mark=True)
    oneswF.buf.wrote(_t)
    SrowF = TB(sb(f"SrowF_{tag}", (1, 8), F32))
    Rp = TB(ps(f"ffR_{tag}", (128, 512)))
    for c in range(8):
        mm(C, Rp.t[0:1, 0:8], C.cf32[:, 1:2], rwR.t[:, 8 * c:8 * (c + 1)],
           start=(c == 0), stop=(c == 7), rbufs=[C.cf32B, rwR.buf],
           wbuf=Rp.buf if c == 0 else None, mark=(c == 7))
    Rp.buf.wrote(("tensor", P.cnt["tensor"], P.opidx["tensor"]))
    Rp.buf.before_read(P, "scalar")
    SrowF.buf.before_write(P, "scalar")
    _t = P.emit("scalar", lambda h, s=Rp.t, d=SrowF.t: h.copy(d[0:1, 0:8], s[0:1, 0:8]),
                mark=True)
    SrowF.buf.wrote(_t); Rp.buf.read(_t)
    b2R = None
    if b2nm is not None:
        b2R = TB(sb(f"b2R_{tag}", (128, 8), F32))
        b2R.buf.before_write(P, "sync")
        b2R.buf.wrote(P.emit("sync", lambda h, d=b2R.t, s=C.D[b2nm]: h.dma_start(
            out=d[:], in_=s), dmaq="b2R" + tag))
    b2rowsR = None
    if b2fold_rows is not None:
        b2rowsR = TB(sb(f"b2rows_{tag}", (8, 1024), BF16))
        ld(b2rowsR, b2fold_rows, "qscalar", "scalar")
    e8R = None
    if want_experts:
        e8R = TB(sb(f"e8R_{tag}", (8, 1024), BF16)); ld(e8R, "e8sel", "qscalar", "scalar")
    xn = TB(sb(f"ffxn_{tag}", (128, 8 * W), BF16))
    tmp = TB(sb(f"fftmp_{tag}", (128, W), F32))
    wTsh = TB(ps(f"ffwT_{tag}", (128, 128), BF16))
    wT = wTsh
    rt = lambda nm, w, dt=F32: TB(sb(f"ffr_{nm}_{tag}", (128, w), dt))
    ctx_bufs = (rt("m1", 1), rt("ge1", 8), rt("Rm", 8), rt("m2", 1),
                rt("negm1", 1), rt("E", 8), rt("ge2", 8), rt("Eg", 8),
                rt("den", 1), rt("rden", 1), rt("wbf", 33, BF16))
    rcolT = (Rp, TB(sb(f"ffrcol_{tag}", (128, 1), F32)))
    _wbf = ctx_bufs[10]
    _wbf.buf.before_write(P, "vector")
    _t = P.emit("vector", lambda h, d=_wbf.t: h.memset(d[:], 0.0), mark=True)
    _wbf.buf.wrote(_t)
    maskrow = TB(sb(f"ffmr_{tag}", (8, W), BF16)) if want_experts else None
    gaterow = TB(sb(f"ffgr_{tag}", (1, W), BF16)) if want_gate else None
    h1t = ps(f"ffh1_{tag}", (128, 512))
    _h1tb = TB(h1t)
    h1p = [_h1tb, _h1tb]
    h1sb = [TB(sb(f"ffh1s_{tag}{i}", (128, W), BF16)) for i in range(2)]
    h1sc = [TB(sb(f"ffh1c_{tag}{i}", (128, W), BF16)) for i in range(2)] \
        if want_experts else None
    mbp = TB(ps(f"ffmbp_{tag}", (128, 512)))
    PO = [TB(ps(f"ffPO_{tag}{i}", (128, 512))) for i in range(4)]
    gtmp = TB(sb(f"ffgt_{tag}", (128, W), F32)) if want_gate else None
    gateS = TB(sb(f"ffgs_{tag}", (128, W), F32)) if want_gate else None

    ntg = (tok_hi - tok_lo) // W
    lnxbf2 = [lnxbf, TB(sb(f"lnx2_{tag}", (128, 8 * W), BF16))]
    _r1 = dict(lnrows)
    _r1.update({k: TB(sb(f"ffrow_{k}_{tag}", (1, W),
                         BF16 if k in ("nmb", "rb") else F32))
                for k in ("nm", "r", "nmb", "rb")})
    lnrows2 = [lnrows, _r1]
    maskrow2 = [maskrow, TB(sb(f"ffmr2_{tag}", (8, W), BF16))] \
        if want_experts else [None, None]
    gaterow2 = [gaterow, TB(sb(f"ffgr2_{tag}", (1, W), BF16))] \
        if want_gate else [None, None]
    pstatR = (_TBV(Rp.t, Rp.buf), _TBV(Rp.t[:, 256:512], Rp.buf))

    def prologue(it):
        if it >= ntg:
            return
        plo = tok_lo + it * W
        xf = [(C.x[c][:, plo:plo + W], C.xB[c]) for c in range(8)]
        lx = lnxbf2[it % 2]
        rws = lnrows2[it % 2]
        ln_group(C, xf, lx, lx, pstatR, rws, W, tag)
        if want_experts or want_gate:
            router_tokmajor(C, plo, W, rwR, rbR, rws["nm"], rws["r"],
                            SrowF, oneswF, rcolT,
                            Rp, list(ctx_bufs), want_experts, want_gate, wT,
                            maskrow2[it % 2], gaterow2[it % 2], 0, tag)

    prologue(0)
    for it in range(ntg):
        lo = tok_lo + it * W
        tsl = slice(lo, lo + W)
        rws = lnrows2[it % 2]
        mrow = maskrow2[it % 2]
        grow = gaterow2[it % 2]
        xf_chunks = [(C.x[c][:, tsl], C.xB[c]) for c in range(8)]
        bcast_mm(C, h1p[0].t[:, 0:W], h1p[0].buf, rws["nmb"].t[0:1, 0:W],
                 rws["nmb"].buf)
        bcast_mm(C, mbp.t[:, 0:W], mbp.buf, rws["rb"].t[0:1, 0:W],
                 rws["rb"].buf)
        xn_tts(C, xf_chunks, (h1p[0], slice(0, W)), (mbp, slice(0, W)), tmp, xn, W)
        if want_gate:
            bcast_mm(C, mbp.t[:, 0:W], mbp.buf, grow.t[0:1, 0:W], grow.buf)
            mbp.buf.before_read(P, "scalar")
            gateS.buf.before_write(P, "scalar")
            _t = P.emit("scalar", lambda h, s=mbp.t, d=gateS.t, W=W: h.copy(
                d[:, 0:W], s[:, 0:W]), mark=True)
            gateS.buf.wrote(_t); mbp.buf.read(_t)
        def emit_oproj(f, h1v, h1vb):
            for o in range(8):
                bankstop = (b2fold_rows is None and f == g.FFCH - 1 and o % 2 == 1)
                lastmm = (b2fold_rows is None and f == g.FFCH - 1 and o == 7)
                mm(C, PO[o // 2].t[:, 256 * (o % 2):256 * (o % 2) + W],
                   w2R.t[:, 1024 * f + 128 * o:1024 * f + 128 * (o + 1)], h1v,
                   start=(f == 0 and o % 2 == 0), stop=bankstop,
                   rbufs=[w2R.buf, h1vb], wbuf=PO[o // 2].buf if (f == 0 and o % 2 == 0) else None,
                   mark=lastmm)

        def w1load(f):
            if f < nres or f >= g.FFCH:
                return
            wS = w1S[f % 4]
            wS.buf.before_write(P, "sync")
            wS.buf.wrote(P.emit("sync", lambda h, d=wS.t, s=C.SC[w1nm], f=f:
                                h.dma_start(out=d[:],
                                            in_=s[:, 1024 * f:1024 * (f + 1)]),
                                dmaq="w1S" + tag + str(f % 4)))

        for _pf in range(nres, nres + 3):
            w1load(_pf)
        pend = None
        for f in range(g.FFCH):
            fb = f % 2
            h1c, fsl = h1p[fb], slice(0, W)
            if f < nres:
                w1ap, w1b = w1R.t[:, 1024 * f:1024 * (f + 1)], w1R.buf
            else:
                w1ap, w1b = w1S[f % 4].t, w1S[f % 4].buf
            w1load(f + 3)
            if f == 12:
                prologue(it + 1)
            for c in range(8):
                mm(C, h1c.t[:, fsl], w1ap[:, 128 * c:128 * (c + 1)],
                   xn.t[:, W * c:W * (c + 1)], start=(c == 0), stop=(c == 7),
                   rbufs=[w1b, xn.buf], wbuf=h1c.buf if c == 0 else None,
                   mark=(c == 7))
            h1c.buf.wrote(("tensor", P.cnt["tensor"], P.opidx["tensor"]))
            h1c.buf.before_read(P, "scalar")
            b1R.buf.before_read(P, "scalar")
            h1sb[fb].buf.before_write(P, "scalar")
            t = P.emit("scalar", lambda h, s=h1c.t, b=b1R.t, d=h1sb[fb].t, f=f,
                       fsl=fsl: h.activation(d[:], s[:, fsl], gelu_af,
                                             bias=b[:, f:f + 1]), mark=True)
            h1sb[fb].buf.wrote(t); h1c.buf.read(t); b1R.buf.read(t)
            h1v, h1vb = h1sb[fb].t[:, 0:W], h1sb[fb].buf
            if pend is not None:
                pend()
            if want_experts:
                n = f // (g.FFCH // 8)
                if f % (g.FFCH // 8) == 0:
                    t = mm(C, mbp.t[:, 0:W], e8R.t[0:8, 128 * n:128 * (n + 1)],
                           mrow.t[0:8, 0:W], start=True, stop=True,
                           rbufs=[e8R.buf, mrow.buf], wbuf=mbp.buf, mark=True)
                    mbp.buf.wrote(t)
                h1sb[fb].buf.before_read(P, "vector")
                mbp.buf.before_read(P, "vector")
                h1sc[fb].buf.before_write(P, "vector")
                t = P.emit("vector", lambda h, a=h1sb[fb].t, b=mbp.t, d=h1sc[fb].t,
                           W=W: h.tensor_mul(d[:], a[:], b[:, 0:W]), mark=True)
                h1sc[fb].buf.wrote(t); h1sb[fb].buf.read(t); mbp.buf.read(t)
                h1v, h1vb = h1sc[fb].t[:, 0:W], h1sc[fb].buf
            pend = (lambda f=f, h1v=h1v, h1vb=h1vb: emit_oproj(f, h1v, h1vb))
        pend()
        tpo = None
        for o in range(8):
            if b2rowsR is not None:
                tpo = mm(C, PO[o // 2].t[:, 256 * (o % 2):256 * (o % 2) + W],
                         b2rowsR.t[0:8, 128 * o:128 * (o + 1)], mrow.t[0:8, 0:W],
                         start=False, stop=(o % 2 == 1), rbufs=[b2rowsR.buf, mrow.buf],
                         wbuf=None, mark=True)
            else:
                pass
        if b2rowsR is None:
            tpo = ("tensor", P.cnt["tensor"], P.opidx["tensor"])
        for o in range(4):
            PO[o].buf.wrote(tpo if isinstance(tpo, tuple) else
                            ("tensor", P.cnt["tensor"], P.opidx["tensor"]))
        for o in range(8):
            POt, sl = PO[o // 2], slice(256 * (o % 2), 256 * (o % 2) + W)
            POt.buf.before_read(P, "vector")
            C.xB[o].before_write(P, "vector")
            if want_gate:
                b2R.buf.before_read(P, "vector")
                gateS.buf.before_read(P, "vector")
                gtmp.buf.before_write(P, "vector")
                t = P.emit("vector", lambda h, s=POt.t, b=b2R.t, m=gateS.t, d=gtmp.t,
                           o=o, sl=sl: h.scalar_tensor_tensor(
                               d[:], s[:, sl], b[:, o:o + 1], m[:, 0:W],
                               op0=ALU.add, op1=ALU.mult), mark=True)
                gtmp.buf.wrote(t); POt.buf.read(t); b2R.buf.read(t); gateS.buf.read(t)
                gtmp.buf.before_read(P, "vector")
                t = P.emit("vector", lambda h, a=gtmp.t, d=C.x[o], ts=tsl:
                           h.tensor_add(d[:, ts], d[:, ts], a[:]), mark=True)
                C.xB[o].wrote(t); gtmp.buf.read(t)
            else:
                t = P.emit("vector", lambda h, s=POt.t, d=C.x[o], ts=tsl, sl=sl:
                           h.tensor_add(d[:, ts], d[:, ts], s[:, sl]), mark=True)
                C.xB[o].wrote(t); POt.buf.read(t)
            if final_store:
                C.xB[o].before_read(P, "sync")
                t = P.emit("sync", lambda h, s=C.x[o], d=C.y, o=o, lo=lo:
                           h.dma_start(
                               out=d[:, g.OWN * o + lo - g.HALO:
                                     g.OWN * o + lo - g.HALO + W],
                               in_=s[:, lo:lo + W]), dmaq="yout" + str(o % 2))
                C.xB[o].read(t)
                C.final_toks.append(t)


def p4_l0_ffn(C):
    g = C.g
    dense_ffn(C, "f0", 0, g.EXT, "w10", "b10", "w20", AF.Gelu, "rw0", "rb0",
              None, "b20r", True, False, nres=12)
    for c in range(8):
        C.dump(f"x2_{c}", C.x[c][:], [C.xB[c].wtok])


def p5_l1_qkv(C):
    g, P, sb, ps = C.g, C.P, C.sb, C.ps
    wR = TB(sb("wqkv1R", (128, 8 * 3088), BF16))
    wR.buf.before_write(P, "sync")
    wR.buf.wrote(P.emit("sync", lambda h, d=wR.t, s=C.SC["wqkv1"]: h.dma_start(
        out=d[:], in_=s), dmaq="wqkv1R"))
    bR = TB(sb("bqkv1R", (1, 3088), BF16))
    bR.buf.before_write(P, "scalar")
    bR.buf.wrote(P.emit("scalar", lambda h, d=bR.t, s=C.SC["bqkv1"]: h.dma_start(
        out=d[:], in_=s), dmaq="bqkv1R"))
    vrowf = TB(sb("vrowf5", (1, g.EXT), F32))
    vrowb = TB(sb("vrowb5", (1, g.EXT), BF16))
    vrowf.buf.before_write(P, "sync")
    vrowf.buf.wrote(P.emit("sync", lambda h, d=vrowf.t, s=C.D["validrow"]:
                           h.dma_start(out=d[:], in_=s), dmaq="vrow"))
    vrowf.buf.before_read(P, "vector")
    vrowb.buf.before_write(P, "vector")
    _t = P.emit("vector", lambda h, a=vrowf.t, d=vrowb.t: h.tensor_copy(d[:], a[:]),
                mark=True)
    vrowb.buf.wrote(_t); vrowf.buf.read(_t)
    vcolR = TB(sb("vcolR5", (128, g.EXTT), F32))
    vcolR.buf.before_write(P, "sync")
    vcolR.buf.wrote(P.emit("sync", lambda h, d=vcolR.t, s=C.D["validcol"]:
                           h.dma_start(out=d[:], in_=s), dmaq="vcol"))
    C.nmbF5 = TB(sb("nmbF5", (1, g.EXT), BF16))
    C.rbF5 = TB(sb("rbF5", (1, g.EXT), BF16))
    bc5 = TB(ps("pbc5", (128, 1024)))
    pstat5 = (_TBV(bc5.t[:, 0:512], bc5.buf), _TBV(bc5.t[:, 512:1024], bc5.buf))
    ln_rows_full(C, 0, g.EXT, (C.nmbF5, C.rbF5), "p5", pstat5)

    xn = TB(sb("xn5", (128, 8 * 512), BF16))
    tmp = TB(sb("tmp5", (128, 512), F32))
    vb = TB(ps("pvb5", (128, 512)))
    vbS = TB(sb("vbS5", (128, 512), F32))
    Pq = TB(ps("pq5", (128, 512)))
    Pv = TB(ps("pv5", (128, 1040)))
    evb2 = [TB(sb(f"evb5{i}", (128, 512), BF16)) for i in range(2)]
    evv2 = [TB(sb(f"evv5{i}", (128, 1040), BF16)) for i in range(2)]

    NQG = g.EXT // 512
    for ig in range(NQG):
        gsl = slice(512 * ig, 512 * (ig + 1))
        bcast_mm(C, bc5.t[:, 0:512], bc5.buf, C.nmbF5.t[0:1, gsl], C.nmbF5.buf)
        bcast_mm(C, bc5.t[:, 512:1024], bc5.buf, C.rbF5.t[0:1, gsl], C.rbF5.buf)
        xf_chunks = [(C.x[c][:, gsl], C.xB[c]) for c in range(8)]
        xn_tts(C, xf_chunks, (bc5, slice(0, 512)), (bc5, slice(512, 1024)), tmp, xn, 512)
        bcast_mm(C, vb.t[:], vb.buf, vrowb.t[0:1, gsl], vrowb.buf)
        vb.buf.before_read(P, "scalar")
        vbS.buf.before_write(P, "scalar")
        _t = P.emit("scalar", lambda h, s=vb.t, d=vbS.t: h.copy(d[:], s[:]), mark=True)
        vbS.buf.wrote(_t); vb.buf.read(_t)
        for ob in range(16):
            evb = evb2[ob % 2]
            oc = 128 * ob
            for c in range(8):
                mm(C, Pq.t[:], wR.t[:, 3088 * c + oc:3088 * c + oc + 128],
                   xn.t[:, 512 * c:512 * (c + 1)], start=(c == 0), stop=False,
                   rbufs=[wR.buf, xn.buf], wbuf=Pq.buf if c == 0 else None)
            t = mm(C, Pq.t[:], bR.t[0:1, oc:oc + 128], C.onesw[0:1, 0:512],
                   start=False, stop=True, rbufs=[bR.buf, C.oneswB], wbuf=None,
                   mark=True)
            Pq.buf.wrote(t)
            is_k = ob >= 8
            Pq.buf.before_read(P, "vector" if is_k else "scalar")
            evb.buf.before_write(P, "vector" if is_k else "scalar")
            if is_k:
                vbS.buf.before_read(P, "vector")
                t = P.emit("vector", lambda h, s=Pq.t, v=vbS.t, d=evb.t:
                           h.tensor_mul(d[:], s[:], v[:]), mark=True)
                vbS.buf.read(t)
            else:
                t = P.emit("scalar", lambda h, s=Pq.t, d=evb.t, sc=C.g.SCL:
                           h.mul(d[:], s[:], sc), mark=True)
            evb.buf.wrote(t); Pq.buf.read(t)
            evb.buf.before_read(P, "sync")
            dst = C.SC["k1"] if is_k else C.SC["q1"]
            blk = ob - 8 if is_k else ob
            t = P.emit("sync", lambda h, s=evb.t, d=dst, b=blk, gs=gsl: h.dma_start(
                out=d[:, g.EXT * b + gs.start:g.EXT * b + gs.stop], in_=s[:]),
                dmaq="kqout" + str(ob % 2))
            evb.buf.read(t)
        for k4 in range(4):
            evv = evv2[k4 % 2]
            tile = 4 * ig + k4
            nparts = ((1040 + 511) // 512)
            for c in range(8):
                for vn in range(nparts):
                    n0 = 512 * vn
                    n1 = min(1040, 512 * (vn + 1))
                    mm(C, Pv.t[:, n0:n1],
                       xn.t[:, 512 * c + 128 * k4:512 * c + 128 * (k4 + 1)],
                       wR.t[:, 3088 * c + 2048 + n0:3088 * c + 2048 + n1],
                       start=(c == 0), stop=False,
                       rbufs=[wR.buf, xn.buf], wbuf=Pv.buf if (c == 0 and vn == 0) else None)
            for vn in range(nparts):
                n0 = 512 * vn
                n1 = min(1040, 512 * (vn + 1))
                t = mm(C, Pv.t[:, n0:n1], C.onesw[0:1, 0:128],
                       bR.t[0:1, 2048 + n0:2048 + n1], start=False, stop=True,
                       rbufs=[C.oneswB, bR.buf], wbuf=None, mark=(vn == nparts - 1))
            Pv.buf.wrote(t)
            Pv.buf.before_read(P, "vector")
            vcolR.buf.before_read(P, "vector")
            evv.buf.before_write(P, "vector")
            t = P.emit("vector", lambda h, s=Pv.t, v=vcolR.t, d=evv.t, ti=tile:
                       h.tensor_scalar(
                           d[:].rearrange("p (a f) -> p a f", f=65)[:, :, 0:64],
                           s[:].rearrange("p (a f) -> p a f", f=65)[:, :, 0:64],
                           v[:, ti:ti + 1], None, op0=ALU.mult), mark=True)
            evv.buf.wrote(t); Pv.buf.read(t); vcolR.buf.read(t)
            Pv.buf.before_read(P, "scalar")
            evv.buf.before_write(P, "scalar")
            t = P.emit("scalar", lambda h, s=Pv.t, d=evv.t: h.copy(
                d[:].rearrange("p (a f) -> p a f", f=65)[:, :, 64:65],
                s[:].rearrange("p (a f) -> p a f", f=65)[:, :, 64:65]), mark=True)
            evv.buf.wrote(t); Pv.buf.read(t)
            evv.buf.before_read(P, "sync")
            t = P.emit("sync", lambda h, s=evv.t, d=C.SC["v1"], ti=tile: h.dma_start(
                out=d[:, 1040 * ti:1040 * (ti + 1)], in_=s[:]), dmaq="vout" + str(k4 % 2))
            evv.buf.read(t)


def p6_l1_attn(C):
    g, P, sb, ps = C.g, C.P, C.sb, C.ps
    wout1R = TB(sb("wout1R", (128, 8 * 1024), BF16))
    wout1R.buf.before_write(P, "sync")
    wout1R.buf.wrote(P.emit("sync", lambda h, d=wout1R.t, s=C.SC["wout1"]:
                            h.dma_start(out=d[:], in_=s), dmaq="wout1R"))
    bout1R = TB(sb("bout1R", (128, 8), F32))
    bout1R.buf.before_write(P, "sync")
    bout1R.buf.wrote(P.emit("sync", lambda h, d=bout1R.t, s=C.D["bout1"]:
                            h.dma_start(out=d[:], in_=s), dmaq="bout1R"))
    bandR = TB(sb("bandR", (128, 640), BF16))
    bandR.buf.before_write(P, "scalar")
    bandR.buf.wrote(P.emit("scalar", lambda h, d=bandR.t, s=C.SC["band"]:
                           h.dma_start(out=d[:], in_=s), dmaq="bandR"))
    bandKQ = TB(sb("bandKQ", (128, 640), BF16))
    bandKQ.buf.before_write(P, "scalar")
    bandKQ.buf.wrote(P.emit("scalar", lambda h, d=bandKQ.t, s=C.SC["bandkq"]:
                            h.dma_start(out=d[:], in_=s), dmaq="bandR"))

    Kwin = [TB(sb(f"Kw{i}", (128, 8 * 1024), BF16)) for i in range(2)]
    Vwin = [TB(sb("Vw0", (128, 8 * 1040), BF16))] * 2
    Qg = [TB(sb(f"Qg{i}", (128, 8 * 512), BF16)) for i in range(2)]
    psE = TB(ps("psE6", (128, 512)))
    ST = [TB(ps(f"pST6{i}", (128, 1024))) for i in range(2)]
    scm = [TB(sb(f"scm6{i}", (128, 640), F32)) for i in range(2)]
    mcol = TB(sb("mcol6", (128, 8), F32))
    negq = [TB(sb(f"negq6{i}", (1, 512), BF16)) for i in range(2)]
    Pmf = [TB(sb(f"Pmf6{i}", (128, 1024), F32)) for i in range(2)]
    Praw = [TB(sb(f"Praw6{i}", (128, 1024), BF16)) for i in range(3)]
    AV0 = TB(ps("pAV06", (65, 512)))
    AV1 = TB(ps("pAV16", (65, 512)))
    PO = TB(ps("pPO6", (128, 512)))
    recf = TB(sb("recf6", (1, 512), F32))
    recb = TB(sb("recb6", (1, 512), BF16))
    recf1 = TB(sb("recf6b", (1, 512), F32))
    recb1 = TB(sb("recb6b", (1, 512), BF16))
    bcS = TB(sb("bcS6", (64, 512), BF16))
    bcS1 = TB(sb("bcS6b", (64, 512), BF16))
    attnout = [TB(sb(f"ao6{p}", (128, 512), BF16)) for p in range(8)]
    CB = 300.0

    NGO = g.OWNT // 4
    relorder = [3, 0, 1, 2, 4, 5, 6, 7]
    for go in range(NGO):
        t0 = 2 + 4 * go
        cur = go % 2
        KwC, VwC, QgC = Kwin[cur], Vwin[cur], Qg[cur]
        KwC.buf.before_write(P, "sync")
        last = None
        for kb in range(8):
            last = P.emit("sync", lambda h, d=KwC.t, s=C.SC["k1"], kb=kb, t0=t0:
                          h.dma_start(out=d[:, 1024 * kb:1024 * (kb + 1)],
                                      in_=s[:, g.EXT * kb + 128 * (t0 - 2):
                                            g.EXT * kb + 128 * (t0 + 6)]),
                          dmaq="Kw" + str(cur))
        KwC.buf.wrote(last)
        VwC.buf.before_write(P, "sync")
        last = P.emit("sync", lambda h, d=VwC.t, s=C.SC["v1"], t0=t0: h.dma_start(
            out=d[:], in_=s[:, 1040 * (t0 - 2):1040 * (t0 + 6)]), dmaq="Vw" + str(cur))
        VwC.buf.wrote(last)
        QgC.buf.before_write(P, "sync")
        last = None
        for p in range(8):
            last = P.emit("sync", lambda h, d=QgC.t, s=C.SC["q1"], p=p, t0=t0:
                          h.dma_start(out=d[:, 512 * p:512 * (p + 1)],
                                      in_=s[:, g.EXT * p + 128 * t0:
                                            g.EXT * p + 128 * (t0 + 4)]),
                          dmaq="Qgl" + str(cur))
        QgC.buf.wrote(last)
        for p in range(8):
            for i in range(4):
                qsl = slice(512 * p + 128 * i, 512 * p + 128 * (i + 1))
                for hh in range(2):
                    r0 = 64 * hh
                    STc = ST[hh]
                    tlast = None
                    for (n0, n1) in ((0, 512), (512, 640)):
                        tlast = mm(C, STc.t[:, n0:n1], QgC.t[r0:r0 + 64, qsl],
                                   KwC.t[r0:r0 + 64, 1024 * p + 128 * i + n0:
                                         1024 * p + 128 * i + n1],
                                   start=True, stop=True,
                                   rbufs=[QgC.buf, KwC.buf], wbuf=STc.buf,
                                   tile_position=(r0, 0), mark=(n0 == 512))
                    STc.buf.wrote(tlast)
                for hh in range(2):
                    STc = ST[hh]
                    sc = scm[hh]
                    STc.buf.before_read(P, "vector")
                    bandR.buf.before_read(P, "vector")
                    sc.buf.before_write(P, "vector")
                    for (n0, n1) in ((0, 512), (512, 640)):
                        t = P.emit("vector", lambda h, a=STc.t, m=bandR.t, d=sc.t,
                                   n0=n0, n1=n1: h.scalar_tensor_tensor(
                                       d[:, n0:n1], a[:, n0:n1], CB, m[:, n0:n1],
                                       op0=ALU.add, op1=ALU.mult), mark=(n0 == 512))
                    sc.buf.wrote(t); STc.buf.read(t); bandR.buf.read(t)
                    sc.buf.before_read(P, "vector")
                    mcol.buf.before_write(P, "vector")
                    t = P.emit("vector", lambda h, a=sc.t, d=mcol.t, mc=2 * i + hh:
                               h.reduce_max(d[:, mc:mc + 1], a[:, 0:640],
                                            axis=AXL.X), mark=True)
                    mcol.buf.wrote(t); sc.buf.read(t)
            mcol.buf.before_read(P, "tensor")
            C.identfB.before_read(P, "tensor")
            psE.buf.before_write(P, "tensor")
            PO.buf.before_write(P, "tensor")
            tt = None
            for i in range(4):
                for hh in range(2):
                    PT = psE if hh == 0 else PO
                    tt = P.emit("tensor", lambda h, s=mcol.t, d=PT.t, mc=2 * i + hh,
                                i=i, i_=C.identf: h.transpose(
                                    d[0:1, 128 * i:128 * (i + 1)], s[:, mc:mc + 1],
                                    i_[:]), mark=(2 * i + hh == 7))
            psE.buf.wrote(tt); PO.buf.wrote(tt)
            mcol.buf.read(tt); C.identfB.read(tt)
            tneg = None
            for hh in range(2):
                PT = psE if hh == 0 else PO
                PT.buf.before_read(P, "scalar")
                negq[hh].buf.before_write(P, "scalar")
                tneg = P.emit("scalar", lambda h, s=PT.t, d=negq[hh].t:
                              h.activation(d[0:1, 0:512], s[0:1, 0:512], AF.Copy,
                                           scale=-1.0, bias=float(CB)),
                              mark=True)
                negq[hh].buf.wrote(tneg)
                PT.buf.read(tneg)
            pend = []
            h1g, h2g = 2 * p, 2 * p + 1

            def emit_av(ri, ag, Pr, N, qlo, qhi):
                st = (ri == 0); sp = (ri == 7)
                Pr.buf.before_read(P, "tensor")
                mm(C, AV0.t[:, qlo:qhi],
                   VwC.t[:, 1040 * ag + 65 * h1g:1040 * ag + 65 * (h1g + 1)],
                   Pr.t[:, 0:N], start=st, stop=sp, rbufs=[VwC.buf],
                   wbuf=AV0.buf if st else None)
                t = mm(C, AV1.t[:, qlo:qhi],
                       VwC.t[:, 1040 * ag + 65 * h2g:1040 * ag + 65 * (h2g + 1)],
                       Pr.t[:, 512:512 + N], start=st, stop=sp, rbufs=[],
                       wbuf=AV1.buf if st else None, mark=sp)
                Pr.buf.read(t)

            for ri, ag in enumerate(relorder):
                qlo = 128 * max(0, ag - 4)
                qhi = 128 * min(4, ag + 1)
                N = qhi - qlo
                zb = 128 * (4 - ag) + qlo
                STc = ST[ri % 2]
                Pf = Pmf[ri % 2]
                Pr = Praw[ri % 3]
                STc.buf.before_write(P, "tensor")
                for hh, (c0, tp) in ((0, (0, (0, 0))), (1, (512, (64, 0)))):
                    r0 = 64 * hh
                    mm(C, STc.t[:, c0:c0 + N],
                       KwC.t[r0:r0 + 64, 1024 * p + 128 * ag:1024 * p + 128 * (ag + 1)],
                       QgC.t[r0:r0 + 64, 512 * p + qlo:512 * p + qhi],
                       start=True, stop=False, rbufs=[KwC.buf, QgC.buf], wbuf=None,
                       tile_position=tp)
                    tlast = mm(C, STc.t[:, c0:c0 + N], C.onesw[0:1, 0:128],
                               negq[hh].t[0:1, qlo:qhi], start=False, stop=True,
                               rbufs=[C.oneswB, negq[hh].buf], wbuf=None,
                               mark=(hh == 1))
                STc.buf.wrote(tlast)
                STc.buf.before_read(P, "vector")
                bandKQ.buf.before_read(P, "vector")
                Pf.buf.before_write(P, "vector")
                for s_ in range(2):
                    t = P.emit("vector", lambda h, a=STc.t, m=bandKQ.t, d=Pf.t,
                               z=zb, N=N, s_=s_: h.scalar_tensor_tensor(
                                   d[:, 512 * s_:512 * s_ + N],
                                   a[:, 512 * s_:512 * s_ + N],
                                   CB, m[:, z:z + N],
                                   op0=ALU.add, op1=ALU.mult), mark=(s_ == 1))
                Pf.buf.wrote(t); STc.buf.read(t); bandKQ.buf.read(t)
                Pf.buf.before_read(P, "scalar")
                Pr.buf.before_write(P, "scalar")
                t = P.emit("scalar", lambda h, s=Pf.t, d=Pr.t, N=N, e=C.cf32:
                           h.activation(d[:, 0:N], s[:, 0:N], AF.Exp,
                                        bias=e[:, 2:3]), mark=False)
                t = P.emit("scalar", lambda h, s=Pf.t, d=Pr.t, N=N, e=C.cf32:
                           h.activation(d[:, 512:512 + N], s[:, 512:512 + N],
                                        AF.Exp, bias=e[:, 2:3]), mark=True)
                Pr.buf.wrote(t); Pf.buf.read(t)
                if len(pend) >= 2:
                    pend.pop(0)()
                pend.append(lambda ri=ri, ag=ag, Pr=Pr, N=N, qlo=qlo, qhi=qhi:
                            emit_av(ri, ag, Pr, N, qlo, qhi))
            while pend:
                pend.pop(0)()
            tav = ("tensor", P.cnt["tensor"], P.opidx["tensor"])
            AV0.buf.wrote(tav); AV1.buf.wrote(tav)
            slots = ((0, AV0, recf, recb, psE, bcS), (1, AV1, recf1, recb1, PO, bcS1))
            for s_, AVs, rf, rb, PB, bS in slots:
                AVs.buf.before_read(P, "scalar")
                rf.buf.before_write(P, "scalar")
                t = P.emit("scalar", lambda h, a=AVs.t, d=rf.t: h.activation(
                    d[0:1, :], a[64:65, :], AF.Ln), mark=True)
                rf.buf.wrote(t); AVs.buf.read(t)
            for s_, AVs, rf, rb, PB, bS in slots:
                rf.buf.before_read(P, "scalar")
                rb.buf.before_write(P, "scalar")
                t = P.emit("scalar", lambda h, a=rf.t, d=rb.t: h.activation(
                    d[0:1, :], a[0:1, :], AF.Exp, scale=-1.0), mark=True)
                rb.buf.wrote(t); rf.buf.read(t)
            for s_, AVs, rf, rb, PB, bS in slots:
                t = mm(C, PB.t[:, 0:512], C.ones[0:1, 0:128], rb.t[0:1, :],
                       start=True, stop=True, rbufs=[C.onesB, rb.buf],
                       wbuf=PB.buf, mark=True)
                PB.buf.wrote(t)
            for s_, AVs, rf, rb, PB, bS in slots:
                PB.buf.before_read(P, "scalar")
                bS.buf.before_write(P, "scalar")
                t = P.emit("scalar", lambda h, s=PB.t, d=bS.t: h.copy(
                    d[0:64, :], s[0:64, 0:512]), mark=True)
                bS.buf.wrote(t); PB.buf.read(t)
            for s_, AVs, rf, rb, PB, bS in slots:
                bS.buf.before_read(P, "vector")
                AVs.buf.before_read(P, "vector")
                attnout[p].buf.before_write(P, "vector")
                r0 = 64 * s_
                t = P.emit("vector", lambda h, a=AVs.t, b=bS.t, d=attnout[p].t,
                           r0=r0: h.tensor_mul(d[r0:r0 + 64, :], a[0:64, :],
                                               b[0:64, :]), mark=True)
                attnout[p].buf.wrote(t)
                AVs.buf.read(t); bS.buf.read(t)
        osl = slice(128 * t0, 128 * (t0 + 4))
        for o in range(8):
            for p in range(8):
                mm(C, PO.t[:], wout1R.t[:, 1024 * p + 128 * o:1024 * p + 128 * (o + 1)],
                   attnout[p].t[:], start=(p == 0), stop=(p == 7),
                   rbufs=[wout1R.buf, attnout[p].buf],
                   wbuf=PO.buf if p == 0 else None, mark=(p == 7))
            tpo = ("tensor", P.cnt["tensor"], P.opidx["tensor"])
            PO.buf.wrote(tpo)
            for p in range(8):
                attnout[p].buf.read(tpo)
            PO.buf.before_read(P, "vector")
            bout1R.buf.before_read(P, "vector")
            C.xB[o].before_write(P, "vector")
            t = P.emit("vector", lambda h, s=PO.t, b=bout1R.t, d=C.x[o], o=o, os_=osl:
                       h.scalar_tensor_tensor(d[:, os_], s[:], b[:, o:o + 1], d[:, os_],
                                              op0=ALU.add, op1=ALU.add), mark=True)
            C.xB[o].wrote(t)
            PO.buf.read(t); bout1R.buf.read(t)
    for c in range(8):
        C.dump(f"x3_{c}", C.x[c][:], [C.xB[c].wtok])


def p7_l1_ffn(C):
    g = C.g
    dense_ffn(C, "f1", g.HALO, g.HALO + g.OWN, "w11", "b11", "w21",
              AF.Gelu_apprx_tanh, "rw1", None, "b21", None, False, True, nres=13,
              final_store=True)


def p8_store(C):
    pass


def kernel(**inputs):
    g = Geom()
    nc = build(g, dbg=(), race_safe=True)
    in_maps = [stage_core_inputs(g, c, inputs) for c in range(8)]
    from concourse.bass_utils import run_bass_kernel_spmd
    res = run_bass_kernel_spmd(nc, in_maps, core_ids=list(range(8)))
    B, S, E = g.B, g.S, g.E
    full = np.zeros((B, S, E), np.float32)
    for c in range(8):
        b, h = c // 2, c % 2
        yc = np.asarray(res.results[c]["y"])
        xm = yc.reshape(128, 8, g.OWN).transpose(1, 0, 2).reshape(E, g.OWN)
        full[b, h * g.OWN:(h + 1) * g.OWN] = xm.T
    return full
